# version 08
# baseline (speedup 1.0000x reference)
import contextlib
import math
import os
import numpy as np
import ml_dtypes
import concourse.bass as bass
import concourse.mybir as mybir
from concourse.bass_utils import run_bass_kernel_spmd

F32 = mybir.dt.float32
BF16 = mybir.dt.bfloat16
U8 = mybir.dt.uint8
AF = mybir.ActivationFunctionType
ALU = mybir.AluOpType
AX = mybir.AxisListType

NCORES = 8
D = 4096
L = 8192
T = 1024
HALO = 3
TH = T + HALO
THP = 1040
CS = 128
NCH = T // CS
H = 128
PH = 64
NST = 128
G = 8
INTER = H * PH
CONV = INTER + 2 * G * NST
PROJ = INTER + CONV + H
GS = INTER // G
EPS = 1e-5
SCALING = 0.5

NOC = PROJ // 128
OC_GATE0, OC_X0, OC_B0, OC_C0, OC_DT = 0, 64, 128, 136, 144
SPANS_XBC = [(0, 512), (512, 1024), (1024, 1027)]
SPANS_GD = [(3, 515), (515, 1027)]

ENGS = ("sync", "scalar", "vector", "tensor", "gpsimd")
GROUPS = ("g_const", "g_w0", "g_w1", "g_sp", "g_rd", "g_cc",
          "g_xc", "g_sw0", "g_sw1", "g_spg", "g_rdg")
SELF_SYNC = {"sync": False, "scalar": False, "vector": False, "tensor": False,
             "gpsimd": False}


class Buf:
    __slots__ = ("name", "ready", "readers")

    def __init__(self, name):
        self.name = name
        self.ready = None
        self.readers = []


class EngRec:
    __slots__ = ("name", "cum", "ops", "last_wait", "sem")

    def __init__(self, name):
        self.name = name
        self.cum = 0
        self.ops = []
        self.last_wait = {}
        self.sem = None


class Builder:
    def __init__(self, nc):
        self.nc = nc
        self.engs = {n: EngRec(n) for n in ENGS}
        self.gcum = {g: 0 for g in GROUPS}
        self.gsem = {}

    def buf(self, name):
        return Buf(name)

    def op(self, eng, closure, reads=(), writes=(), inc=1, group=None):
        e = self.engs[eng]
        waits = {}

        def need(tok):
            if tok is None:
                return
            s, v = tok
            if s in self.gcum:
                v = self.gcum[s]
                if v == 0:
                    return
            elif s == eng and not SELF_SYNC[eng]:
                return
            if e.last_wait.get(s, 0) >= v:
                return
            if waits.get(s, 0) < v:
                waits[s] = v

        for b in reads:
            need(b.ready)
        for b in writes:
            need(b.ready)
            for t in b.readers:
                need(t)
        for s, v in waits.items():
            e.last_wait[s] = v
        if group is not None:
            self.gcum[group] += inc
            tok = (group, self.gcum[group])
            e.ops.append((waits, closure, (group, inc)))
            e.cum += 0
        else:
            tok = (eng, e.cum + inc)
            e.ops.append((waits, closure, inc))
            e.cum += inc
        for b in reads:
            b.readers.append(tok)
        for b in writes:
            b.ready = tok
            b.readers = []
        return tok

    def snap(self):
        d = {n: self.engs[n].cum for n in ENGS}
        d.update({g: self.gcum[g] for g in GROUPS})
        return d

    def barrier(self, snap=None):
        if snap is None:
            snap = self.snap()
        for n in ENGS:
            e = self.engs[n]
            waits = {}
            for m, v in snap.items():
                if m == n or v == 0:
                    continue
                if e.last_wait.get(m, 0) >= v:
                    continue
                waits[m] = v
                e.last_wait[m] = v
            e.ops.append((waits, lambda q: q.nop(), 1))
            e.cum += 1

    def flush(self, block):
        return

    def finalize(self, block):
        waited = {n: set() for n in ENGS}
        for n in ENGS:
            for w, _, _ in self.engs[n].ops:
                for s, v in w.items():
                    if s in waited:
                        waited[s].add(v)
        CHUNK = 3000
        for n in ENGS:
            e = self.engs[n]
            if not e.ops:
                continue
            ops = e.ops
            e.ops = []
            sems = {m: self.engs[m].sem for m in ENGS}
            sems.update(self.gsem)
            mysem = e.sem
            wset = waited[n]
            nops = len(ops)
            state = {"pending": 0, "cum": 0}
            for lo in range(0, nops, CHUNK):
                hi = min(lo + CHUNK, nops)

                def body(q, lo=lo, hi=hi, ops=ops, sems=sems, mysem=mysem,
                         wset=wset, nops=nops, state=state, n=n):
                    for i in range(lo, hi):
                        waits, closure, inc = ops[i]
                        for s, v in sorted(waits.items()):
                            q.wait_ge(sems[s], v)
                        ins = closure(q)
                        if isinstance(inc, tuple):
                            ins.then_inc(sems[inc[0]], inc[1])
                            continue
                        state["pending"] += inc
                        state["cum"] += inc
                        dense = n != "tensor"
                        if (dense or state["cum"] in wset or i == nops - 1
                                or i == hi - 1):
                            ins.then_inc(mysem, state["pending"])
                            state["pending"] = 0

                getattr(block, n)(body)


def _build_FT_bf16():
    n = D
    t = np.arange(n)[:, None].astype(np.float64)
    F = np.zeros((n, n), np.float64)
    F[:, 0] = 1.0 / math.sqrt(n)
    maxf = (n - 1) // 2
    f = np.arange(1, maxf + 1)[None, :].astype(np.float64)
    ang = 2.0 * np.pi * f * t / n
    F[:, 1:1 + 2 * maxf:2] = math.sqrt(2.0 / n) * np.cos(ang)
    F[:, 2:2 + 2 * maxf:2] = math.sqrt(2.0 / n) * np.sin(ang)
    F[:, n - 1] = ((-1.0) ** np.arange(n)) / math.sqrt(n)
    return np.ascontiguousarray(F.T).astype(ml_dtypes.bfloat16)


_FT_CACHE = None


def _host_prep(inputs):
    global _FT_CACHE
    x = np.asarray(inputs["L0_attn_input_states"], np.float32).reshape(L, D)
    w_in = np.asarray(inputs["L0_attn_in_proj_w"], np.float32)
    conv_w = np.asarray(inputs["L0_attn_conv_w"], np.float32)
    conv_b = np.asarray(inputs["L0_attn_conv_b"], np.float32)
    dt_bias = np.asarray(inputs["L0_attn_dt_bias"], np.float32)
    A_log = np.asarray(inputs["L0_attn_A_log"], np.float32)
    Dp = np.asarray(inputs["L0_attn_D"], np.float32)
    norm_w = np.asarray(inputs["L0_attn_norm_w"], np.float32)
    out_w = np.asarray(inputs["L0_attn_out_w"], np.float32)
    w_base = np.asarray(inputs["L0_ffn_W_base"], np.float32)
    b_base = np.asarray(inputs["L0_ffn_b_base"], np.float32)
    enc0 = np.asarray(inputs["L0_ffn_enc0"], np.float32)
    enc1 = np.asarray(inputs["L0_ffn_enc1"], np.float32)
    c0 = np.asarray(inputs["L0_ffn_coefs0"], np.float32)
    c1 = np.asarray(inputs["L0_ffn_coefs1"], np.float32)
    m0 = np.asarray(inputs["L0_ffn_mask0"])
    m1 = np.asarray(inputs["L0_ffn_mask1"])

    sh = {}
    wiT = np.ascontiguousarray(w_in.T)
    sh["w_in"] = np.ascontiguousarray(wiT.reshape(32, 128, NOC, 128).transpose(2, 0, 1, 3))
    woT = np.ascontiguousarray(out_w.T)
    sh["w_out"] = np.ascontiguousarray(woT.reshape(64, 128, 32, 128).transpose(2, 0, 1, 3))
    wbT = np.ascontiguousarray(w_base.T)
    sh["w_base"] = np.ascontiguousarray(wbT.reshape(32, 128, 32, 128).transpose(2, 0, 1, 3))
    if _FT_CACHE is None:
        _FT_CACHE = _build_FT_bf16()
    sh["ft"] = _FT_CACHE
    sh["enc"] = np.ascontiguousarray(np.concatenate([enc0, enc1], axis=0).T)
    sh["c01"] = np.ascontiguousarray(np.concatenate(
        [c0.transpose(1, 2, 0).reshape(D, 96),
         c1.transpose(1, 2, 0).reshape(D, 24)], axis=1))
    sh["m01"] = np.ascontiguousarray(np.concatenate(
        [m0.transpose(1, 2, 0).reshape(D, 96),
         m1.transpose(1, 2, 0).reshape(D, 24)], axis=1)).view(np.uint8)
    sh["cw"] = np.ascontiguousarray(conv_w.reshape(80, 128, 4).transpose(1, 0, 2).reshape(128, 320))
    sh["cb"] = np.ascontiguousarray(conv_b.reshape(80, 128).T)
    sh["dtb"] = np.ascontiguousarray(dt_bias.reshape(128, 1))
    sh["alog"] = np.ascontiguousarray(A_log.reshape(128, 1))
    dexp = np.empty((128, 64), np.float32)
    for oc in range(64):
        dexp[0:64, oc] = Dp[2 * oc]
        dexp[64:128, oc] = Dp[2 * oc + 1]
    sh["dexp"] = dexp
    sh["nwc"] = np.ascontiguousarray(norm_w.reshape(64, 128).T)
    sh["bbc"] = np.ascontiguousarray(b_base.reshape(32, 128).T)
    sh["ident"] = np.eye(128, dtype=np.float32)
    sh["ones1"] = np.ones((1, 128), np.float32)
    sh["onest"] = np.ones((128, CS), np.float32)
    sh["tril"] = np.ascontiguousarray(
        (np.arange(CS)[None, :] >= np.arange(128)[:, None]).astype(np.float32))
    sh["invgs"] = np.full((128, 1), 1.0 / GS, np.float32).astype(ml_dtypes.bfloat16)
    sh["invd"] = np.full((128, 1), 1.0 / D, np.float32).astype(ml_dtypes.bfloat16)
    sh["epsc"] = np.full((128, 1), EPS, np.float32)
    sh["onec"] = np.ones((128, 1), np.float32)

    per_core = []
    for k in range(NCORES):
        d = {}
        s = k * T
        if k == 0:
            xs = np.concatenate([np.zeros((HALO, D), np.float32), x[0:T]], axis=0)
        else:
            xs = x[s - HALO:s + T]
        d["x"] = np.ascontiguousarray(xs)
        pmt = np.zeros((8, 8), np.float32)
        for m in range(8):
            for j in range(8):
                if j < m < k:
                    pmt[m, j] = 1.0
        d["pmt"] = pmt
        zm = np.zeros((128, 8), np.float32)
        zm[:, 0:k] = 1.0
        d["zmask"] = zm
        per_core.append(d)
    return sh, per_core


PHASE_LIMIT = int(os.environ.get("KPHASES", "99"))
P4SUB = int(os.environ.get("KP4SUB", "99"))
P4OPS = int(os.environ.get("KP4OPS", "99"))
P4Y = int(os.environ.get("KP4Y", "2"))


def build_program():
    nc = bass.Bass("TRN2", target_bir_lowering=False, debug=False)
    B = Builder(nc)

    def ein(name, shape, dt):
        return nc.dram_tensor(name, shape, dt, kind="ExternalInput").ap(), B.buf("d:" + name)

    x_in, b_x_in = ein("x", [TH, D], F32)
    w_in, b_w_in = ein("w_in", [NOC, 32, 128, 128], F32)
    w_out, b_w_out = ein("w_out", [32, 64, 128, 128], F32)
    w_baseT, b_w_base = ein("w_base", [32, 32, 128, 128], F32)
    ft_in, b_ft = ein("ft", [D, D], BF16)
    enc_in, b_enc = ein("enc", [D, 40], F32)
    c01_in, b_c01 = ein("c01", [D, 120], F32)
    m01_in, b_m01 = ein("m01", [D, 120], U8)
    cw_in, b_cw = ein("cw", [128, 320], F32)
    cb_in, b_cb = ein("cb", [128, 80], F32)
    dtb_in, b_dtb = ein("dtb", [128, 1], F32)
    alog_in, b_alog = ein("alog", [128, 1], F32)
    dexp_in, b_dexp = ein("dexp", [128, 64], F32)
    nwc_in, b_nwc = ein("nwc", [128, 64], F32)
    bbc_in, b_bbc = ein("bbc", [128, 32], F32)
    ident_in, b_ident_in = ein("ident", [128, 128], F32)
    ones1_in, b_ones1_in = ein("ones1", [1, 128], F32)
    onest_in, b_onest_in = ein("onest", [128, CS], F32)
    tril_in, b_tril_in = ein("tril", [128, CS], F32)
    invgs_in, b_invgs_in = ein("invgs", [128, 1], BF16)
    invd_in, b_invd_in = ein("invd", [128, 1], BF16)
    epsc_in, b_epsc_in = ein("epsc", [128, 1], F32)
    onec_in, b_onec_in = ein("onec", [128, 1], F32)
    pmt_in, b_pmt_in = ein("pmt", [8, 8], F32)
    zmask_in, b_zmask_in = ein("zmask", [128, 8], F32)

    out_ext = nc.dram_tensor("out", [T, D], F32, kind="ExternalOutput").ap()
    b_out = B.buf("d:out")
    KDEBUG = os.environ.get("KDEBUG", "") != ""
    dbg = {}
    if KDEBUG:
        for nm, shape, dt in [("dbg_x", [INTER, T], BF16), ("dbg_sg", [INTER, T], BF16),
                              ("dbg_b", [G * NST, T], BF16), ("dbg_attn", [D, T], F32),
                              ("dbg_h", [D, T], F32), ("dbg_cum", [NCH, H, CS], F32),
                              ("dbg_g", [INTER, T], BF16), ("dbg_z2", [D, T], F32),
                              ("dbg_agb", [NCORES * NST, INTER], BF16),
                              ("dbg_agd", [NCORES, H], F32)]:
            dbg[nm] = (nc.dram_tensor(nm, shape, dt, kind="ExternalOutput").ap(),
                       B.buf("d:" + nm))

    def idram(name, shape, dt):
        return nc.dram_tensor(name, shape, dt).ap(), B.buf("i:" + name)

    d_xbf, b_xbf = idram("d_xbf", [THP, D], BF16)
    d_x, b_dx = idram("d_x", [INTER, T], BF16)
    d_b, b_db = idram("d_b", [G * NST, T], BF16)
    d_sg, b_dsg = idram("d_sg", [INTER, T], BF16)
    d_g, b_dg = idram("d_g", [INTER, T], BF16)
    d_attn, b_dattn = idram("d_attn", [D, T], F32)
    d_h, b_dh = idram("d_h", [D, T], F32)
    d_z2, b_dz2 = idram("d_z2", [D, T], F32)
    d_cum, b_dcum = idram("d_cum", [NCH, H, CS], F32)
    d_agbi, b_agbi = idram("d_agbi", [NST, INTER], BF16)
    d_agbo, b_agbo = idram("d_agbo", [NCORES * NST, INTER], BF16)
    d_agdi, b_agdi = idram("d_agdi", [1, H], F32)
    d_agdo, b_agdo = idram("d_agdo", [NCORES, H], F32)

    stack_all = contextlib.ExitStack()

    def sb(name, shape, dt, st=None):
        hdl = (st or stack_all).enter_context(nc.sbuf_tensor("sb_" + name, shape, dt))
        return hdl, B.buf("s:" + name)

    def ps(name, st=None):
        hdl = (st or stack_all).enter_context(nc.psum_tensor("ps_" + name, [128, 512], F32))
        return hdl, B.buf("p:" + name)

    s_ident, b_ident = sb("ident", [128, 128], F32)
    s_ones1, b_ones1 = sb("ones1", [1, 128], F32)
    s_onest, b_onest = sb("onest", [128, CS], F32)
    s_tril, b_tril = sb("tril", [128, CS], F32)
    s_cw, b_scw = sb("cw", [128, 320], F32)
    s_cb, b_scb = sb("cb", [128, 80], F32)
    s_dtb, b_sdtb = sb("dtb", [128, 1], F32)
    s_alog, b_salog = sb("alog", [128, 1], F32)
    s_dexp, b_sdexp = sb("dexp", [128, 64], F32)
    s_nwc, b_snwc = sb("nwc", [128, 64], F32)
    s_bbc, b_sbbc = sb("bbc", [128, 32], F32)
    s_invgs, b_sinvgs = sb("invgs", [128, 1], BF16)
    s_invd, b_sinvd = sb("invd", [128, 1], BF16)
    s_epsc, b_sepsc = sb("epsc", [128, 1], F32)
    s_onec, b_sonec = sb("onec", [128, 1], F32)
    s_pmt, b_spmt = sb("pmt", [8, 8], F32)
    s_zmask, b_szmask = sb("zmask", [128, 8], F32)
    s_negA, b_negA = sb("negA", [128, 1], F32)
    s_row, b_srow = sb("rowtmp", [1, 128], F32)
    s_enc, b_senc = sb("encbf", [128, 32 * 40], BF16)
    s_fwt, b_sfwt = sb("fwt", [40, D], BF16)
    s_e01, b_se01 = sb("e01", [40, T], BF16)

    sem_stack = contextlib.ExitStack()
    for n in ENGS:
        B.engs[n].sem = sem_stack.enter_context(nc.semaphore("sem_" + n))
    for g in GROUPS:
        B.gsem[g] = sem_stack.enter_context(nc.semaphore("sem_" + g))
    block = sem_stack.enter_context(nc.Block())

    def dma(eng, out_ap, in_ap, reads, writes, group="g_rd", **kw):
        return B.op(eng, lambda q, o=out_ap, i=in_ap, kw=kw: q.dma_start(out=o, in_=i, **kw),
                    reads=reads, writes=writes, inc=16, group=group)

    st14 = contextlib.ExitStack()
    s_dt, b_sdt = sb("dtraw", [128, TH], F32, st14)
    s_cumT, b_scumT = sb("cumT", [128, NCH * CS], F32, st14)
    s_dtpT, b_sdtpT = sb("dtpT", [128, NCH * CS], BF16, st14)
    s_dlastT, b_sdlastT = sb("dlastT", [128, NCH * CS], BF16, st14)
    s_fT, b_sfT = sb("fT", [128, CS], BF16, st14)
    s_cdrow, b_scdrow = sb("cdrow", [1, NCH * H], F32, st14)
    s_B, b_sB = sb("Bfm", [128, G * T], BF16, st14)
    s_C, b_sC = sb("Cfm", [128, G * T], BF16, st14)
    st1 = contextlib.ExitStack()
    s_xfm, b_xfm = [], []
    for kt in range(32):
        h_, bb = sb(f"xfm{kt}", [128, THP], BF16, st1)
        s_xfm.append(h_)
        b_xfm.append(bb)
    s_wbuf, b_wbuf = [], []
    for i in range(2):
        h_, bb = sb(f"win{i}", [128, 32 * 128], BF16, st1)
        s_wbuf.append(h_)
        b_wbuf.append(bb)
    p_ip = [ps(f"pip{i}", st1) for i in range(3)]
    s_pre, b_pre = [], []
    s_cvt, b_cvt = [], []
    s_evo, b_evo = [], []
    for i in range(2):
        h_, bb = sb(f"pre{i}", [128, TH], BF16, st1)
        s_pre.append(h_)
        b_pre.append(bb)
        h_, bb = sb(f"cvt{i}", [128, T], BF16, st1)
        s_cvt.append(h_)
        b_cvt.append(bb)
        h_, bb = sb(f"evo{i}", [128, TH], BF16, st1)
        s_evo.append(h_)
        b_evo.append(bb)

    for s_t, d_ap, b_s, b_d in [
        (s_ident, ident_in, b_ident, b_ident_in), (s_ones1, ones1_in, b_ones1, b_ones1_in),
        (s_onest, onest_in, b_onest, b_onest_in), (s_tril, tril_in, b_tril, b_tril_in),
        (s_cw, cw_in, b_scw, b_cw), (s_cb, cb_in, b_scb, b_cb),
        (s_dtb, dtb_in, b_sdtb, b_dtb), (s_alog, alog_in, b_salog, b_alog),
        (s_dexp, dexp_in, b_sdexp, b_dexp), (s_nwc, nwc_in, b_snwc, b_nwc),
        (s_bbc, bbc_in, b_sbbc, b_bbc), (s_invgs, invgs_in, b_sinvgs, b_invgs_in),
        (s_invd, invd_in, b_sinvd, b_invd_in), (s_epsc, epsc_in, b_sepsc, b_epsc_in),
        (s_onec, onec_in, b_sonec, b_onec_in), (s_pmt, pmt_in, b_spmt, b_pmt_in),
        (s_zmask, zmask_in, b_szmask, b_zmask_in),
    ]:
        dma("sync", s_t[:, :], d_ap, reads=[b_d], writes=[b_s], group="g_const")

    B.op("scalar", lambda q: q.activation(out=s_negA[:, :], in_=s_alog[:, :], func=AF.Exp),
         reads=[b_salog], writes=[b_negA])
    B.op("vector", lambda q: q.tensor_scalar(out=s_negA[:, :], in0=s_negA[:, :],
                                             scalar1=-1.0, scalar2=None, op0=ALU.mult),
         reads=[b_negA], writes=[b_negA])

    dma("gpsimd", d_xbf[0:TH, :], x_in, reads=[b_x_in], writes=[b_xbf], group="g_xc")
    for kt in range(32):
        dma("sync", s_xfm[kt][:, :], d_xbf[:, kt * 128:(kt + 1) * 128],
            reads=[b_xbf], writes=[b_xfm[kt]], transpose=True, group="g_const")

    for oc in range(NOC):
        wb, bwb = s_wbuf[oc % 2], b_wbuf[oc % 2]
        dma("gpsimd", wb[:, :].rearrange("p (a b) -> p a b", a=32),
            w_in[oc].rearrange("a p b -> p a b"),
            reads=[b_w_in], writes=[bwb], group="g_sw%d" % (oc % 2))
        par = oc % 2
        spans = SPANS_XBC if (OC_X0 <= oc < OC_DT) else SPANS_GD
        for ti, (t0, t1) in enumerate(spans):
            pp, bpp = p_ip[(oc * 3 + ti) % 3]
            n = t1 - t0
            for k in range(32):
                B.op("tensor",
                     lambda q, pp=pp, wb=wb, k=k, t0=t0, t1=t1: q.matmul(
                         pp[:, 0:(t1 - t0)], wb[:, k * 128:(k + 1) * 128],
                         s_xfm[k][:, t0:t1], start=(k == 0), stop=(k == 31)),
                     reads=[bwb, b_xfm[k]], writes=[bpp])
            if oc < OC_X0:
                B.op("scalar", lambda q, pp=pp, t0=t0, n=n, par=par: q.activation(
                    out=s_evo[par][:, t0:t0 + n], in_=pp[:, 0:n], func=AF.Silu),
                    reads=[bpp], writes=[b_evo[par]])
            elif oc == OC_DT:
                B.op("scalar", lambda q, pp=pp, t0=t0, n=n: q.activation(
                    out=s_dt[:, t0:t0 + n], in_=pp[:, 0:n], func=AF.Copy),
                    reads=[bpp], writes=[b_sdt])
            else:
                B.op("scalar", lambda q, pp=pp, t0=t0, n=n, par=par: q.activation(
                    out=s_pre[par][:, t0:t0 + n], in_=pp[:, 0:n], func=AF.Copy),
                    reads=[bpp], writes=[b_pre[par]])
        if oc < OC_X0:
            dma("sync", d_sg[oc * 128:(oc + 1) * 128, :], s_evo[par][:, HALO:TH],
                reads=[b_evo[par]], writes=[b_dsg], group="g_sp")
        elif oc == OC_DT:
            pass
        else:
            co = oc - OC_X0
            B.op("vector", lambda q, co=co, par=par: q.tensor_scalar(
                out=s_cvt[par][:, :], in0=s_pre[par][:, 3:3 + T],
                scalar1=s_cw[:, 4 * co + 3:4 * co + 4], scalar2=None, op0=ALU.mult),
                reads=[b_pre[par], b_scw], writes=[b_cvt[par]])
            for kk in (2, 1, 0):
                B.op("vector", lambda q, co=co, kk=kk, par=par: q.scalar_tensor_tensor(
                    out=s_cvt[par][:, :], in0=s_pre[par][:, kk:kk + T],
                    scalar=s_cw[:, 4 * co + kk:4 * co + kk + 1], in1=s_cvt[par][:, :],
                    op0=ALU.mult, op1=ALU.add),
                    reads=[b_pre[par], b_scw, b_cvt[par]], writes=[b_cvt[par]])
            B.op("scalar", lambda q, co=co, par=par: q.activation(
                out=s_evo[par][:, 0:T], in_=s_cvt[par][:, :], func=AF.Silu,
                bias=s_cb[:, co:co + 1]), reads=[b_cvt[par], b_scb], writes=[b_evo[par]])
            if oc < OC_B0:
                dma("sync", d_x[co * 128:(co + 1) * 128, :], s_evo[par][:, 0:T],
                    reads=[b_evo[par]], writes=[b_dx], group="g_sp")
            elif oc < OC_C0:
                g = oc - OC_B0
                B.op("vector", lambda q, g=g, par=par: q.tensor_copy(
                    out=s_B[:, g * T:(g + 1) * T], in_=s_evo[par][:, 0:T]),
                    reads=[b_evo[par]], writes=[b_sB])
                dma("sync", d_b[g * 128:(g + 1) * 128, :], s_evo[par][:, 0:T],
                    reads=[b_evo[par]], writes=[b_db], group="g_sp")
            else:
                g = oc - OC_C0
                B.op("vector", lambda q, g=g, par=par: q.tensor_copy(
                    out=s_C[:, g * T:(g + 1) * T], in_=s_evo[par][:, 0:T]),
                    reads=[b_evo[par]], writes=[b_sC])

    B.flush(block)
    snap1 = B.snap()
    B.barrier(snap1)
    st1.close()
    if PHASE_LIMIT <= 1:
        B.finalize(block)
        sem_stack.close()
        return nc

    st2 = contextlib.ExitStack()
    s_dtp, b_sdtp = sb("dtp", [128, T], F32, st2)
    s_dA, b_sdA = sb("dA", [128, T], F32, st2)
    s_brun, b_brun = sb("brun", [128, INTER], F32, st2)
    s_bigT, b_bigT = sb("bigT", [128, INTER], BF16, st2)
    s_BT, b_sBT = sb("BT", [128, G * NST], BF16, st2)
    s_cum, b_scum = sb("cumc", [128, CS], F32, st2)
    s_tmpA, b_tmpA = sb("tmpA", [128, CS], F32, st2)
    s_tmpB, b_tmpB = sb("tmpB", [128, 4], F32, st2)
    p_tr0 = ps("ptr0", st2)
    p_tr1 = ps("ptr1", st2)
    p_row = ps("prow", st2)
    p_cdb = ps("pcdb", st2)
    p_st = [ps("pst0", st2), ps("pst1", st2)]
    p_dtot = ps("pdtot", st2)

    B.op("vector", lambda q: q.tensor_scalar(
        out=s_dtp[:, :], in0=s_dt[:, HALO:TH], scalar1=s_dtb[:, :], scalar2=None,
        op0=ALU.add), reads=[b_sdt, b_sdtb], writes=[b_sdtp])
    B.op("scalar", lambda q: q.activation(out=s_dA[:, :], in_=s_dtp[:, :], func=AF.Abs),
         reads=[b_sdtp], writes=[b_sdA])
    B.op("scalar", lambda q: q.activation(out=s_dA[:, :], in_=s_dA[:, :], func=AF.Exp,
                                          scale=-1.0), reads=[b_sdA], writes=[b_sdA])
    B.op("scalar", lambda q: q.activation(out=s_dA[:, :], in_=s_dA[:, :], func=AF.Ln,
                                          bias=s_onec[:, :]),
         reads=[b_sdA, b_sonec], writes=[b_sdA])
    B.op("vector", lambda q: q.tensor_scalar(
        out=s_dtp[:, :], in0=s_dtp[:, :], scalar1=0.0, scalar2=None, op0=ALU.max),
        reads=[b_sdtp], writes=[b_sdtp])
    B.op("vector", lambda q: q.tensor_add(out=s_dtp[:, :], in0=s_dtp[:, :], in1=s_dA[:, :]),
         reads=[b_sdtp, b_sdA], writes=[b_sdtp])
    B.op("vector", lambda q: q.tensor_scalar(
        out=s_dA[:, :], in0=s_dtp[:, :], scalar1=s_negA[:, :], scalar2=None, op0=ALU.mult),
        reads=[b_sdtp, b_negA], writes=[b_sdA])
    B.op("vector", lambda q: q.memset(s_brun[:, :], 0.0), reads=[], writes=[b_brun])

    for c in range(NCH):
        c0, c1 = c * CS, (c + 1) * CS
        B.op("vector", lambda q, c0=c0, c1=c1: q.tensor_tensor_scan(
            out=s_cum[:, :], data0=s_onest[:, :], data1=s_dA[:, c0:c1],
            initial=0.0, op0=ALU.mult, op1=ALU.add), reads=[b_onest, b_sdA], writes=[b_scum])
        dma("sync", d_cum[c], s_cum[:, :], reads=[b_scum], writes=[b_dcum], group="g_sp")
        B.op("tensor", lambda q: q.transpose(p_tr0[0][:, 0:CS], s_cum[:, :], s_ident[:, :]),
             reads=[b_scum, b_ident], writes=[p_tr0[1]])
        B.op("vector", lambda q, c0=c0, c1=c1: q.tensor_copy(
            out=s_cumT[:, c0:c1], in_=p_tr0[0][:, 0:CS]),
            reads=[p_tr0[1]], writes=[b_scumT])
        B.op("scalar", lambda q: q.activation(
            out=s_tmpA[:, :], in_=s_cum[:, :], func=AF.Exp, scale=-1.0,
            bias=s_cum[:, CS - 1:CS]), reads=[b_scum], writes=[b_tmpA])
        B.op("tensor", lambda q, c0=c0, c1=c1: q.transpose(
            p_tr1[0][:, 0:CS], s_dtp[:, c0:c1], s_ident[:, :]),
            reads=[b_sdtp, b_ident], writes=[p_tr1[1]])
        B.op("vector", lambda q, c0=c0, c1=c1: q.tensor_copy(
            out=s_dtpT[:, c0:c1], in_=p_tr1[0][:, 0:CS]),
            reads=[p_tr1[1]], writes=[b_sdtpT])
        B.op("tensor", lambda q: q.transpose(p_tr1[0][:, 0:CS], s_tmpA[:, :], s_ident[:, :]),
             reads=[b_tmpA, b_ident], writes=[p_tr1[1]])
        B.op("vector", lambda q, c0=c0, c1=c1: q.tensor_copy(
            out=s_dlastT[:, c0:c1], in_=p_tr1[0][:, 0:CS]),
            reads=[p_tr1[1]], writes=[b_sdlastT])
        B.op("vector", lambda q, c0=c0, c1=c1: q.reduce_sum(
            out=s_tmpB[:, 0:1], in_=s_dA[:, c0:c1], axis=AX.X),
            reads=[b_sdA], writes=[b_tmpB])
        B.op("tensor", lambda q, c=c: q.matmul(
            p_dtot[0][0:1, 0:128], s_tmpB[:, 0:1], s_ident[:, :],
            start=(c == 0), stop=(c == NCH - 1)),
            reads=[b_tmpB, b_ident], writes=[p_dtot[1]])
        B.op("scalar", lambda q: q.activation(out=s_tmpB[:, 1:2], in_=s_tmpB[:, 0:1],
                                              func=AF.Exp), reads=[b_tmpB], writes=[b_tmpB])
        B.op("tensor", lambda q: q.matmul(
            p_row[0][0:1, 0:128], s_tmpB[:, 1:2], s_ident[:, :], start=True, stop=True),
            reads=[b_tmpB, b_ident], writes=[p_row[1]])
        B.op("vector", lambda q, c=c: q.tensor_copy(
            out=s_cdrow[:, c * H:(c + 1) * H], in_=p_row[0][0:1, 0:128]),
            reads=[p_row[1]], writes=[b_scdrow])
        if c > 0:
            B.op("tensor", lambda q, c=c: q.matmul(
                p_cdb[0][:, 0:128], s_ones1[:, :], s_cdrow[0:1, c * H:(c + 1) * H],
                start=True, stop=True), reads=[b_ones1, b_scdrow], writes=[p_cdb[1]])
            B.op("vector", lambda q: q.tensor_tensor(
                out=s_brun[:, :].rearrange("p (h a) -> p h a", h=H),
                in0=s_brun[:, :].rearrange("p (h a) -> p h a", h=H),
                in1=p_cdb[0][:, 0:128].unsqueeze(2).broadcast_to([128, H, PH]),
                op=ALU.mult), reads=[b_brun, p_cdb[1]], writes=[b_brun])
        dma("sync", s_bigT[:, :], d_x[:, c0:c1], reads=[b_dx], writes=[b_bigT], transpose=True)
        B.op("vector", lambda q, c0=c0, c1=c1: q.tensor_mul(
            out=s_fT[:, :], in0=s_dtpT[:, c0:c1], in1=s_dlastT[:, c0:c1]),
            reads=[b_sdtpT, b_sdlastT], writes=[b_sfT])
        B.op("vector", lambda q: q.tensor_tensor(
            out=s_bigT[:, :].rearrange("p (h a) -> p h a", h=H),
            in0=s_bigT[:, :].rearrange("p (h a) -> p h a", h=H),
            in1=s_fT[:, :].unsqueeze(2).broadcast_to([128, H, PH]),
            op=ALU.mult), reads=[b_bigT, b_sfT], writes=[b_bigT])
        dma("sync", s_BT[:, :], d_b[:, c0:c1], reads=[b_db], writes=[b_sBT], transpose=True)
        for hb in range(16):
            pst, bst = p_st[hb % 2]
            for hh in range(8):
                h = hb * 8 + hh
                g = h // 16
                B.op("tensor", lambda q, pst=pst, g=g, h=h, hh=hh: q.matmul(
                    pst[:, hh * PH:(hh + 1) * PH], s_BT[:, g * NST:(g + 1) * NST],
                    s_bigT[:, h * PH:(h + 1) * PH], start=True, stop=True),
                    reads=[b_sBT, b_bigT], writes=[bst])
            B.op("vector", lambda q, pst=pst, hb=hb: q.tensor_tensor(
                out=s_brun[:, hb * 512:(hb + 1) * 512], in0=s_brun[:, hb * 512:(hb + 1) * 512],
                in1=pst[:, :], op=ALU.add), reads=[bst, b_brun], writes=[b_brun])

    dma("gpsimd", d_agbi, s_brun[:, :], reads=[b_brun], writes=[b_agbi], group="g_spg")
    B.op("vector", lambda q: q.tensor_copy(out=s_row[:, :], in_=p_dtot[0][0:1, 0:128]),
         reads=[p_dtot[1]], writes=[b_srow])
    dma("sync", d_agdi, s_row[:, :], reads=[b_srow], writes=[b_agdi], group="g_sp")
    snap2 = B.snap()
    B.op("gpsimd", lambda q: q.collective_compute(
        "AllGather", ALU.bypass, replica_groups=[list(range(NCORES))],
        ins=[d_agbi], outs=[d_agbo]), reads=[b_agbi], writes=[b_agbo], inc=1,
        group="g_cc")
    B.op("gpsimd", lambda q: q.collective_compute(
        "AllGather", ALU.bypass, replica_groups=[list(range(NCORES))],
        ins=[d_agdi], outs=[d_agdo]), reads=[b_agdi], writes=[b_agdo], inc=1,
        group="g_cc")
    B.flush(block)
    B.barrier(snap2)
    st2.close()
    if PHASE_LIMIT <= 2:
        B.finalize(block)
        sem_stack.close()
        return nc

    st3 = contextlib.ExitStack()
    s_ftb, b_ftb = [], []
    for i in range(2):
        h_, bb = sb(f"ftb{i}", [128, D], BF16, st3)
        s_ftb.append(h_)
        b_ftb.append(bb)
    s_c01, b_sc01 = [], []
    s_m01f, b_sm01f = [], []
    for i in range(2):
        h_, bb = sb(f"c01_{i}", [128, 120], F32, st3)
        s_c01.append(h_)
        b_sc01.append(bb)
        h_, bb = sb(f"m01_{i}", [128, 120], F32, st3)
        s_m01f.append(h_)
        b_sm01f.append(bb)
    s_wm, b_swm = sb("wm", [128, 120], F32, st3)
    s_wt, b_swt = sb("wt", [128, 40], F32, st3)
    s_w01, b_sw01 = sb("w01", [128, 32 * 40], BF16, st3)
    p_fwt = [ps(f"pfwt{i}", st3) for i in range(8)]

    dma("gpsimd", s_enc[:, :].rearrange("p (a r) -> p a r", a=32),
        enc_in.rearrange("(a p) r -> p a r", p=128), reads=[b_enc], writes=[b_senc],
        group="g_rdg")
    for kt in range(32):
        i2 = kt % 2
        dma("sync", s_c01[i2][:, :], c01_in[kt * 128:(kt + 1) * 128, :],
            reads=[b_c01], writes=[b_sc01[i2]])
        dma("gpsimd", s_m01f[i2][:, :], m01_in[kt * 128:(kt + 1) * 128, :],
            reads=[b_m01], writes=[b_sm01f[i2]], group="g_rdg")
        B.op("vector", lambda q, i2=i2: q.tensor_mul(
            out=s_wm[:, :], in0=s_c01[i2][:, :], in1=s_m01f[i2][:, :]),
            reads=[b_sc01[i2], b_sm01f[i2]], writes=[b_swm])
        wmv = s_wm[:, :].rearrange("p (r e) -> p r e", r=40)
        B.op("vector", lambda q, wmv=wmv: q.tensor_add(
            out=s_wt[:, :].unsqueeze(2), in0=wmv[:, :, 0:1], in1=wmv[:, :, 1:2]),
            reads=[b_swm], writes=[b_swt])
        B.op("vector", lambda q, wmv=wmv, kt=kt: q.tensor_add(
            out=s_w01[:, kt * 40:(kt + 1) * 40].unsqueeze(2),
            in0=s_wt[:, :].unsqueeze(2), in1=wmv[:, :, 2:3]),
            reads=[b_swt, b_swm], writes=[b_sw01])
    for kt in range(32):
        fb, bfb = s_ftb[kt % 2], b_ftb[kt % 2]
        dma("sync", fb[:, :], ft_in[kt * 128:(kt + 1) * 128, :], reads=[b_ft], writes=[bfb],
            group="g_w%d" % (kt % 2))
        for ot in range(8):
            pf, bpf = p_fwt[ot]
            B.op("tensor", lambda q, pf=pf, fb=fb, kt=kt, ot=ot: q.matmul(
                pf[0:40, 0:512], s_w01[:, kt * 40:(kt + 1) * 40],
                fb[:, ot * 512:(ot + 1) * 512], start=(kt == 0), stop=(kt == 31)),
                reads=[b_sw01, bfb], writes=[bpf])
    for ot in range(8):
        pf, bpf = p_fwt[ot]
        B.op("scalar", lambda q, pf=pf, ot=ot: q.activation(
            out=s_fwt[:, ot * 512:(ot + 1) * 512], in_=pf[0:40, 0:512],
            func=AF.Copy, scale=SCALING), reads=[bpf], writes=[b_sfwt])
    B.flush(block)
    snap3 = B.snap()
    B.barrier(snap3)
    st3.close()
    if PHASE_LIMIT <= 3:
        B.finalize(block)
        sem_stack.close()
        return nc

    st4 = contextlib.ExitStack()
    s_prev, b_prev = sb("prev", [128, INTER], F32, st4)
    s_prevb, b_prevb = sb("prevb", [128, INTER], BF16, st4)
    s_big, b_big = sb("big", [128, INTER], BF16, st4)
    s_bigH, b_bigH = sb("bigH", [128, INTER], BF16, st4)
    s_BT4, b_sBT4 = sb("BT4", [128, G * NST], BF16, st4)
    s_sgc, b_ssgc = sb("sgc", [128, INTER], BF16, st4)
    s_xfc, b_sxfc = sb("xfc", [128, INTER], BF16, st4)
    s_scm, b_sscm = sb("scm", [128, G * CS], BF16, st4)
    s_cumq, s_cumi, s_E4, s_S4, s_ce4, s_ecb, s_gq, s_sbc = [], [], [], [], [], [], [], []
    b_scumq, b_scumi, b_sE4, b_sS4, b_sce4, b_secb, b_sgq, b_ssbc = [], [], [], [], [], [], [], []
    for i in range(2):
        t_, bb = sb(f"cumq{i}", [1, 8 * CS], F32, st4)
        s_cumq.append(t_); b_scumq.append(bb)
        t_, bb = sb(f"cumi{i}", [128, 512], F32, st4)
        s_cumi.append(t_); b_scumi.append(bb)
        t_, bb = sb(f"E4{i}", [128, 512], BF16, st4)
        s_E4.append(t_); b_sE4.append(bb)
        t_, bb = sb(f"S4{i}", [128, 512], BF16, st4)
        s_S4.append(t_); b_sS4.append(bb)
        t_, bb = sb(f"ce4{i}", [128, 512], BF16, st4)
        s_ce4.append(t_); b_sce4.append(bb)
        t_, bb = sb(f"ecb{i}", [128, 512], BF16, st4)
        s_ecb.append(t_); b_secb.append(bb)
        t_, bb = sb(f"gq{i}", [128, CS], BF16, st4)
        s_gq.append(t_); b_sgq.append(bb)
        t_, bb = sb(f"sbc{i}", [128, CS], F32, st4)
        s_sbc.append(t_); b_ssbc.append(bb)
    s_wz, b_swz = sb("wz", [128, 8], F32, st4)
    s_dtg, b_sdtg = sb("dtg", [8, 128], F32, st4)
    s_yt, b_syt = sb("yt", [128, 128], F32, st4)
    s_g0, b_sg0 = sb("g0", [128, 8 * CS], BF16, st4)
    p_sc = [ps("psc0", st4), ps("psc1", st4)]
    p_bc = [ps("pbc0", st4), ps("pbc1", st4)]
    p_y = [ps("py0", st4), ps("py1", st4)]
    p_v = ps("pv", st4)
    p_sb2 = ps("psb", st4)
    var_ps = [p_v, p_sc[0]]
    sbc_ps = [p_sb2, p_sc[1]]

    dma("sync", s_dtg[:, :], d_agdo, reads=[b_agdo], writes=[b_sdtg])
    B.op("tensor", lambda q: q.matmul(p_bc[0][0][:, 0:8], s_dtg[:, :], s_pmt[:, :],
                                      start=True, stop=True),
         reads=[b_sdtg, b_spmt], writes=[p_bc[0][1]])
    B.op("scalar", lambda q: q.activation(out=s_wz[:, :], in_=p_bc[0][0][:, 0:8],
                                          func=AF.Exp), reads=[p_bc[0][1]], writes=[b_swz])
    B.op("vector", lambda q: q.tensor_mul(out=s_wz[:, :], in0=s_wz[:, :],
                                          in1=s_zmask[:, :]),
         reads=[b_swz, b_szmask], writes=[b_swz])
    B.op("vector", lambda q: q.memset(s_prev[:, :], 0.0), reads=[], writes=[b_prev])
    for j in range(NCORES - 1):
        dma("sync", s_bigH[:, :], d_agbo[j * NST:(j + 1) * NST, :],
            reads=[b_agbo], writes=[b_bigH])
        B.op("tensor", lambda q, j=j: q.matmul(
            p_bc[1][0][0:1, 0:128],
            s_wz[:, j:j + 1], s_ident[:, :], start=True, stop=True),
            reads=[b_swz, b_ident], writes=[p_bc[1][1]])
        B.op("vector", lambda q: q.tensor_copy(out=s_row[:, :], in_=p_bc[1][0][0:1, 0:128]),
             reads=[p_bc[1][1]], writes=[b_srow])
        B.op("tensor", lambda q: q.matmul(
            p_bc[0][0][:, 0:128], s_ones1[:, :], s_row[0:1, :], start=True, stop=True),
            reads=[b_ones1, b_srow], writes=[p_bc[0][1]])
        B.op("vector", lambda q: q.tensor_tensor(
            out=s_big[:, :].rearrange("p (h a) -> p h a", h=H),
            in0=s_bigH[:, :].rearrange("p (h a) -> p h a", h=H),
            in1=p_bc[0][0][:, 0:128].unsqueeze(2).broadcast_to([128, H, PH]),
            op=ALU.mult), reads=[b_bigH, p_bc[0][1]], writes=[b_big])
        B.op("vector", lambda q: q.tensor_tensor(
            out=s_prev[:, :], in0=s_prev[:, :], in1=s_big[:, :], op=ALU.add),
            reads=[b_prev, b_big], writes=[b_prev])
    B.op("vector", lambda q: q.tensor_copy(out=s_prevb[:, :], in_=s_prev[:, :]),
         reads=[b_prev], writes=[b_prevb])

    for c in range(NCH):
        c0, c1 = c * CS, (c + 1) * CS
        dma("sync", s_big[:, :], d_x[:, c0:c1], reads=[b_dx], writes=[b_big], transpose=True)
        B.op("vector", lambda q, c0=c0, c1=c1: q.tensor_tensor(
            out=s_big[:, :].rearrange("p (h a) -> p h a", h=H),
            in0=s_big[:, :].rearrange("p (h a) -> p h a", h=H),
            in1=s_dtpT[:, c0:c1].unsqueeze(2).broadcast_to([128, H, PH]),
            op=ALU.mult), reads=[b_big, b_sdtpT], writes=[b_big])
        B.op("vector", lambda q, c0=c0, c1=c1: q.tensor_tensor(
            out=s_bigH[:, :].rearrange("p (h a) -> p h a", h=H),
            in0=s_big[:, :].rearrange("p (h a) -> p h a", h=H),
            in1=s_dlastT[:, c0:c1].unsqueeze(2).broadcast_to([128, H, PH]),
            op=ALU.mult), reads=[b_big, b_sdlastT], writes=[b_bigH])
        dma("sync", s_BT4[:, :], d_b[:, c0:c1], reads=[b_db], writes=[b_sBT4], transpose=True)
        dma("sync", s_sgc[:, :].rearrange("p (a t) -> p a t", a=64),
            d_sg.rearrange("(a p) t -> p a t", p=128)[:, :, c0:c1],
            reads=[b_dsg], writes=[b_ssgc])
        dma("sync", s_xfc[:, :].rearrange("p (a t) -> p a t", a=64),
            d_x.rearrange("(a p) t -> p a t", p=128)[:, :, c0:c1],
            reads=[b_dx], writes=[b_sxfc])
        for g in range(G):
            pg, bpg = p_sc[g // 4]
            B.op("tensor", lambda q, pg=pg, g=g, c0=c0, c1=c1: q.matmul(
                pg[:, (g % 4) * CS:(g % 4 + 1) * CS], s_B[:, g * T + c0:g * T + c1],
                s_C[:, g * T + c0:g * T + c1], start=True, stop=True),
                reads=[b_sB, b_sC], writes=[bpg])
        for g in range(G):
            pg, bpg = p_sc[g // 4]
            B.op("vector", lambda q, pg=pg, g=g: q.tensor_tensor(
                out=s_scm[:, g * CS:(g + 1) * CS], in0=pg[:, (g % 4) * CS:(g % 4 + 1) * CS],
                in1=s_tril[:, :], op=ALU.mult), reads=[bpg, b_tril], writes=[b_sscm])
        for slab in range(16):
            sp = slab % 2
            dma("sync", s_cumq[sp][:, :],
                d_cum[c, slab * 8:(slab + 1) * 8, :].rearrange("a t -> (a t)").unsqueeze(0),
                reads=[b_dcum], writes=[b_scumq[sp]])
            for hb in range(2):
                h0 = slab * 8 + hb * 4
                g = h0 // 16
                bpar = (slab * 2 + hb) % 2
                pb, bpb = p_bc[bpar]
                B.op("tensor", lambda q, pb=pb, hb=hb, sp=sp: q.matmul(
                    pb[:, 0:512], s_ones1[:, :],
                    s_cumq[sp][0:1, hb * 512:(hb + 1) * 512], start=True, stop=True),
                    reads=[b_ones1, b_scumq[sp]], writes=[bpb])
                B.op("scalar", lambda q, pb=pb, bpar=bpar: q.activation(
                    out=s_ecb[bpar][:, :], in_=pb[:, 0:512], func=AF.Exp),
                    reads=[bpb], writes=[b_secb[bpar]])
                for hh in range(4):
                    B.op("scalar", lambda q, pb=pb, c0=c0, h0=h0, hh=hh, bpar=bpar:
                         q.activation(
                             out=s_cumi[bpar][:, hh * CS:(hh + 1) * CS],
                             in_=pb[:, hh * CS:(hh + 1) * CS], func=AF.Relu,
                             scale=-1.0,
                             bias=s_cumT[:, c0 + h0 + hh:c0 + h0 + hh + 1]),
                         reads=[bpb, b_scumT], writes=[b_scumi[bpar]])
                B.op("scalar", lambda q, bpar=bpar: q.activation(
                    out=s_E4[bpar][:, :], in_=s_cumi[bpar][:, :], func=AF.Exp,
                    scale=-1.0),
                    reads=[b_scumi[bpar]], writes=[b_sE4[bpar]])
                B.op("vector", lambda q, g=g, bpar=bpar: q.tensor_tensor(
                    out=s_S4[bpar][:, :].rearrange("p (a t) -> p a t", a=4),
                    in0=s_E4[bpar][:, :].rearrange("p (a t) -> p a t", a=4),
                    in1=s_scm[:, g * CS:(g + 1) * CS].unsqueeze(1).broadcast_to([128, 4, CS]),
                    op=ALU.mult), reads=[b_sE4[bpar], b_sscm], writes=[b_sS4[bpar]])
                B.op("vector", lambda q, g=g, c0=c0, c1=c1, bpar=bpar: q.tensor_tensor(
                    out=s_ce4[bpar][:, :].rearrange("p (a t) -> p a t", a=4),
                    in0=s_C[:, g * T + c0:g * T + c1].unsqueeze(1).broadcast_to([128, 4, CS]),
                    in1=s_ecb[bpar][:, :].rearrange("p (a t) -> p a t", a=4),
                    op=ALU.mult), reads=[b_sC, b_secb[bpar]], writes=[b_sce4[bpar]])
                for hh in range(4):
                    h = h0 + hh
                    pr = h // 2
                    py, bpy = p_y[pr % 2]
                    po = (h % 2) * PH
                    B.op("tensor", lambda q, py=py, po=po, h=h, hh=hh, bpar=bpar: q.matmul(
                        py[po:po + PH, 0:CS], s_big[:, h * PH:(h + 1) * PH],
                        s_S4[bpar][:, hh * CS:(hh + 1) * CS], start=True, stop=False),
                        reads=[b_big, b_sS4[bpar]], writes=[bpy])
                    B.op("tensor", lambda q, py=py, po=po, h=h, hh=hh, bpar=bpar: q.matmul(
                        py[po:po + PH, 0:CS], s_prevb[:, h * PH:(h + 1) * PH],
                        s_ce4[bpar][:, hh * CS:(hh + 1) * CS], start=False, stop=True),
                        reads=[b_prevb, b_sce4[bpar]], writes=[bpy])
                    if h % 2 == 1:
                        py2, bpy2 = p_y[pr % 2]
                        gg = pr // 8
                        gqp = pr % 2
                        B.op("vector", lambda q, py2=py2, pr=pr: q.scalar_tensor_tensor(
                            out=s_yt[:, :], in0=s_xfc[:, pr * CS:(pr + 1) * CS],
                            scalar=s_dexp[:, pr:pr + 1], in1=py2[:, 0:CS],
                            op0=ALU.mult, op1=ALU.add),
                            reads=[b_sxfc, b_sdexp, bpy2], writes=[b_syt])
                        B.op("vector", lambda q, pr=pr: q.tensor_tensor(
                            out=s_g0[:, (pr % 8) * CS:(pr % 8 + 1) * CS], in0=s_yt[:, :],
                            in1=s_sgc[:, pr * CS:(pr + 1) * CS], op=ALU.mult),
                            reads=[b_syt, b_ssgc], writes=[b_sg0])
                        B.op("scalar", lambda q, pr=pr, gqp=gqp: q.activation(
                            out=s_gq[gqp][:, :], in_=s_g0[:, (pr % 8) * CS:(pr % 8 + 1) * CS],
                            func=AF.Square), reads=[b_sg0], writes=[b_sgq[gqp]])
                        pv, bpv = var_ps[gg % 2]
                        B.op("tensor", lambda q, pr=pr, pv=pv, gqp=gqp: q.matmul(
                            pv[0:1, 0:CS], s_invgs[:, :], s_gq[gqp][:, :],
                            start=(pr % 8 == 0), stop=(pr % 8 == 7)),
                            reads=[b_sgq[gqp], b_sinvgs], writes=[bpv])
                        if pr % 8 == 7:
                            psb, bpsb = sbc_ps[gg % 2]
                            B.op("scalar", lambda q, pv=pv: q.activation(
                                out=s_row[:, :],
                                in_=pv[0:1, 0:CS], func=AF.Ln,
                                bias=s_epsc[0:1, :]), reads=[bpv, b_sepsc], writes=[b_srow])
                            B.op("scalar", lambda q: q.activation(
                                out=s_row[:, :], in_=s_row[:, :], func=AF.Exp, scale=-0.5),
                                reads=[b_srow], writes=[b_srow])
                            B.op("tensor", lambda q, psb=psb: q.matmul(
                                psb[:, 0:CS], s_ones1[:, :], s_row[0:1, :],
                                start=True, stop=True),
                                reads=[b_ones1, b_srow], writes=[bpsb])
                            B.op("scalar", lambda q, psb=psb, gg=gg: q.activation(
                                out=s_sbc[gg % 2][:, :], in_=psb[:, 0:CS], func=AF.Copy),
                                reads=[bpsb], writes=[b_ssbc[gg % 2]])
                            for p8 in range(8):
                                prr = gg * 8 + p8
                                B.op("vector", lambda q, p8=p8, prr=prr, gg=gg: q.scalar_tensor_tensor(
                                    out=s_sgc[:, prr * CS:(prr + 1) * CS],
                                    in0=s_g0[:, p8 * CS:(p8 + 1) * CS],
                                    scalar=s_nwc[:, prr:prr + 1], in1=s_sbc[gg % 2][:, :],
                                    op0=ALU.mult, op1=ALU.mult),
                                    reads=[b_sg0, b_snwc, b_ssbc[gg % 2]], writes=[b_ssgc])
        dma("scalar", d_g.rearrange("(a p) t -> p a t", p=128)[:, :, c0:c1],
            s_sgc[:, :].rearrange("p (a t) -> p a t", a=64),
            reads=[b_ssgc], writes=[b_dg], group="g_sp")
        B.op("tensor", lambda q, c=c: q.matmul(
            p_bc[0][0][:, 0:128],
            s_ones1[:, :], s_cdrow[0:1, c * H:(c + 1) * H], start=True, stop=True),
            reads=[b_ones1, b_scdrow], writes=[p_bc[0][1]])
        B.op("vector", lambda q: q.tensor_tensor(
            out=s_prev[:, :].rearrange("p (h a) -> p h a", h=H),
            in0=s_prev[:, :].rearrange("p (h a) -> p h a", h=H),
            in1=p_bc[0][0][:, 0:128].unsqueeze(2).broadcast_to([128, H, PH]),
            op=ALU.mult), reads=[b_prev, p_bc[0][1]], writes=[b_prev])
        for hb in range(16):
            pst, bst = p_sc[hb % 2]
            for hh in range(8):
                h = hb * 8 + hh
                g = h // 16
                B.op("tensor", lambda q, pst=pst, g=g, h=h, hh=hh: q.matmul(
                    pst[:, hh * PH:(hh + 1) * PH], s_BT4[:, g * NST:(g + 1) * NST],
                    s_bigH[:, h * PH:(h + 1) * PH], start=True, stop=True),
                    reads=[b_sBT4, b_bigH], writes=[bst])
            B.op("vector", lambda q, pst=pst, hb=hb: q.tensor_tensor(
                out=s_prev[:, hb * 512:(hb + 1) * 512],
                in0=s_prev[:, hb * 512:(hb + 1) * 512],
                in1=pst[:, :], op=ALU.add), reads=[bst, b_prev], writes=[b_prev])
        if c < NCH - 1:
            B.op("vector", lambda q: q.tensor_copy(out=s_prevb[:, :], in_=s_prev[:, :]),
                 reads=[b_prev], writes=[b_prevb])
    B.flush(block)
    snap4 = B.snap()
    B.barrier(snap4)
    st4.close()
    st14.close()
    if PHASE_LIMIT <= 4:
        B.finalize(block)
        sem_stack.close()
        return nc

    st5 = contextlib.ExitStack()
    s_gall, b_gall = sb("gall", [128, 64 * T], BF16, st5)
    s_owb, b_owb = [], []
    for i in range(2):
        h_, bb = sb(f"owb{i}", [128, 64 * 128], BF16, st5)
        s_owb.append(h_)
        b_owb.append(bb)
    s_aev, b_aev = [], []
    for i in range(2):
        h_, bb = sb(f"aev{i}", [128, 512], F32, st5)
        s_aev.append(h_)
        b_aev.append(bb)
    p_op = [ps(f"pop{i}", st5) for i in range(3)]

    dma("sync", s_gall[:, :].rearrange("p (a t) -> p a t", a=64),
        d_g.rearrange("(a p) t -> p a t", p=128), reads=[b_dg], writes=[b_gall])
    for oc in range(32):
        owb, bowb = s_owb[oc % 2], b_owb[oc % 2]
        dma("gpsimd", owb[:, :].rearrange("p (a b) -> p a b", a=64),
            w_out[oc].rearrange("a p b -> p a b"),
            reads=[b_w_out], writes=[bowb], group="g_sw%d" % (oc % 2))
        for tt in range(2):
            pp, bpp = p_op[(oc * 2 + tt) % 3]
            for kt in range(64):
                B.op("tensor", lambda q, pp=pp, owb=owb, kt=kt, tt=tt: q.matmul(
                    pp[:, 0:512], owb[:, kt * 128:(kt + 1) * 128],
                    s_gall[:, kt * T + tt * 512:kt * T + (tt + 1) * 512],
                    start=(kt == 0), stop=(kt == 63)),
                    reads=[bowb, b_gall], writes=[bpp])
            ae, bae = s_aev[tt], b_aev[tt]
            B.op("vector", lambda q, pp=pp, ae=ae: q.tensor_copy(
                out=ae[:, :], in_=pp[:, 0:512]), reads=[bpp], writes=[bae])
            dma("scalar", d_attn[oc * 128:(oc + 1) * 128, tt * 512:(tt + 1) * 512],
                ae[:, :], reads=[bae], writes=[b_dattn], group="g_sp")
    B.flush(block)
    snap5 = B.snap()
    B.barrier(snap5)
    st5.close()
    if PHASE_LIMIT <= 5:
        B.finalize(block)
        sem_stack.close()
        return nc

    stHB = contextlib.ExitStack()
    s_hbf, b_shbf = sb("hbf", [128, 32 * T], BF16, stHB)
    st6 = contextlib.ExitStack()
    s_xtm, b_sxtm = sb("xtm", [128, D], F32, st6)
    s_arb, b_sarb, s_zb, b_szb, s_zq, b_szq, s_hfm, b_shfm = [], [], [], [], [], [], [], []
    s_mrow, b_smrow, s_qrow, b_sqrow, s_vrow, b_svrow = [], [], [], [], [], []
    for i in range(2):
        t_, bb = sb(f"arb{i}", [128, D], F32, st6)
        s_arb.append(t_); b_sarb.append(bb)
        t_, bb = sb(f"zbf{i}", [128, D], BF16, st6)
        s_zb.append(t_); b_szb.append(bb)
        t_, bb = sb(f"zsq{i}", [128, D], BF16, st6)
        s_zq.append(t_); b_szq.append(bb)
        t_, bb = sb(f"hfm{i}", [128, D], F32, st6)
        s_hfm.append(t_); b_shfm.append(bb)
        t_, bb = sb(f"mrow{i}", [1, 128], F32, st6)
        s_mrow.append(t_); b_smrow.append(bb)
        t_, bb = sb(f"qrow{i}", [1, 128], F32, st6)
        s_qrow.append(t_); b_sqrow.append(bb)
        t_, bb = sb(f"vrow{i}", [1, 128], F32, st6)
        s_vrow.append(t_); b_svrow.append(bb)
    p_tx = [ps("ptx0", st6), ps("ptx1", st6)]
    p_m = ps("pm", st6)
    p_q = ps("pq", st6)
    p_s1 = [ps("ps1a", st6), ps("ps1b", st6)]
    p_s2 = [ps("ps2a", st6), ps("ps2b", st6)]

    def ln_rows(i, pm, pq, ps1, ps2):
        B.op("vector", lambda q: q.tensor_copy(out=s_mrow[i][:, :], in_=pm[0][0:1, 0:128]),
             reads=[pm[1]], writes=[b_smrow[i]])
        B.op("vector", lambda q: q.tensor_copy(out=s_qrow[i][:, :], in_=pq[0][0:1, 0:128]),
             reads=[pq[1]], writes=[b_sqrow[i]])
        B.op("vector", lambda q: q.tensor_mul(out=s_vrow[i][:, :], in0=s_mrow[i][:, :],
                                              in1=s_mrow[i][:, :]),
             reads=[b_smrow[i]], writes=[b_svrow[i]])
        B.op("vector", lambda q: q.tensor_sub(out=s_vrow[i][:, :], in0=s_qrow[i][:, :],
                                              in1=s_vrow[i][:, :]),
             reads=[b_sqrow[i], b_svrow[i]], writes=[b_svrow[i]])
        B.op("scalar", lambda q: q.activation(out=s_vrow[i][:, :], in_=s_vrow[i][:, :],
                                              func=AF.Ln, bias=s_epsc[0:1, :]),
             reads=[b_svrow[i], b_sepsc], writes=[b_svrow[i]])
        B.op("scalar", lambda q: q.activation(out=s_vrow[i][:, :], in_=s_vrow[i][:, :],
                                              func=AF.Exp, scale=-0.5),
             reads=[b_svrow[i]], writes=[b_svrow[i]])
        B.op("vector", lambda q: q.tensor_mul(out=s_mrow[i][:, :], in0=s_mrow[i][:, :],
                                              in1=s_vrow[i][:, :]),
             reads=[b_smrow[i], b_svrow[i]], writes=[b_smrow[i]])
        B.op("tensor", lambda q: q.matmul(ps1[0][:, 0:128], s_ones1[:, :],
                                          s_vrow[i][0:1, :], start=True, stop=True),
             reads=[b_ones1, b_svrow[i]], writes=[ps1[1]])
        B.op("tensor", lambda q: q.matmul(ps2[0][:, 0:128], s_ones1[:, :],
                                          s_mrow[i][0:1, :], start=True, stop=True),
             reads=[b_ones1, b_smrow[i]], writes=[ps2[1]])

    for tt in range(8):
        i = tt % 2
        dma("sync", s_xtm[:, :], x_in[HALO + tt * 128:HALO + (tt + 1) * 128, :],
            reads=[b_x_in], writes=[b_sxtm])
        dma("sync", s_arb[i][:, :].rearrange("p (a t) -> p a t", a=32),
            d_attn.rearrange("(a p) t -> p a t", p=128)[:, :, tt * 128:(tt + 1) * 128],
            reads=[b_dattn], writes=[b_sarb[i]])
        for dc in range(32):
            ptx, bptx = p_tx[dc % 2]
            B.op("tensor", lambda q, ptx=ptx, dc=dc: q.transpose(
                ptx[:, 0:128], s_xtm[:, dc * 128:(dc + 1) * 128], s_ident[:, :]),
                reads=[b_sxtm, b_ident], writes=[bptx])
            B.op("vector", lambda q, ptx=ptx, dc=dc, i=i: q.tensor_tensor(
                out=s_arb[i][:, dc * 128:(dc + 1) * 128], in0=ptx[:, 0:128],
                in1=s_arb[i][:, dc * 128:(dc + 1) * 128], op=ALU.add),
                reads=[bptx, b_sarb[i]], writes=[b_sarb[i]])
        B.op("scalar", lambda q, i=i: q.activation(out=s_zb[i][:, :], in_=s_arb[i][:, :],
                                                   func=AF.Copy),
             reads=[b_sarb[i]], writes=[b_szb[i]])
        B.op("scalar", lambda q, i=i: q.activation(out=s_zq[i][:, :], in_=s_arb[i][:, :],
                                                   func=AF.Square),
             reads=[b_sarb[i]], writes=[b_szq[i]])
        for dc in range(32):
            B.op("tensor", lambda q, dc=dc, i=i: q.matmul(
                p_m[0][0:1, 0:128], s_invd[:, :], s_zb[i][:, dc * 128:(dc + 1) * 128],
                start=(dc == 0), stop=(dc == 31)), reads=[b_sinvd, b_szb[i]], writes=[p_m[1]])
        for dc in range(32):
            B.op("tensor", lambda q, dc=dc, i=i: q.matmul(
                p_q[0][0:1, 0:128], s_invd[:, :], s_zq[i][:, dc * 128:(dc + 1) * 128],
                start=(dc == 0), stop=(dc == 31)), reads=[b_sinvd, b_szq[i]], writes=[p_q[1]])
        ln_rows(i, p_m, p_q, p_s1[i], p_s2[i])
        for dc in range(32):
            B.op("vector", lambda q, dc=dc, i=i: q.tensor_tensor(
                out=s_hfm[i][:, dc * 128:(dc + 1) * 128],
                in0=s_arb[i][:, dc * 128:(dc + 1) * 128],
                in1=p_s1[i][0][:, 0:128], op=ALU.mult),
                reads=[b_sarb[i], p_s1[i][1]], writes=[b_shfm[i]])
        B.op("vector", lambda q, i=i: q.tensor_tensor(
            out=s_hfm[i][:, :].rearrange("p (a t) -> p a t", a=32),
            in0=s_hfm[i][:, :].rearrange("p (a t) -> p a t", a=32),
            in1=p_s2[i][0][:, 0:128].unsqueeze(1).broadcast_to([128, 32, 128]),
            op=ALU.subtract), reads=[b_shfm[i], p_s2[i][1]], writes=[b_shfm[i]])
        dma("scalar", d_h.rearrange("(a p) t -> p a t", p=128)[:, :, tt * 128:(tt + 1) * 128],
            s_hfm[i][:, :].rearrange("p (a t) -> p a t", a=32),
            reads=[b_shfm[i]], writes=[b_dh], group="g_sp")
        B.op("scalar", lambda q, tt=tt, i=i: q.activation(
            out=s_hbf[:, :].rearrange("p (a t) -> p a t", a=32)[:, :, tt * 128:(tt + 1) * 128],
            in_=s_hfm[i][:, :].rearrange("p (a t) -> p a t", a=32), func=AF.Copy),
            reads=[b_shfm[i]], writes=[b_shbf])
    B.flush(block)
    snap6 = B.snap()
    B.barrier(snap6)
    st6.close()
    if PHASE_LIMIT <= 6:
        B.finalize(block)
        sem_stack.close()
        return nc

    st7 = contextlib.ExitStack()
    s_wbb, b_wbb = [], []
    for i in range(2):
        h_, bb = sb(f"wbb{i}", [128, 32 * 128], BF16, st7)
        s_wbb.append(h_)
        b_wbb.append(bb)
    s_hrb, b_shrb = [], []
    for i in range(2):
        h_, bb = sb(f"hrb{i}", [128, 512], F32, st7)
        s_hrb.append(h_)
        b_shrb.append(bb)
    s_z2e, b_sz2e = [], []
    for i in range(2):
        h_, bb = sb(f"z2e{i}", [128, 512], F32, st7)
        s_z2e.append(h_)
        b_sz2e.append(bb)
    p_ff = [ps(f"pff{i}", st7) for i in range(3)]
    p_e = ps("pe", st7)

    for tt in range(2):
        for kt in range(32):
            B.op("tensor", lambda q, kt=kt, tt=tt: q.matmul(
                p_e[0][0:40, 0:512], s_enc[:, kt * 40:(kt + 1) * 40],
                s_hbf[:, kt * T + tt * 512:kt * T + (tt + 1) * 512],
                start=(kt == 0), stop=(kt == 31)),
                reads=[b_senc, b_shbf], writes=[p_e[1]])
        B.op("vector", lambda q, tt=tt: q.tensor_copy(
            out=s_e01[0:40, tt * 512:(tt + 1) * 512], in_=p_e[0][0:40, 0:512]),
            reads=[p_e[1]], writes=[b_se01])
    for oc in range(32):
        wbb, bwbb = s_wbb[oc % 2], b_wbb[oc % 2]
        dma("gpsimd", wbb[:, :].rearrange("p (a b) -> p a b", a=32),
            w_baseT[oc].rearrange("a p b -> p a b"),
            reads=[b_w_base], writes=[bwbb], group="g_sw%d" % (oc % 2))
        for tt in range(2):
            pp, bpp = p_ff[(oc * 2 + tt) % 3]
            for kt in range(32):
                B.op("tensor", lambda q, pp=pp, wbb=wbb, kt=kt, tt=tt: q.matmul(
                    pp[:, 0:512], wbb[:, kt * 128:(kt + 1) * 128],
                    s_hbf[:, kt * T + tt * 512:kt * T + (tt + 1) * 512],
                    start=(kt == 0), stop=False),
                    reads=[bwbb, b_shbf], writes=[bpp])
            B.op("tensor", lambda q, pp=pp, oc=oc, tt=tt: q.matmul(
                pp[:, 0:512], s_fwt[:, oc * 128:(oc + 1) * 128],
                s_e01[0:40, tt * 512:(tt + 1) * 512], start=False, stop=True),
                reads=[b_sfwt, b_se01], writes=[bpp])
            hr, bhr = s_hrb[tt], b_shrb[tt]
            dma("sync", hr[:, :], d_h[oc * 128:(oc + 1) * 128, tt * 512:(tt + 1) * 512],
                reads=[b_dh], writes=[bhr])
            ze, bze = s_z2e[tt], b_sz2e[tt]
            B.op("vector", lambda q, pp=pp, hr=hr, ze=ze, oc=oc: q.scalar_tensor_tensor(
                out=ze[:, :], in0=pp[:, 0:512], scalar=s_bbc[:, oc:oc + 1], in1=hr[:, :],
                op0=ALU.add, op1=ALU.add), reads=[bpp, b_sbbc, bhr], writes=[bze])
            dma("scalar", d_z2[oc * 128:(oc + 1) * 128, tt * 512:(tt + 1) * 512],
                ze[:, :], reads=[bze], writes=[b_dz2], group="g_sp")
    B.flush(block)
    snap7 = B.snap()
    B.barrier(snap7)
    st7.close()
    stHB.close()
    if PHASE_LIMIT <= 7:
        B.finalize(block)
        sem_stack.close()
        return nc

    st8 = contextlib.ExitStack()
    s_z2, b_sz2, s_zb2, b_szb2, s_zq2, b_szq2, s_otm, b_sotm = [], [], [], [], [], [], [], []
    s_mr2, b_smr2, s_qr2, b_sqr2, s_vr2, b_svr2 = [], [], [], [], [], []
    for i in range(2):
        t_, bb = sb(f"z2fm{i}", [128, D], F32, st8)
        s_z2.append(t_); b_sz2.append(bb)
        t_, bb = sb(f"zbf2{i}", [128, D], BF16, st8)
        s_zb2.append(t_); b_szb2.append(bb)
        t_, bb = sb(f"zsq2{i}", [128, D], BF16, st8)
        s_zq2.append(t_); b_szq2.append(bb)
        t_, bb = sb(f"otm{i}", [128, D], F32, st8)
        s_otm.append(t_); b_sotm.append(bb)
        t_, bb = sb(f"mr2{i}", [1, 128], F32, st8)
        s_mr2.append(t_); b_smr2.append(bb)
        t_, bb = sb(f"qr2{i}", [1, 128], F32, st8)
        s_qr2.append(t_); b_sqr2.append(bb)
        t_, bb = sb(f"vr2{i}", [1, 128], F32, st8)
        s_vr2.append(t_); b_svr2.append(bb)
    p_tx2 = [ps("ptx20", st8), ps("ptx21", st8)]
    p_m2 = ps("pm2", st8)
    p_q2 = ps("pq2", st8)
    p_s12 = [ps("ps12a", st8), ps("ps12b", st8)]
    p_s22 = [ps("ps22a", st8), ps("ps22b", st8)]

    for tt in range(8):
        i = tt % 2
        dma("sync", s_z2[i][:, :].rearrange("p (a t) -> p a t", a=32),
            d_z2.rearrange("(a p) t -> p a t", p=128)[:, :, tt * 128:(tt + 1) * 128],
            reads=[b_dz2], writes=[b_sz2[i]])
        B.op("scalar", lambda q, i=i: q.activation(out=s_zb2[i][:, :], in_=s_z2[i][:, :],
                                                   func=AF.Copy),
             reads=[b_sz2[i]], writes=[b_szb2[i]])
        B.op("scalar", lambda q, i=i: q.activation(out=s_zq2[i][:, :], in_=s_z2[i][:, :],
                                                   func=AF.Square),
             reads=[b_sz2[i]], writes=[b_szq2[i]])
        for dc in range(32):
            B.op("tensor", lambda q, dc=dc, i=i: q.matmul(
                p_m2[0][0:1, 0:128], s_invd[:, :], s_zb2[i][:, dc * 128:(dc + 1) * 128],
                start=(dc == 0), stop=(dc == 31)), reads=[b_sinvd, b_szb2[i]],
                writes=[p_m2[1]])
        for dc in range(32):
            B.op("tensor", lambda q, dc=dc, i=i: q.matmul(
                p_q2[0][0:1, 0:128], s_invd[:, :], s_zq2[i][:, dc * 128:(dc + 1) * 128],
                start=(dc == 0), stop=(dc == 31)), reads=[b_sinvd, b_szq2[i]],
                writes=[p_q2[1]])
        B.op("vector", lambda q, i=i: q.tensor_copy(out=s_mr2[i][:, :], in_=p_m2[0][0:1, 0:128]),
             reads=[p_m2[1]], writes=[b_smr2[i]])
        B.op("vector", lambda q, i=i: q.tensor_copy(out=s_qr2[i][:, :], in_=p_q2[0][0:1, 0:128]),
             reads=[p_q2[1]], writes=[b_sqr2[i]])
        B.op("vector", lambda q, i=i: q.tensor_mul(out=s_vr2[i][:, :], in0=s_mr2[i][:, :],
                                                   in1=s_mr2[i][:, :]),
             reads=[b_smr2[i]], writes=[b_svr2[i]])
        B.op("vector", lambda q, i=i: q.tensor_sub(out=s_vr2[i][:, :], in0=s_qr2[i][:, :],
                                                   in1=s_vr2[i][:, :]),
             reads=[b_sqr2[i], b_svr2[i]], writes=[b_svr2[i]])
        B.op("scalar", lambda q, i=i: q.activation(out=s_vr2[i][:, :], in_=s_vr2[i][:, :],
                                                   func=AF.Ln, bias=s_epsc[0:1, :]),
             reads=[b_svr2[i], b_sepsc], writes=[b_svr2[i]])
        B.op("scalar", lambda q, i=i: q.activation(out=s_vr2[i][:, :], in_=s_vr2[i][:, :],
                                                   func=AF.Exp, scale=-0.5),
             reads=[b_svr2[i]], writes=[b_svr2[i]])
        B.op("vector", lambda q, i=i: q.tensor_mul(out=s_mr2[i][:, :], in0=s_mr2[i][:, :],
                                                   in1=s_vr2[i][:, :]),
             reads=[b_smr2[i], b_svr2[i]], writes=[b_smr2[i]])
        B.op("tensor", lambda q, i=i: q.matmul(p_s12[i][0][:, 0:128], s_ones1[:, :],
                                               s_vr2[i][0:1, :], start=True, stop=True),
             reads=[b_ones1, b_svr2[i]], writes=[p_s12[i][1]])
        B.op("tensor", lambda q, i=i: q.matmul(p_s22[i][0][:, 0:128], s_ones1[:, :],
                                               s_mr2[i][0:1, :], start=True, stop=True),
             reads=[b_ones1, b_smr2[i]], writes=[p_s22[i][1]])
        for dc in range(32):
            B.op("vector", lambda q, dc=dc, i=i: q.tensor_tensor(
                out=s_z2[i][:, dc * 128:(dc + 1) * 128],
                in0=s_z2[i][:, dc * 128:(dc + 1) * 128],
                in1=p_s12[i][0][:, 0:128], op=ALU.mult),
                reads=[b_sz2[i], p_s12[i][1]], writes=[b_sz2[i]])
        B.op("vector", lambda q, i=i: q.tensor_tensor(
            out=s_z2[i][:, :].rearrange("p (a t) -> p a t", a=32),
            in0=s_z2[i][:, :].rearrange("p (a t) -> p a t", a=32),
            in1=p_s22[i][0][:, 0:128].unsqueeze(1).broadcast_to([128, 32, 128]),
            op=ALU.subtract), reads=[b_sz2[i], p_s22[i][1]], writes=[b_sz2[i]])
        for dc in range(32):
            ptx, bptx = p_tx2[dc % 2]
            B.op("tensor", lambda q, ptx=ptx, dc=dc, i=i: q.transpose(
                ptx[:, 0:128], s_z2[i][:, dc * 128:(dc + 1) * 128], s_ident[:, :]),
                reads=[b_sz2[i], b_ident], writes=[bptx])
            B.op("vector", lambda q, ptx=ptx, dc=dc, i=i: q.tensor_copy(
                out=s_otm[i][:, dc * 128:(dc + 1) * 128], in_=ptx[:, 0:128]),
                reads=[bptx], writes=[b_sotm[i]])
        dma("sync", out_ext[tt * 128:(tt + 1) * 128, :], s_otm[i][:, :],
            reads=[b_sotm[i]], writes=[b_out], group="g_sp")
    if KDEBUG:
        for nm, srcap, bsrc in [("dbg_x", d_x, b_dx), ("dbg_sg", d_sg, b_dsg),
                                ("dbg_b", d_b, b_db), ("dbg_attn", d_attn, b_dattn),
                                ("dbg_h", d_h, b_dh), ("dbg_cum", d_cum, b_dcum),
                                ("dbg_g", d_g, b_dg), ("dbg_z2", d_z2, b_dz2),
                                ("dbg_agb", d_agbo, b_agbo), ("dbg_agd", d_agdo, b_agdo)]:
            dma("sync", dbg[nm][0], srcap, reads=[bsrc], writes=[dbg[nm][1]], group="g_sp")
    B.op("gpsimd", lambda q: q.nop(), reads=[b_out], writes=[])
    B.flush(block)
    st8.close()

    B.finalize(block)
    sem_stack.close()
    stack_all.close()
    return nc


_NC_CACHE = None


def kernel(**inputs):
    global _NC_CACHE
    sh, per_core = _host_prep(inputs)
    if _NC_CACHE is None:
        _NC_CACHE = build_program()
    nc = _NC_CACHE
    in_maps = []
    for k in range(NCORES):
        m = dict(sh)
        m.update(per_core[k])
        in_maps.append(m)
    res = run_bass_kernel_spmd(nc, in_maps, core_ids=list(range(NCORES)))
    out = np.concatenate([np.asarray(res.results[k]["out"]) for k in range(NCORES)], axis=0)
    return out.reshape(1, L, D).astype(np.float32)


# version 10
# speedup vs baseline: 1.0369x; 1.0369x over previous
import contextlib
import math
import os
import numpy as np
import ml_dtypes
import concourse.bass as bass
import concourse.mybir as mybir
from concourse.bass_utils import run_bass_kernel_spmd

F32 = mybir.dt.float32
BF16 = mybir.dt.bfloat16
U8 = mybir.dt.uint8
AF = mybir.ActivationFunctionType
ALU = mybir.AluOpType
AX = mybir.AxisListType

NCORES = 8
D = 4096
L = 8192
T = 1024
HALO = 3
TH = T + HALO
THP = 1040
CS = 128
NCH = T // CS
H = 128
PH = 64
NST = 128
G = 8
INTER = H * PH
CONV = INTER + 2 * G * NST
PROJ = INTER + CONV + H
GS = INTER // G
EPS = 1e-5
SCALING = 0.5

NOC = PROJ // 128
OC_GATE0, OC_X0, OC_B0, OC_C0, OC_DT = 0, 64, 128, 136, 144
SPANS_XBC = [(0, 512), (512, 1024), (1024, 1027)]
SPANS_GD = [(3, 515), (515, 1027)]

ENGS = ("sync", "scalar", "vector", "tensor", "gpsimd")
GROUPS = ("g_const", "g_w0", "g_w1", "g_sp", "g_rd", "g_cc",
          "g_xc", "g_sw0", "g_sw1", "g_spg", "g_rdg")
SELF_SYNC = {"sync": False, "scalar": False, "vector": False, "tensor": False,
             "gpsimd": False}


class Buf:
    __slots__ = ("name", "ready", "readers")

    def __init__(self, name):
        self.name = name
        self.ready = None
        self.readers = []


class EngRec:
    __slots__ = ("name", "cum", "ops", "last_wait", "sem")

    def __init__(self, name):
        self.name = name
        self.cum = 0
        self.ops = []
        self.last_wait = {}
        self.sem = None


class Builder:
    def __init__(self, nc):
        self.nc = nc
        self.engs = {n: EngRec(n) for n in ENGS}
        self.gcum = {g: 0 for g in GROUPS}
        self.gsem = {}

    def buf(self, name):
        return Buf(name)

    def op(self, eng, closure, reads=(), writes=(), inc=1, group=None):
        e = self.engs[eng]
        waits = {}

        def need(tok):
            if tok is None:
                return
            s, v = tok
            if s in self.gcum:
                v = self.gcum[s]
                if v == 0:
                    return
            elif s == eng and not SELF_SYNC[eng]:
                return
            if e.last_wait.get(s, 0) >= v:
                return
            if waits.get(s, 0) < v:
                waits[s] = v

        for b in reads:
            need(b.ready)
        for b in writes:
            need(b.ready)
            for t in b.readers:
                need(t)
        for s, v in waits.items():
            e.last_wait[s] = v
        if group is not None:
            self.gcum[group] += inc
            tok = (group, self.gcum[group])
            e.ops.append((waits, closure, (group, inc)))
            e.cum += 0
        else:
            tok = (eng, e.cum + inc)
            e.ops.append((waits, closure, inc))
            e.cum += inc
        for b in reads:
            b.readers.append(tok)
        for b in writes:
            b.ready = tok
            b.readers = []
        return tok

    def snap(self):
        d = {n: self.engs[n].cum for n in ENGS}
        d.update({g: self.gcum[g] for g in GROUPS})
        return d

    def barrier(self, snap=None):
        if snap is None:
            snap = self.snap()
        for n in ENGS:
            e = self.engs[n]
            waits = {}
            for m, v in snap.items():
                if m == n or v == 0:
                    continue
                if e.last_wait.get(m, 0) >= v:
                    continue
                waits[m] = v
                e.last_wait[m] = v
            e.ops.append((waits, lambda q: q.nop(), 1))
            e.cum += 1

    def flush(self, block):
        return

    def finalize(self, block):
        waited = {n: set() for n in ENGS}
        for n in ENGS:
            for w, _, _ in self.engs[n].ops:
                for s, v in w.items():
                    if s in waited:
                        waited[s].add(v)
        CHUNK = 3000
        for n in ENGS:
            e = self.engs[n]
            if not e.ops:
                continue
            ops = e.ops
            e.ops = []
            sems = {m: self.engs[m].sem for m in ENGS}
            sems.update(self.gsem)
            mysem = e.sem
            wset = waited[n]
            nops = len(ops)
            state = {"pending": 0, "cum": 0}
            for lo in range(0, nops, CHUNK):
                hi = min(lo + CHUNK, nops)

                def body(q, lo=lo, hi=hi, ops=ops, sems=sems, mysem=mysem,
                         wset=wset, nops=nops, state=state, n=n):
                    for i in range(lo, hi):
                        waits, closure, inc = ops[i]
                        for s, v in sorted(waits.items()):
                            q.wait_ge(sems[s], v)
                        ins = closure(q)
                        if isinstance(inc, tuple):
                            ins.then_inc(sems[inc[0]], inc[1])
                            continue
                        state["pending"] += inc
                        state["cum"] += inc
                        dense = n != "tensor"
                        if (dense or state["cum"] in wset or i == nops - 1
                                or i == hi - 1):
                            ins.then_inc(mysem, state["pending"])
                            state["pending"] = 0

                getattr(block, n)(body)


def _build_FT_bf16():
    n = D
    t = np.arange(n)[:, None].astype(np.float64)
    F = np.zeros((n, n), np.float64)
    F[:, 0] = 1.0 / math.sqrt(n)
    maxf = (n - 1) // 2
    f = np.arange(1, maxf + 1)[None, :].astype(np.float64)
    ang = 2.0 * np.pi * f * t / n
    F[:, 1:1 + 2 * maxf:2] = math.sqrt(2.0 / n) * np.cos(ang)
    F[:, 2:2 + 2 * maxf:2] = math.sqrt(2.0 / n) * np.sin(ang)
    F[:, n - 1] = ((-1.0) ** np.arange(n)) / math.sqrt(n)
    return np.ascontiguousarray(F.T).astype(ml_dtypes.bfloat16)


_FT_CACHE = None


def _host_prep(inputs):
    global _FT_CACHE
    x = np.asarray(inputs["L0_attn_input_states"], np.float32).reshape(L, D)
    w_in = np.asarray(inputs["L0_attn_in_proj_w"], np.float32)
    conv_w = np.asarray(inputs["L0_attn_conv_w"], np.float32)
    conv_b = np.asarray(inputs["L0_attn_conv_b"], np.float32)
    dt_bias = np.asarray(inputs["L0_attn_dt_bias"], np.float32)
    A_log = np.asarray(inputs["L0_attn_A_log"], np.float32)
    Dp = np.asarray(inputs["L0_attn_D"], np.float32)
    norm_w = np.asarray(inputs["L0_attn_norm_w"], np.float32)
    out_w = np.asarray(inputs["L0_attn_out_w"], np.float32)
    w_base = np.asarray(inputs["L0_ffn_W_base"], np.float32)
    b_base = np.asarray(inputs["L0_ffn_b_base"], np.float32)
    enc0 = np.asarray(inputs["L0_ffn_enc0"], np.float32)
    enc1 = np.asarray(inputs["L0_ffn_enc1"], np.float32)
    c0 = np.asarray(inputs["L0_ffn_coefs0"], np.float32)
    c1 = np.asarray(inputs["L0_ffn_coefs1"], np.float32)
    m0 = np.asarray(inputs["L0_ffn_mask0"])
    m1 = np.asarray(inputs["L0_ffn_mask1"])

    sh = {}
    wiT = np.ascontiguousarray(w_in.T)
    sh["w_in"] = np.ascontiguousarray(wiT.reshape(32, 128, NOC, 128).transpose(2, 0, 1, 3))
    woT = np.ascontiguousarray(out_w.T)
    sh["w_out"] = np.ascontiguousarray(woT.reshape(64, 128, 32, 128).transpose(2, 0, 1, 3))
    wbT = np.ascontiguousarray(w_base.T)
    sh["w_base"] = np.ascontiguousarray(wbT.reshape(32, 128, 32, 128).transpose(2, 0, 1, 3))
    if _FT_CACHE is None:
        _FT_CACHE = _build_FT_bf16()
    sh["ft"] = _FT_CACHE
    sh["enc"] = np.ascontiguousarray(np.concatenate([enc0, enc1], axis=0).T)
    sh["c01"] = np.ascontiguousarray(np.concatenate(
        [c0.transpose(1, 2, 0).reshape(D, 96),
         c1.transpose(1, 2, 0).reshape(D, 24)], axis=1))
    sh["m01"] = np.ascontiguousarray(np.concatenate(
        [m0.transpose(1, 2, 0).reshape(D, 96),
         m1.transpose(1, 2, 0).reshape(D, 24)], axis=1)).view(np.uint8)
    sh["cw"] = np.ascontiguousarray(conv_w.reshape(80, 128, 4).transpose(1, 0, 2).reshape(128, 320))
    sh["cb"] = np.ascontiguousarray(conv_b.reshape(80, 128).T)
    sh["dtb"] = np.ascontiguousarray(dt_bias.reshape(128, 1))
    sh["alog"] = np.ascontiguousarray(A_log.reshape(128, 1))
    dexp = np.empty((128, 64), np.float32)
    for oc in range(64):
        dexp[0:64, oc] = Dp[2 * oc]
        dexp[64:128, oc] = Dp[2 * oc + 1]
    sh["dexp"] = dexp
    sh["nwc"] = np.ascontiguousarray(norm_w.reshape(64, 128).T)
    sh["bbc"] = np.ascontiguousarray(b_base.reshape(32, 128).T)
    sh["ident"] = np.eye(128, dtype=np.float32)
    sh["ones1"] = np.ones((1, 128), np.float32)
    sh["onest"] = np.ones((128, CS), np.float32)
    sh["tril"] = np.ascontiguousarray(
        (np.arange(CS)[None, :] >= np.arange(128)[:, None]).astype(np.float32))
    sh["invgs"] = np.full((128, 1), 1.0 / GS, np.float32).astype(ml_dtypes.bfloat16)
    sh["invd"] = np.full((128, 1), 1.0 / D, np.float32).astype(ml_dtypes.bfloat16)
    sh["epsc"] = np.full((128, 1), EPS, np.float32)
    sh["onec"] = np.ones((128, 1), np.float32)

    per_core = []
    for k in range(NCORES):
        d = {}
        s = k * T
        if k == 0:
            xs = np.concatenate([np.zeros((HALO, D), np.float32), x[0:T]], axis=0)
        else:
            xs = x[s - HALO:s + T]
        d["x"] = np.ascontiguousarray(xs)
        pmt = np.zeros((8, 8), np.float32)
        for m in range(8):
            for j in range(8):
                if j < m < k:
                    pmt[m, j] = 1.0
        d["pmt"] = pmt
        zm = np.zeros((128, 8), np.float32)
        zm[:, 0:k] = 1.0
        d["zmask"] = zm
        per_core.append(d)
    return sh, per_core


PHASE_LIMIT = int(os.environ.get("KPHASES", "99"))
P4SUB = int(os.environ.get("KP4SUB", "99"))
P4OPS = int(os.environ.get("KP4OPS", "99"))
P4Y = int(os.environ.get("KP4Y", "2"))


def build_program():
    nc = bass.Bass("TRN2", target_bir_lowering=False, debug=False)
    B = Builder(nc)

    def ein(name, shape, dt):
        return nc.dram_tensor(name, shape, dt, kind="ExternalInput").ap(), B.buf("d:" + name)

    x_in, b_x_in = ein("x", [TH, D], F32)
    w_in, b_w_in = ein("w_in", [NOC, 32, 128, 128], F32)
    w_out, b_w_out = ein("w_out", [32, 64, 128, 128], F32)
    w_baseT, b_w_base = ein("w_base", [32, 32, 128, 128], F32)
    ft_in, b_ft = ein("ft", [D, D], BF16)
    enc_in, b_enc = ein("enc", [D, 40], F32)
    c01_in, b_c01 = ein("c01", [D, 120], F32)
    m01_in, b_m01 = ein("m01", [D, 120], U8)
    cw_in, b_cw = ein("cw", [128, 320], F32)
    cb_in, b_cb = ein("cb", [128, 80], F32)
    dtb_in, b_dtb = ein("dtb", [128, 1], F32)
    alog_in, b_alog = ein("alog", [128, 1], F32)
    dexp_in, b_dexp = ein("dexp", [128, 64], F32)
    nwc_in, b_nwc = ein("nwc", [128, 64], F32)
    bbc_in, b_bbc = ein("bbc", [128, 32], F32)
    ident_in, b_ident_in = ein("ident", [128, 128], F32)
    ones1_in, b_ones1_in = ein("ones1", [1, 128], F32)
    onest_in, b_onest_in = ein("onest", [128, CS], F32)
    tril_in, b_tril_in = ein("tril", [128, CS], F32)
    invgs_in, b_invgs_in = ein("invgs", [128, 1], BF16)
    invd_in, b_invd_in = ein("invd", [128, 1], BF16)
    epsc_in, b_epsc_in = ein("epsc", [128, 1], F32)
    onec_in, b_onec_in = ein("onec", [128, 1], F32)
    pmt_in, b_pmt_in = ein("pmt", [8, 8], F32)
    zmask_in, b_zmask_in = ein("zmask", [128, 8], F32)

    out_ext = nc.dram_tensor("out", [T, D], F32, kind="ExternalOutput").ap()
    b_out = B.buf("d:out")
    KDEBUG = os.environ.get("KDEBUG", "") != ""
    dbg = {}
    if KDEBUG:
        for nm, shape, dt in [("dbg_x", [INTER, T], BF16), ("dbg_sg", [INTER, T], BF16),
                              ("dbg_b", [G * NST, T], BF16), ("dbg_attn", [D, T], F32),
                              ("dbg_h", [D, T], F32), ("dbg_cum", [NCH, H, CS], F32),
                              ("dbg_g", [INTER, T], BF16), ("dbg_z2", [D, T], F32),
                              ("dbg_agb", [NCORES * NST, INTER], BF16),
                              ("dbg_agd", [NCORES, H], F32)]:
            dbg[nm] = (nc.dram_tensor(nm, shape, dt, kind="ExternalOutput").ap(),
                       B.buf("d:" + nm))

    def idram(name, shape, dt):
        return nc.dram_tensor(name, shape, dt).ap(), B.buf("i:" + name)

    d_xbf, b_xbf = idram("d_xbf", [THP, D], BF16)
    d_x, b_dx = idram("d_x", [INTER, T], BF16)
    d_b, b_db = idram("d_b", [G * NST, T], BF16)
    d_sg, b_dsg = idram("d_sg", [INTER, T], BF16)
    d_g, b_dg = idram("d_g", [INTER, T], BF16)
    d_attn, b_dattn = idram("d_attn", [D, T], F32)
    d_h, b_dh = idram("d_h", [D, T], F32)
    d_z2, b_dz2 = idram("d_z2", [D, T], F32)
    d_cum, b_dcum = idram("d_cum", [NCH, H, CS], F32)
    d_agbi, b_agbi = idram("d_agbi", [NST, INTER], BF16)
    d_agbo, b_agbo = idram("d_agbo", [NCORES * NST, INTER], BF16)
    d_agdi, b_agdi = idram("d_agdi", [1, H], F32)
    d_agdo, b_agdo = idram("d_agdo", [NCORES, H], F32)

    stack_all = contextlib.ExitStack()

    def sb(name, shape, dt, st=None):
        hdl = (st or stack_all).enter_context(nc.sbuf_tensor("sb_" + name, shape, dt))
        return hdl, B.buf("s:" + name)

    def ps(name, st=None):
        hdl = (st or stack_all).enter_context(nc.psum_tensor("ps_" + name, [128, 512], F32))
        return hdl, B.buf("p:" + name)

    s_ident, b_ident = sb("ident", [128, 128], F32)
    s_ones1, b_ones1 = sb("ones1", [1, 128], F32)
    s_onest, b_onest = sb("onest", [128, CS], F32)
    s_tril, b_tril = sb("tril", [128, CS], F32)
    s_cw, b_scw = sb("cw", [128, 320], F32)
    s_cb, b_scb = sb("cb", [128, 80], F32)
    s_dtb, b_sdtb = sb("dtb", [128, 1], F32)
    s_alog, b_salog = sb("alog", [128, 1], F32)
    s_dexp, b_sdexp = sb("dexp", [128, 64], F32)
    s_nwc, b_snwc = sb("nwc", [128, 64], F32)
    s_bbc, b_sbbc = sb("bbc", [128, 32], F32)
    s_invgs, b_sinvgs = sb("invgs", [128, 1], BF16)
    s_invd, b_sinvd = sb("invd", [128, 1], BF16)
    s_epsc, b_sepsc = sb("epsc", [128, 1], F32)
    s_onec, b_sonec = sb("onec", [128, 1], F32)
    s_pmt, b_spmt = sb("pmt", [8, 8], F32)
    s_zmask, b_szmask = sb("zmask", [128, 8], F32)
    s_negA, b_negA = sb("negA", [128, 1], F32)
    s_row, b_srow = sb("rowtmp", [1, 128], F32)
    s_enc, b_senc = sb("encbf", [128, 32 * 40], BF16)
    s_fwt, b_sfwt = sb("fwt", [40, D], BF16)
    s_e01, b_se01 = sb("e01", [40, T], BF16)

    sem_stack = contextlib.ExitStack()
    for n in ENGS:
        B.engs[n].sem = sem_stack.enter_context(nc.semaphore("sem_" + n))
    for g in GROUPS:
        B.gsem[g] = sem_stack.enter_context(nc.semaphore("sem_" + g))
    block = sem_stack.enter_context(nc.Block())

    def dma(eng, out_ap, in_ap, reads, writes, group="g_rd", **kw):
        return B.op(eng, lambda q, o=out_ap, i=in_ap, kw=kw: q.dma_start(out=o, in_=i, **kw),
                    reads=reads, writes=writes, inc=16, group=group)

    st14 = contextlib.ExitStack()
    s_dt, b_sdt = sb("dtraw", [128, TH], F32, st14)
    s_cumT, b_scumT = sb("cumT", [128, NCH * CS], F32, st14)
    s_dtpT, b_sdtpT = sb("dtpT", [128, NCH * CS], BF16, st14)
    s_dlastT, b_sdlastT = sb("dlastT", [128, NCH * CS], BF16, st14)
    s_fT, b_sfT = sb("fT", [128, CS], BF16, st14)
    s_cdrow, b_scdrow = sb("cdrow", [1, NCH * H], F32, st14)
    s_B, b_sB = sb("Bfm", [128, G * T], BF16, st14)
    s_C, b_sC = sb("Cfm", [128, G * T], BF16, st14)
    st1 = contextlib.ExitStack()
    s_xfm, b_xfm = [], []
    for kt in range(32):
        h_, bb = sb(f"xfm{kt}", [128, THP], BF16, st1)
        s_xfm.append(h_)
        b_xfm.append(bb)
    s_wbuf, b_wbuf = [], []
    for i in range(2):
        h_, bb = sb(f"win{i}", [128, 32 * 128], BF16, st1)
        s_wbuf.append(h_)
        b_wbuf.append(bb)
    p_ip = [ps(f"pip{i}", st1) for i in range(3)]
    s_pre, b_pre = [], []
    s_cvt, b_cvt = [], []
    s_evo, b_evo = [], []
    for i in range(2):
        h_, bb = sb(f"pre{i}", [128, TH], BF16, st1)
        s_pre.append(h_)
        b_pre.append(bb)
        h_, bb = sb(f"cvt{i}", [128, T], BF16, st1)
        s_cvt.append(h_)
        b_cvt.append(bb)
        h_, bb = sb(f"evo{i}", [128, TH], BF16, st1)
        s_evo.append(h_)
        b_evo.append(bb)

    for s_t, d_ap, b_s, b_d in [
        (s_ident, ident_in, b_ident, b_ident_in), (s_ones1, ones1_in, b_ones1, b_ones1_in),
        (s_onest, onest_in, b_onest, b_onest_in), (s_tril, tril_in, b_tril, b_tril_in),
        (s_cw, cw_in, b_scw, b_cw), (s_cb, cb_in, b_scb, b_cb),
        (s_dtb, dtb_in, b_sdtb, b_dtb), (s_alog, alog_in, b_salog, b_alog),
        (s_dexp, dexp_in, b_sdexp, b_dexp), (s_nwc, nwc_in, b_snwc, b_nwc),
        (s_bbc, bbc_in, b_sbbc, b_bbc), (s_invgs, invgs_in, b_sinvgs, b_invgs_in),
        (s_invd, invd_in, b_sinvd, b_invd_in), (s_epsc, epsc_in, b_sepsc, b_epsc_in),
        (s_onec, onec_in, b_sonec, b_onec_in), (s_pmt, pmt_in, b_spmt, b_pmt_in),
        (s_zmask, zmask_in, b_szmask, b_zmask_in),
    ]:
        dma("sync", s_t[:, :], d_ap, reads=[b_d], writes=[b_s], group="g_const")

    B.op("scalar", lambda q: q.activation(out=s_negA[:, :], in_=s_alog[:, :], func=AF.Exp),
         reads=[b_salog], writes=[b_negA])
    B.op("vector", lambda q: q.tensor_scalar(out=s_negA[:, :], in0=s_negA[:, :],
                                             scalar1=-1.0, scalar2=None, op0=ALU.mult),
         reads=[b_negA], writes=[b_negA])

    dma("gpsimd", d_xbf[0:TH, :], x_in, reads=[b_x_in], writes=[b_xbf], group="g_xc")
    for kt in range(32):
        dma("sync", s_xfm[kt][:, :], d_xbf[:, kt * 128:(kt + 1) * 128],
            reads=[b_xbf], writes=[b_xfm[kt]], transpose=True, group="g_const")

    for oc in range(NOC):
        wb, bwb = s_wbuf[oc % 2], b_wbuf[oc % 2]
        dma("gpsimd", wb[:, :].rearrange("p (a b) -> p a b", a=32),
            w_in[oc].rearrange("a p b -> p a b"),
            reads=[b_w_in], writes=[bwb], group="g_sw%d" % (oc % 2))
        par = oc % 2
        spans = SPANS_XBC if (OC_X0 <= oc < OC_DT) else SPANS_GD
        for ti, (t0, t1) in enumerate(spans):
            pp, bpp = p_ip[(oc * 3 + ti) % 3]
            n = t1 - t0
            for k in range(32):
                B.op("tensor",
                     lambda q, pp=pp, wb=wb, k=k, t0=t0, t1=t1: q.matmul(
                         pp[:, 0:(t1 - t0)], wb[:, k * 128:(k + 1) * 128],
                         s_xfm[k][:, t0:t1], start=(k == 0), stop=(k == 31)),
                     reads=[bwb, b_xfm[k]], writes=[bpp])
            if oc < OC_X0:
                B.op("scalar", lambda q, pp=pp, t0=t0, n=n, par=par: q.activation(
                    out=s_evo[par][:, t0:t0 + n], in_=pp[:, 0:n], func=AF.Silu),
                    reads=[bpp], writes=[b_evo[par]])
            elif oc == OC_DT:
                B.op("scalar", lambda q, pp=pp, t0=t0, n=n: q.activation(
                    out=s_dt[:, t0:t0 + n], in_=pp[:, 0:n], func=AF.Copy),
                    reads=[bpp], writes=[b_sdt])
            else:
                B.op("scalar", lambda q, pp=pp, t0=t0, n=n, par=par: q.activation(
                    out=s_pre[par][:, t0:t0 + n], in_=pp[:, 0:n], func=AF.Copy),
                    reads=[bpp], writes=[b_pre[par]])
        if oc < OC_X0:
            dma("sync", d_sg[oc * 128:(oc + 1) * 128, :], s_evo[par][:, HALO:TH],
                reads=[b_evo[par]], writes=[b_dsg], group="g_sp")
        elif oc == OC_DT:
            pass
        else:
            co = oc - OC_X0
            B.op("vector", lambda q, co=co, par=par: q.tensor_scalar(
                out=s_cvt[par][:, :], in0=s_pre[par][:, 3:3 + T],
                scalar1=s_cw[:, 4 * co + 3:4 * co + 4], scalar2=None, op0=ALU.mult),
                reads=[b_pre[par], b_scw], writes=[b_cvt[par]])
            for kk in (2, 1, 0):
                B.op("vector", lambda q, co=co, kk=kk, par=par: q.scalar_tensor_tensor(
                    out=s_cvt[par][:, :], in0=s_pre[par][:, kk:kk + T],
                    scalar=s_cw[:, 4 * co + kk:4 * co + kk + 1], in1=s_cvt[par][:, :],
                    op0=ALU.mult, op1=ALU.add),
                    reads=[b_pre[par], b_scw, b_cvt[par]], writes=[b_cvt[par]])
            B.op("scalar", lambda q, co=co, par=par: q.activation(
                out=s_evo[par][:, 0:T], in_=s_cvt[par][:, :], func=AF.Silu,
                bias=s_cb[:, co:co + 1]), reads=[b_cvt[par], b_scb], writes=[b_evo[par]])
            if oc < OC_B0:
                dma("sync", d_x[co * 128:(co + 1) * 128, :], s_evo[par][:, 0:T],
                    reads=[b_evo[par]], writes=[b_dx], group="g_sp")
            elif oc < OC_C0:
                g = oc - OC_B0
                B.op("vector", lambda q, g=g, par=par: q.tensor_copy(
                    out=s_B[:, g * T:(g + 1) * T], in_=s_evo[par][:, 0:T]),
                    reads=[b_evo[par]], writes=[b_sB])
                dma("sync", d_b[g * 128:(g + 1) * 128, :], s_evo[par][:, 0:T],
                    reads=[b_evo[par]], writes=[b_db], group="g_sp")
            else:
                g = oc - OC_C0
                B.op("vector", lambda q, g=g, par=par: q.tensor_copy(
                    out=s_C[:, g * T:(g + 1) * T], in_=s_evo[par][:, 0:T]),
                    reads=[b_evo[par]], writes=[b_sC])

    B.flush(block)
    snap1 = B.snap()
    B.barrier(snap1)
    st1.close()
    if PHASE_LIMIT <= 1:
        B.finalize(block)
        sem_stack.close()
        return nc

    st2 = contextlib.ExitStack()
    s_dtp, b_sdtp = sb("dtp", [128, T], F32, st2)
    s_dA, b_sdA = sb("dA", [128, T], F32, st2)
    s_brun, b_brun = sb("brun", [128, INTER], F32, st2)
    s_bigT, b_bigT = sb("bigT", [128, INTER], BF16, st2)
    s_BT, b_sBT = sb("BT", [128, G * NST], BF16, st2)
    s_cum, b_scum = sb("cumc", [128, CS], F32, st2)
    s_tmpA, b_tmpA = sb("tmpA", [128, CS], F32, st2)
    s_tmpB, b_tmpB = sb("tmpB", [128, 4], F32, st2)
    p_tr0 = ps("ptr0", st2)
    p_tr1 = ps("ptr1", st2)
    p_row = ps("prow", st2)
    p_cdb = ps("pcdb", st2)
    p_st = [ps("pst0", st2), ps("pst1", st2)]
    p_dtot = ps("pdtot", st2)

    B.op("vector", lambda q: q.tensor_scalar(
        out=s_dtp[:, :], in0=s_dt[:, HALO:TH], scalar1=s_dtb[:, :], scalar2=None,
        op0=ALU.add), reads=[b_sdt, b_sdtb], writes=[b_sdtp])
    B.op("scalar", lambda q: q.activation(out=s_dA[:, :], in_=s_dtp[:, :], func=AF.Abs),
         reads=[b_sdtp], writes=[b_sdA])
    B.op("scalar", lambda q: q.activation(out=s_dA[:, :], in_=s_dA[:, :], func=AF.Exp,
                                          scale=-1.0), reads=[b_sdA], writes=[b_sdA])
    B.op("scalar", lambda q: q.activation(out=s_dA[:, :], in_=s_dA[:, :], func=AF.Ln,
                                          bias=s_onec[:, :]),
         reads=[b_sdA, b_sonec], writes=[b_sdA])
    B.op("vector", lambda q: q.tensor_scalar(
        out=s_dtp[:, :], in0=s_dtp[:, :], scalar1=0.0, scalar2=None, op0=ALU.max),
        reads=[b_sdtp], writes=[b_sdtp])
    B.op("vector", lambda q: q.tensor_add(out=s_dtp[:, :], in0=s_dtp[:, :], in1=s_dA[:, :]),
         reads=[b_sdtp, b_sdA], writes=[b_sdtp])
    B.op("vector", lambda q: q.tensor_scalar(
        out=s_dA[:, :], in0=s_dtp[:, :], scalar1=s_negA[:, :], scalar2=None, op0=ALU.mult),
        reads=[b_sdtp, b_negA], writes=[b_sdA])
    B.op("vector", lambda q: q.memset(s_brun[:, :], 0.0), reads=[], writes=[b_brun])

    for c in range(NCH):
        c0, c1 = c * CS, (c + 1) * CS
        B.op("vector", lambda q, c0=c0, c1=c1: q.tensor_tensor_scan(
            out=s_cum[:, :], data0=s_onest[:, :], data1=s_dA[:, c0:c1],
            initial=0.0, op0=ALU.mult, op1=ALU.add), reads=[b_onest, b_sdA], writes=[b_scum])
        dma("sync", d_cum[c], s_cum[:, :], reads=[b_scum], writes=[b_dcum], group="g_sp")
        B.op("tensor", lambda q: q.transpose(p_tr0[0][:, 0:CS], s_cum[:, :], s_ident[:, :]),
             reads=[b_scum, b_ident], writes=[p_tr0[1]])
        B.op("vector", lambda q, c0=c0, c1=c1: q.tensor_copy(
            out=s_cumT[:, c0:c1], in_=p_tr0[0][:, 0:CS]),
            reads=[p_tr0[1]], writes=[b_scumT])
        B.op("scalar", lambda q: q.activation(
            out=s_tmpA[:, :], in_=s_cum[:, :], func=AF.Exp, scale=-1.0,
            bias=s_cum[:, CS - 1:CS]), reads=[b_scum], writes=[b_tmpA])
        B.op("tensor", lambda q, c0=c0, c1=c1: q.transpose(
            p_tr1[0][:, 0:CS], s_dtp[:, c0:c1], s_ident[:, :]),
            reads=[b_sdtp, b_ident], writes=[p_tr1[1]])
        B.op("vector", lambda q, c0=c0, c1=c1: q.tensor_copy(
            out=s_dtpT[:, c0:c1], in_=p_tr1[0][:, 0:CS]),
            reads=[p_tr1[1]], writes=[b_sdtpT])
        B.op("tensor", lambda q: q.transpose(p_tr1[0][:, 0:CS], s_tmpA[:, :], s_ident[:, :]),
             reads=[b_tmpA, b_ident], writes=[p_tr1[1]])
        B.op("vector", lambda q, c0=c0, c1=c1: q.tensor_copy(
            out=s_dlastT[:, c0:c1], in_=p_tr1[0][:, 0:CS]),
            reads=[p_tr1[1]], writes=[b_sdlastT])
        B.op("vector", lambda q, c0=c0, c1=c1: q.reduce_sum(
            out=s_tmpB[:, 0:1], in_=s_dA[:, c0:c1], axis=AX.X),
            reads=[b_sdA], writes=[b_tmpB])
        B.op("tensor", lambda q, c=c: q.matmul(
            p_dtot[0][0:1, 0:128], s_tmpB[:, 0:1], s_ident[:, :],
            start=(c == 0), stop=(c == NCH - 1)),
            reads=[b_tmpB, b_ident], writes=[p_dtot[1]])
        B.op("scalar", lambda q: q.activation(out=s_tmpB[:, 1:2], in_=s_tmpB[:, 0:1],
                                              func=AF.Exp), reads=[b_tmpB], writes=[b_tmpB])
        B.op("tensor", lambda q: q.matmul(
            p_row[0][0:1, 0:128], s_tmpB[:, 1:2], s_ident[:, :], start=True, stop=True),
            reads=[b_tmpB, b_ident], writes=[p_row[1]])
        B.op("vector", lambda q, c=c: q.tensor_copy(
            out=s_cdrow[:, c * H:(c + 1) * H], in_=p_row[0][0:1, 0:128]),
            reads=[p_row[1]], writes=[b_scdrow])
        if c > 0:
            B.op("tensor", lambda q, c=c: q.matmul(
                p_cdb[0][:, 0:128], s_ones1[:, :], s_cdrow[0:1, c * H:(c + 1) * H],
                start=True, stop=True), reads=[b_ones1, b_scdrow], writes=[p_cdb[1]])
            B.op("vector", lambda q: q.tensor_tensor(
                out=s_brun[:, :].rearrange("p (h a) -> p h a", h=H),
                in0=s_brun[:, :].rearrange("p (h a) -> p h a", h=H),
                in1=p_cdb[0][:, 0:128].unsqueeze(2).broadcast_to([128, H, PH]),
                op=ALU.mult), reads=[b_brun, p_cdb[1]], writes=[b_brun])
        dma("sync", s_bigT[:, :], d_x[:, c0:c1], reads=[b_dx], writes=[b_bigT], transpose=True)
        B.op("vector", lambda q, c0=c0, c1=c1: q.tensor_mul(
            out=s_fT[:, :], in0=s_dtpT[:, c0:c1], in1=s_dlastT[:, c0:c1]),
            reads=[b_sdtpT, b_sdlastT], writes=[b_sfT])
        B.op("vector", lambda q: q.tensor_tensor(
            out=s_bigT[:, :].rearrange("p (h a) -> p h a", h=H),
            in0=s_bigT[:, :].rearrange("p (h a) -> p h a", h=H),
            in1=s_fT[:, :].unsqueeze(2).broadcast_to([128, H, PH]),
            op=ALU.mult), reads=[b_bigT, b_sfT], writes=[b_bigT])
        dma("sync", s_BT[:, :], d_b[:, c0:c1], reads=[b_db], writes=[b_sBT], transpose=True)
        for hb in range(16):
            pst, bst = p_st[hb % 2]
            for hh in range(8):
                h = hb * 8 + hh
                g = h // 16
                B.op("tensor", lambda q, pst=pst, g=g, h=h, hh=hh: q.matmul(
                    pst[:, hh * PH:(hh + 1) * PH], s_BT[:, g * NST:(g + 1) * NST],
                    s_bigT[:, h * PH:(h + 1) * PH], start=True, stop=True),
                    reads=[b_sBT, b_bigT], writes=[bst])
            B.op("vector", lambda q, pst=pst, hb=hb: q.tensor_tensor(
                out=s_brun[:, hb * 512:(hb + 1) * 512], in0=s_brun[:, hb * 512:(hb + 1) * 512],
                in1=pst[:, :], op=ALU.add), reads=[bst, b_brun], writes=[b_brun])

    dma("gpsimd", d_agbi, s_brun[:, :], reads=[b_brun], writes=[b_agbi], group="g_spg")
    B.op("vector", lambda q: q.tensor_copy(out=s_row[:, :], in_=p_dtot[0][0:1, 0:128]),
         reads=[p_dtot[1]], writes=[b_srow])
    dma("sync", d_agdi, s_row[:, :], reads=[b_srow], writes=[b_agdi], group="g_sp")
    snap2 = B.snap()
    B.op("gpsimd", lambda q: q.collective_compute(
        "AllGather", ALU.bypass, replica_groups=[list(range(NCORES))],
        ins=[d_agbi], outs=[d_agbo]), reads=[b_agbi], writes=[b_agbo], inc=1,
        group="g_cc")
    B.op("gpsimd", lambda q: q.collective_compute(
        "AllGather", ALU.bypass, replica_groups=[list(range(NCORES))],
        ins=[d_agdi], outs=[d_agdo]), reads=[b_agdi], writes=[b_agdo], inc=1,
        group="g_cc")
    B.flush(block)
    B.barrier(snap2)
    st2.close()
    if PHASE_LIMIT <= 2:
        B.finalize(block)
        sem_stack.close()
        return nc

    st3 = contextlib.ExitStack()
    s_ftb, b_ftb = [], []
    for i in range(2):
        h_, bb = sb(f"ftb{i}", [128, D], BF16, st3)
        s_ftb.append(h_)
        b_ftb.append(bb)
    s_c01, b_sc01 = [], []
    s_m01f, b_sm01f = [], []
    for i in range(2):
        h_, bb = sb(f"c01_{i}", [128, 120], F32, st3)
        s_c01.append(h_)
        b_sc01.append(bb)
        h_, bb = sb(f"m01_{i}", [128, 120], F32, st3)
        s_m01f.append(h_)
        b_sm01f.append(bb)
    s_wm, b_swm = sb("wm", [128, 120], F32, st3)
    s_wt, b_swt = sb("wt", [128, 40], F32, st3)
    s_w01, b_sw01 = sb("w01", [128, 32 * 40], BF16, st3)
    p_fwt = [ps(f"pfwt{i}", st3) for i in range(8)]

    dma("gpsimd", s_enc[:, :].rearrange("p (a r) -> p a r", a=32),
        enc_in.rearrange("(a p) r -> p a r", p=128), reads=[b_enc], writes=[b_senc],
        group="g_rdg")
    for kt in range(32):
        i2 = kt % 2
        dma("sync", s_c01[i2][:, :], c01_in[kt * 128:(kt + 1) * 128, :],
            reads=[b_c01], writes=[b_sc01[i2]])
        dma("gpsimd", s_m01f[i2][:, :], m01_in[kt * 128:(kt + 1) * 128, :],
            reads=[b_m01], writes=[b_sm01f[i2]], group="g_rdg")
        B.op("vector", lambda q, i2=i2: q.tensor_mul(
            out=s_wm[:, :], in0=s_c01[i2][:, :], in1=s_m01f[i2][:, :]),
            reads=[b_sc01[i2], b_sm01f[i2]], writes=[b_swm])
        wmv = s_wm[:, :].rearrange("p (r e) -> p r e", r=40)
        B.op("vector", lambda q, wmv=wmv: q.tensor_add(
            out=s_wt[:, :].unsqueeze(2), in0=wmv[:, :, 0:1], in1=wmv[:, :, 1:2]),
            reads=[b_swm], writes=[b_swt])
        B.op("vector", lambda q, wmv=wmv, kt=kt: q.tensor_add(
            out=s_w01[:, kt * 40:(kt + 1) * 40].unsqueeze(2),
            in0=s_wt[:, :].unsqueeze(2), in1=wmv[:, :, 2:3]),
            reads=[b_swt, b_swm], writes=[b_sw01])
    for kt in range(32):
        fb, bfb = s_ftb[kt % 2], b_ftb[kt % 2]
        dma("sync", fb[:, :], ft_in[kt * 128:(kt + 1) * 128, :], reads=[b_ft], writes=[bfb],
            group="g_w%d" % (kt % 2))
        for ot in range(8):
            pf, bpf = p_fwt[ot]
            B.op("tensor", lambda q, pf=pf, fb=fb, kt=kt, ot=ot: q.matmul(
                pf[0:40, 0:512], s_w01[:, kt * 40:(kt + 1) * 40],
                fb[:, ot * 512:(ot + 1) * 512], start=(kt == 0), stop=(kt == 31)),
                reads=[b_sw01, bfb], writes=[bpf])
    for ot in range(8):
        pf, bpf = p_fwt[ot]
        B.op("scalar", lambda q, pf=pf, ot=ot: q.activation(
            out=s_fwt[:, ot * 512:(ot + 1) * 512], in_=pf[0:40, 0:512],
            func=AF.Copy, scale=SCALING), reads=[bpf], writes=[b_sfwt])
    B.flush(block)
    snap3 = B.snap()
    B.barrier(snap3)
    st3.close()
    if PHASE_LIMIT <= 3:
        B.finalize(block)
        sem_stack.close()
        return nc

    st4 = contextlib.ExitStack()
    s_prev, b_prev = sb("prev", [128, INTER], F32, st4)
    s_prevb, b_prevb = sb("prevb", [128, INTER], BF16, st4)
    s_big, b_big = sb("big", [128, INTER], BF16, st4)
    s_bigH, b_bigH = sb("bigH", [128, INTER], BF16, st4)
    s_BT4, b_sBT4 = sb("BT4", [128, G * NST], BF16, st4)
    s_sgc, b_ssgc = sb("sgc", [128, INTER], BF16, st4)
    s_xfc, b_sxfc = sb("xfc", [128, INTER], BF16, st4)
    s_scm, b_sscm = sb("scm", [128, G * CS], BF16, st4)
    s_cumq, s_cumi, s_E4, s_S4, s_ce4, s_ecb, s_gq, s_sbc = [], [], [], [], [], [], [], []
    b_scumq, b_scumi, b_sE4, b_sS4, b_sce4, b_secb, b_sgq, b_ssbc = [], [], [], [], [], [], [], []
    for i in range(2):
        t_, bb = sb(f"cumq{i}", [1, 4 * CS], F32, st4)
        s_cumq.append(t_); b_scumq.append(bb)
        t_, bb = sb(f"cumi{i}", [128, 512], F32, st4)
        s_cumi.append(t_); b_scumi.append(bb)
        t_, bb = sb(f"E4{i}", [128, 512], BF16, st4)
        s_E4.append(t_); b_sE4.append(bb)
        t_, bb = sb(f"S4{i}", [128, 512], BF16, st4)
        s_S4.append(t_); b_sS4.append(bb)
        t_, bb = sb(f"ce4{i}", [128, 512], BF16, st4)
        s_ce4.append(t_); b_sce4.append(bb)
        t_, bb = sb(f"ecb{i}", [128, 512], BF16, st4)
        s_ecb.append(t_); b_secb.append(bb)
        t_, bb = sb(f"gq{i}", [128, CS], BF16, st4)
        s_gq.append(t_); b_sgq.append(bb)
        t_, bb = sb(f"sbc{i}", [128, CS], F32, st4)
        s_sbc.append(t_); b_ssbc.append(bb)
    s_wz, b_swz = sb("wz", [128, 8], F32, st4)
    s_dtg, b_sdtg = sb("dtg", [8, 128], F32, st4)
    s_yt, b_syt = sb("yt", [128, 128], F32, st4)
    s_g0, b_sg0 = [], []
    for i in range(2):
        t_, bb = sb(f"g0_{i}", [128, 8 * CS], BF16, st4)
        s_g0.append(t_); b_sg0.append(bb)
    p_sc = [ps("psc0", st4), ps("psc1", st4)]
    p_bc = [ps("pbc0", st4), ps("pbc1", st4)]
    p_y = [ps("py0", st4), ps("py1", st4)]
    p_v = ps("pv", st4)
    p_sb2 = ps("psb", st4)
    var_ps = [p_v, p_sc[0]]
    sbc_ps = [p_sb2, p_sc[1]]

    dma("sync", s_dtg[:, :], d_agdo, reads=[b_agdo], writes=[b_sdtg])
    B.op("tensor", lambda q: q.matmul(p_bc[0][0][:, 0:8], s_dtg[:, :], s_pmt[:, :],
                                      start=True, stop=True),
         reads=[b_sdtg, b_spmt], writes=[p_bc[0][1]])
    B.op("scalar", lambda q: q.activation(out=s_wz[:, :], in_=p_bc[0][0][:, 0:8],
                                          func=AF.Exp), reads=[p_bc[0][1]], writes=[b_swz])
    B.op("vector", lambda q: q.tensor_mul(out=s_wz[:, :], in0=s_wz[:, :],
                                          in1=s_zmask[:, :]),
         reads=[b_swz, b_szmask], writes=[b_swz])
    B.op("vector", lambda q: q.memset(s_prev[:, :], 0.0), reads=[], writes=[b_prev])
    for j in range(NCORES - 1):
        dma("sync", s_bigH[:, :], d_agbo[j * NST:(j + 1) * NST, :],
            reads=[b_agbo], writes=[b_bigH])
        B.op("tensor", lambda q, j=j: q.matmul(
            p_bc[1][0][0:1, 0:128],
            s_wz[:, j:j + 1], s_ident[:, :], start=True, stop=True),
            reads=[b_swz, b_ident], writes=[p_bc[1][1]])
        B.op("vector", lambda q: q.tensor_copy(out=s_row[:, :], in_=p_bc[1][0][0:1, 0:128]),
             reads=[p_bc[1][1]], writes=[b_srow])
        B.op("tensor", lambda q: q.matmul(
            p_bc[0][0][:, 0:128], s_ones1[:, :], s_row[0:1, :], start=True, stop=True),
            reads=[b_ones1, b_srow], writes=[p_bc[0][1]])
        B.op("vector", lambda q: q.tensor_tensor(
            out=s_big[:, :].rearrange("p (h a) -> p h a", h=H),
            in0=s_bigH[:, :].rearrange("p (h a) -> p h a", h=H),
            in1=p_bc[0][0][:, 0:128].unsqueeze(2).broadcast_to([128, H, PH]),
            op=ALU.mult), reads=[b_bigH, p_bc[0][1]], writes=[b_big])
        B.op("vector", lambda q: q.tensor_tensor(
            out=s_prev[:, :], in0=s_prev[:, :], in1=s_big[:, :], op=ALU.add),
            reads=[b_prev, b_big], writes=[b_prev])
    B.op("vector", lambda q: q.tensor_copy(out=s_prevb[:, :], in_=s_prev[:, :]),
         reads=[b_prev], writes=[b_prevb])

    for c in range(NCH):
        c0, c1 = c * CS, (c + 1) * CS
        dma("sync", s_big[:, :], d_x[:, c0:c1], reads=[b_dx], writes=[b_big], transpose=True)
        B.op("vector", lambda q, c0=c0, c1=c1: q.tensor_tensor(
            out=s_big[:, :].rearrange("p (h a) -> p h a", h=H),
            in0=s_big[:, :].rearrange("p (h a) -> p h a", h=H),
            in1=s_dtpT[:, c0:c1].unsqueeze(2).broadcast_to([128, H, PH]),
            op=ALU.mult), reads=[b_big, b_sdtpT], writes=[b_big])
        B.op("vector", lambda q, c0=c0, c1=c1: q.tensor_tensor(
            out=s_bigH[:, :].rearrange("p (h a) -> p h a", h=H),
            in0=s_big[:, :].rearrange("p (h a) -> p h a", h=H),
            in1=s_dlastT[:, c0:c1].unsqueeze(2).broadcast_to([128, H, PH]),
            op=ALU.mult), reads=[b_big, b_sdlastT], writes=[b_bigH])
        dma("sync", s_BT4[:, :], d_b[:, c0:c1], reads=[b_db], writes=[b_sBT4], transpose=True)
        dma("sync", s_sgc[:, :].rearrange("p (a t) -> p a t", a=64),
            d_sg.rearrange("(a p) t -> p a t", p=128)[:, :, c0:c1],
            reads=[b_dsg], writes=[b_ssgc])
        dma("sync", s_xfc[:, :].rearrange("p (a t) -> p a t", a=64),
            d_x.rearrange("(a p) t -> p a t", p=128)[:, :, c0:c1],
            reads=[b_dx], writes=[b_sxfc])
        for g in range(G):
            pg, bpg = p_sc[g // 4]
            B.op("tensor", lambda q, pg=pg, g=g, c0=c0, c1=c1: q.matmul(
                pg[:, (g % 4) * CS:(g % 4 + 1) * CS], s_B[:, g * T + c0:g * T + c1],
                s_C[:, g * T + c0:g * T + c1], start=True, stop=True),
                reads=[b_sB, b_sC], writes=[bpg])
        for g in range(G):
            pg, bpg = p_sc[g // 4]
            B.op("vector", lambda q, pg=pg, g=g: q.tensor_tensor(
                out=s_scm[:, g * CS:(g + 1) * CS], in0=pg[:, (g % 4) * CS:(g % 4 + 1) * CS],
                in1=s_tril[:, :], op=ALU.mult), reads=[bpg, b_tril], writes=[b_sscm])
        def emit_decay(hb, c0, c1):
            bpar = hb % 2
            g = hb // 4
            h0 = hb * 4
            dma("sync", s_cumq[bpar][:, :],
                d_cum[c, h0:h0 + 4, :].rearrange("a t -> (a t)").unsqueeze(0),
                reads=[b_dcum], writes=[b_scumq[bpar]])
            pb, bpb = p_bc[bpar]
            B.op("tensor", lambda q, pb=pb, bpar=bpar: q.matmul(
                pb[:, 0:512], s_ones1[:, :],
                s_cumq[bpar][0:1, 0:512], start=True, stop=True),
                reads=[b_ones1, b_scumq[bpar]], writes=[bpb])
            B.op("scalar", lambda q, pb=pb, bpar=bpar: q.activation(
                out=s_ecb[bpar][:, :], in_=pb[:, 0:512], func=AF.Exp),
                reads=[bpb], writes=[b_secb[bpar]])
            for hh in range(4):
                B.op("scalar", lambda q, pb=pb, c0=c0, h0=h0, hh=hh, bpar=bpar:
                     q.activation(
                         out=s_cumi[bpar][:, hh * CS:(hh + 1) * CS],
                         in_=pb[:, hh * CS:(hh + 1) * CS], func=AF.Relu,
                         scale=-1.0,
                         bias=s_cumT[:, c0 + h0 + hh:c0 + h0 + hh + 1]),
                     reads=[bpb, b_scumT], writes=[b_scumi[bpar]])
            B.op("scalar", lambda q, bpar=bpar: q.activation(
                out=s_E4[bpar][:, :], in_=s_cumi[bpar][:, :], func=AF.Exp,
                scale=-1.0),
                reads=[b_scumi[bpar]], writes=[b_sE4[bpar]])
            B.op("vector", lambda q, g=g, bpar=bpar: q.tensor_tensor(
                out=s_S4[bpar][:, :].rearrange("p (a t) -> p a t", a=4),
                in0=s_E4[bpar][:, :].rearrange("p (a t) -> p a t", a=4),
                in1=s_scm[:, g * CS:(g + 1) * CS].unsqueeze(1).broadcast_to([128, 4, CS]),
                op=ALU.mult), reads=[b_sE4[bpar], b_sscm], writes=[b_sS4[bpar]])
            B.op("vector", lambda q, g=g, c0=c0, c1=c1, bpar=bpar: q.tensor_tensor(
                out=s_ce4[bpar][:, :].rearrange("p (a t) -> p a t", a=4),
                in0=s_C[:, g * T + c0:g * T + c1].unsqueeze(1).broadcast_to([128, 4, CS]),
                in1=s_ecb[bpar][:, :].rearrange("p (a t) -> p a t", a=4),
                op=ALU.mult), reads=[b_sC, b_secb[bpar]], writes=[b_sce4[bpar]])

        def emit_y(hb):
            bpar = hb % 2
            h0 = hb * 4
            for hh in range(4):
                h = h0 + hh
                pr = h // 2
                py, bpy = p_y[pr % 2]
                po = (h % 2) * PH
                B.op("tensor", lambda q, py=py, po=po, h=h, hh=hh, bpar=bpar: q.matmul(
                    py[po:po + PH, 0:CS], s_big[:, h * PH:(h + 1) * PH],
                    s_S4[bpar][:, hh * CS:(hh + 1) * CS], start=True, stop=False),
                    reads=[b_big, b_sS4[bpar]], writes=[bpy])
                B.op("tensor", lambda q, py=py, po=po, h=h, hh=hh, bpar=bpar: q.matmul(
                    py[po:po + PH, 0:CS], s_prevb[:, h * PH:(h + 1) * PH],
                    s_ce4[bpar][:, hh * CS:(hh + 1) * CS], start=False, stop=True),
                    reads=[b_prevb, b_sce4[bpar]], writes=[bpy])

        def emit_gating(hb):
            h0 = hb * 4
            for pri in range(2):
                pr = h0 // 2 + pri
                gg = pr // 8
                gp = gg % 2
                gqp = pr % 2
                py2, bpy2 = p_y[pr % 2]
                B.op("vector", lambda q, py2=py2, pr=pr: q.scalar_tensor_tensor(
                    out=s_yt[:, :], in0=s_xfc[:, pr * CS:(pr + 1) * CS],
                    scalar=s_dexp[:, pr:pr + 1], in1=py2[:, 0:CS],
                    op0=ALU.mult, op1=ALU.add),
                    reads=[b_sxfc, b_sdexp, bpy2], writes=[b_syt])
                B.op("vector", lambda q, pr=pr, gp=gp: q.tensor_tensor(
                    out=s_g0[gp][:, (pr % 8) * CS:(pr % 8 + 1) * CS], in0=s_yt[:, :],
                    in1=s_sgc[:, pr * CS:(pr + 1) * CS], op=ALU.mult),
                    reads=[b_syt, b_ssgc], writes=[b_sg0[gp]])
                B.op("scalar", lambda q, pr=pr, gp=gp, gqp=gqp: q.activation(
                    out=s_gq[gqp][:, :], in_=s_g0[gp][:, (pr % 8) * CS:(pr % 8 + 1) * CS],
                    func=AF.Square), reads=[b_sg0[gp]], writes=[b_sgq[gqp]])
                pv, bpv = var_ps[gg % 2]
                B.op("tensor", lambda q, pr=pr, pv=pv, gqp=gqp: q.matmul(
                    pv[0:1, 0:CS], s_invgs[:, :], s_gq[gqp][:, :],
                    start=(pr % 8 == 0), stop=(pr % 8 == 7)),
                    reads=[b_sgq[gqp], b_sinvgs], writes=[bpv])
                if pr % 8 == 7:
                    psb, bpsb = sbc_ps[gg % 2]
                    B.op("scalar", lambda q, pv=pv: q.activation(
                        out=s_row[:, :], in_=pv[0:1, 0:CS], func=AF.Ln,
                        bias=s_epsc[0:1, :]), reads=[bpv, b_sepsc], writes=[b_srow])
                    B.op("scalar", lambda q: q.activation(
                        out=s_row[:, :], in_=s_row[:, :], func=AF.Exp, scale=-0.5),
                        reads=[b_srow], writes=[b_srow])
                    B.op("tensor", lambda q, psb=psb: q.matmul(
                        psb[:, 0:CS], s_ones1[:, :], s_row[0:1, :],
                        start=True, stop=True),
                        reads=[b_ones1, b_srow], writes=[bpsb])
                    B.op("scalar", lambda q, psb=psb, gp=gp: q.activation(
                        out=s_sbc[gp][:, :], in_=psb[:, 0:CS], func=AF.Copy),
                        reads=[bpsb], writes=[b_ssbc[gp]])
                    for p8 in range(8):
                        prr = gg * 8 + p8
                        B.op("vector", lambda q, p8=p8, prr=prr, gp=gp: q.scalar_tensor_tensor(
                            out=s_sgc[:, prr * CS:(prr + 1) * CS],
                            in0=s_g0[gp][:, p8 * CS:(p8 + 1) * CS],
                            scalar=s_nwc[:, prr:prr + 1], in1=s_sbc[gp][:, :],
                            op0=ALU.mult, op1=ALU.mult),
                            reads=[b_sg0[gp], b_snwc, b_ssbc[gp]], writes=[b_ssgc])

        for idx in range(32 + 2):
            if idx < 32:
                emit_decay(idx, c0, c1)
            if 0 <= idx - 2 < 32:
                emit_gating(idx - 2)
            if 0 <= idx - 1 < 32:
                emit_y(idx - 1)
        dma("scalar", d_g.rearrange("(a p) t -> p a t", p=128)[:, :, c0:c1],
            s_sgc[:, :].rearrange("p (a t) -> p a t", a=64),
            reads=[b_ssgc], writes=[b_dg], group="g_sp")
        B.op("tensor", lambda q, c=c: q.matmul(
            p_bc[0][0][:, 0:128],
            s_ones1[:, :], s_cdrow[0:1, c * H:(c + 1) * H], start=True, stop=True),
            reads=[b_ones1, b_scdrow], writes=[p_bc[0][1]])
        B.op("vector", lambda q: q.tensor_tensor(
            out=s_prev[:, :].rearrange("p (h a) -> p h a", h=H),
            in0=s_prev[:, :].rearrange("p (h a) -> p h a", h=H),
            in1=p_bc[0][0][:, 0:128].unsqueeze(2).broadcast_to([128, H, PH]),
            op=ALU.mult), reads=[b_prev, p_bc[0][1]], writes=[b_prev])
        for hb in range(16):
            pst, bst = p_sc[hb % 2]
            for hh in range(8):
                h = hb * 8 + hh
                g = h // 16
                B.op("tensor", lambda q, pst=pst, g=g, h=h, hh=hh: q.matmul(
                    pst[:, hh * PH:(hh + 1) * PH], s_BT4[:, g * NST:(g + 1) * NST],
                    s_bigH[:, h * PH:(h + 1) * PH], start=True, stop=True),
                    reads=[b_sBT4, b_bigH], writes=[bst])
            B.op("vector", lambda q, pst=pst, hb=hb: q.tensor_tensor(
                out=s_prev[:, hb * 512:(hb + 1) * 512],
                in0=s_prev[:, hb * 512:(hb + 1) * 512],
                in1=pst[:, :], op=ALU.add), reads=[bst, b_prev], writes=[b_prev])
        if c < NCH - 1:
            B.op("vector", lambda q: q.tensor_copy(out=s_prevb[:, :], in_=s_prev[:, :]),
                 reads=[b_prev], writes=[b_prevb])
    B.flush(block)
    snap4 = B.snap()
    B.barrier(snap4)
    st4.close()
    st14.close()
    if PHASE_LIMIT <= 4:
        B.finalize(block)
        sem_stack.close()
        return nc

    st5 = contextlib.ExitStack()
    s_gall, b_gall = sb("gall", [128, 64 * T], BF16, st5)
    s_owb, b_owb = [], []
    for i in range(2):
        h_, bb = sb(f"owb{i}", [128, 64 * 128], BF16, st5)
        s_owb.append(h_)
        b_owb.append(bb)
    s_aev, b_aev = [], []
    for i in range(2):
        h_, bb = sb(f"aev{i}", [128, 512], F32, st5)
        s_aev.append(h_)
        b_aev.append(bb)
    p_op = [ps(f"pop{i}", st5) for i in range(3)]

    dma("sync", s_gall[:, :].rearrange("p (a t) -> p a t", a=64),
        d_g.rearrange("(a p) t -> p a t", p=128), reads=[b_dg], writes=[b_gall])
    for oc in range(32):
        owb, bowb = s_owb[oc % 2], b_owb[oc % 2]
        dma("gpsimd", owb[:, :].rearrange("p (a b) -> p a b", a=64),
            w_out[oc].rearrange("a p b -> p a b"),
            reads=[b_w_out], writes=[bowb], group="g_sw%d" % (oc % 2))
        for tt in range(2):
            pp, bpp = p_op[(oc * 2 + tt) % 3]
            for kt in range(64):
                B.op("tensor", lambda q, pp=pp, owb=owb, kt=kt, tt=tt: q.matmul(
                    pp[:, 0:512], owb[:, kt * 128:(kt + 1) * 128],
                    s_gall[:, kt * T + tt * 512:kt * T + (tt + 1) * 512],
                    start=(kt == 0), stop=(kt == 63)),
                    reads=[bowb, b_gall], writes=[bpp])
            ae, bae = s_aev[tt], b_aev[tt]
            B.op("vector", lambda q, pp=pp, ae=ae: q.tensor_copy(
                out=ae[:, :], in_=pp[:, 0:512]), reads=[bpp], writes=[bae])
            dma("scalar", d_attn[oc * 128:(oc + 1) * 128, tt * 512:(tt + 1) * 512],
                ae[:, :], reads=[bae], writes=[b_dattn], group="g_sp")
    B.flush(block)
    snap5 = B.snap()
    B.barrier(snap5)
    st5.close()
    if PHASE_LIMIT <= 5:
        B.finalize(block)
        sem_stack.close()
        return nc

    stHB = contextlib.ExitStack()
    s_hbf, b_shbf = sb("hbf", [128, 32 * T], BF16, stHB)
    st6 = contextlib.ExitStack()
    s_xtm, b_sxtm = sb("xtm", [128, D], F32, st6)
    s_arb, b_sarb, s_zb, b_szb, s_zq, b_szq, s_hfm, b_shfm = [], [], [], [], [], [], [], []
    s_mrow, b_smrow, s_qrow, b_sqrow, s_vrow, b_svrow = [], [], [], [], [], []
    for i in range(2):
        t_, bb = sb(f"arb{i}", [128, D], F32, st6)
        s_arb.append(t_); b_sarb.append(bb)
        t_, bb = sb(f"zbf{i}", [128, D], BF16, st6)
        s_zb.append(t_); b_szb.append(bb)
        t_, bb = sb(f"zsq{i}", [128, D], BF16, st6)
        s_zq.append(t_); b_szq.append(bb)
        t_, bb = sb(f"hfm{i}", [128, D], F32, st6)
        s_hfm.append(t_); b_shfm.append(bb)
        t_, bb = sb(f"mrow{i}", [1, 128], F32, st6)
        s_mrow.append(t_); b_smrow.append(bb)
        t_, bb = sb(f"qrow{i}", [1, 128], F32, st6)
        s_qrow.append(t_); b_sqrow.append(bb)
        t_, bb = sb(f"vrow{i}", [1, 128], F32, st6)
        s_vrow.append(t_); b_svrow.append(bb)
    p_tx = [ps("ptx0", st6), ps("ptx1", st6)]
    p_m = ps("pm", st6)
    p_q = ps("pq", st6)
    p_s1 = [ps("ps1a", st6), ps("ps1b", st6)]
    p_s2 = [ps("ps2a", st6), ps("ps2b", st6)]

    def ln_rows(i, pm, pq, ps1, ps2):
        B.op("vector", lambda q: q.tensor_copy(out=s_mrow[i][:, :], in_=pm[0][0:1, 0:128]),
             reads=[pm[1]], writes=[b_smrow[i]])
        B.op("vector", lambda q: q.tensor_copy(out=s_qrow[i][:, :], in_=pq[0][0:1, 0:128]),
             reads=[pq[1]], writes=[b_sqrow[i]])
        B.op("vector", lambda q: q.tensor_mul(out=s_vrow[i][:, :], in0=s_mrow[i][:, :],
                                              in1=s_mrow[i][:, :]),
             reads=[b_smrow[i]], writes=[b_svrow[i]])
        B.op("vector", lambda q: q.tensor_sub(out=s_vrow[i][:, :], in0=s_qrow[i][:, :],
                                              in1=s_vrow[i][:, :]),
             reads=[b_sqrow[i], b_svrow[i]], writes=[b_svrow[i]])
        B.op("scalar", lambda q: q.activation(out=s_vrow[i][:, :], in_=s_vrow[i][:, :],
                                              func=AF.Ln, bias=s_epsc[0:1, :]),
             reads=[b_svrow[i], b_sepsc], writes=[b_svrow[i]])
        B.op("scalar", lambda q: q.activation(out=s_vrow[i][:, :], in_=s_vrow[i][:, :],
                                              func=AF.Exp, scale=-0.5),
             reads=[b_svrow[i]], writes=[b_svrow[i]])
        B.op("vector", lambda q: q.tensor_mul(out=s_mrow[i][:, :], in0=s_mrow[i][:, :],
                                              in1=s_vrow[i][:, :]),
             reads=[b_smrow[i], b_svrow[i]], writes=[b_smrow[i]])
        B.op("tensor", lambda q: q.matmul(ps1[0][:, 0:128], s_ones1[:, :],
                                          s_vrow[i][0:1, :], start=True, stop=True),
             reads=[b_ones1, b_svrow[i]], writes=[ps1[1]])
        B.op("tensor", lambda q: q.matmul(ps2[0][:, 0:128], s_ones1[:, :],
                                          s_mrow[i][0:1, :], start=True, stop=True),
             reads=[b_ones1, b_smrow[i]], writes=[ps2[1]])

    for tt in range(8):
        i = tt % 2
        dma("sync", s_xtm[:, :], x_in[HALO + tt * 128:HALO + (tt + 1) * 128, :],
            reads=[b_x_in], writes=[b_sxtm])
        dma("sync", s_arb[i][:, :].rearrange("p (a t) -> p a t", a=32),
            d_attn.rearrange("(a p) t -> p a t", p=128)[:, :, tt * 128:(tt + 1) * 128],
            reads=[b_dattn], writes=[b_sarb[i]])
        for dc in range(32):
            ptx, bptx = p_tx[dc % 2]
            B.op("tensor", lambda q, ptx=ptx, dc=dc: q.transpose(
                ptx[:, 0:128], s_xtm[:, dc * 128:(dc + 1) * 128], s_ident[:, :]),
                reads=[b_sxtm, b_ident], writes=[bptx])
            B.op("vector", lambda q, ptx=ptx, dc=dc, i=i: q.tensor_tensor(
                out=s_arb[i][:, dc * 128:(dc + 1) * 128], in0=ptx[:, 0:128],
                in1=s_arb[i][:, dc * 128:(dc + 1) * 128], op=ALU.add),
                reads=[bptx, b_sarb[i]], writes=[b_sarb[i]])
        B.op("scalar", lambda q, i=i: q.activation(out=s_zb[i][:, :], in_=s_arb[i][:, :],
                                                   func=AF.Copy),
             reads=[b_sarb[i]], writes=[b_szb[i]])
        B.op("scalar", lambda q, i=i: q.activation(out=s_zq[i][:, :], in_=s_arb[i][:, :],
                                                   func=AF.Square),
             reads=[b_sarb[i]], writes=[b_szq[i]])
        for dc in range(32):
            B.op("tensor", lambda q, dc=dc, i=i: q.matmul(
                p_m[0][0:1, 0:128], s_invd[:, :], s_zb[i][:, dc * 128:(dc + 1) * 128],
                start=(dc == 0), stop=(dc == 31)), reads=[b_sinvd, b_szb[i]], writes=[p_m[1]])
        for dc in range(32):
            B.op("tensor", lambda q, dc=dc, i=i: q.matmul(
                p_q[0][0:1, 0:128], s_invd[:, :], s_zq[i][:, dc * 128:(dc + 1) * 128],
                start=(dc == 0), stop=(dc == 31)), reads=[b_sinvd, b_szq[i]], writes=[p_q[1]])
        ln_rows(i, p_m, p_q, p_s1[i], p_s2[i])
        for dc in range(32):
            B.op("vector", lambda q, dc=dc, i=i: q.tensor_tensor(
                out=s_hfm[i][:, dc * 128:(dc + 1) * 128],
                in0=s_arb[i][:, dc * 128:(dc + 1) * 128],
                in1=p_s1[i][0][:, 0:128], op=ALU.mult),
                reads=[b_sarb[i], p_s1[i][1]], writes=[b_shfm[i]])
        B.op("vector", lambda q, i=i: q.tensor_tensor(
            out=s_hfm[i][:, :].rearrange("p (a t) -> p a t", a=32),
            in0=s_hfm[i][:, :].rearrange("p (a t) -> p a t", a=32),
            in1=p_s2[i][0][:, 0:128].unsqueeze(1).broadcast_to([128, 32, 128]),
            op=ALU.subtract), reads=[b_shfm[i], p_s2[i][1]], writes=[b_shfm[i]])
        dma("scalar", d_h.rearrange("(a p) t -> p a t", p=128)[:, :, tt * 128:(tt + 1) * 128],
            s_hfm[i][:, :].rearrange("p (a t) -> p a t", a=32),
            reads=[b_shfm[i]], writes=[b_dh], group="g_sp")
        B.op("scalar", lambda q, tt=tt, i=i: q.activation(
            out=s_hbf[:, :].rearrange("p (a t) -> p a t", a=32)[:, :, tt * 128:(tt + 1) * 128],
            in_=s_hfm[i][:, :].rearrange("p (a t) -> p a t", a=32), func=AF.Copy),
            reads=[b_shfm[i]], writes=[b_shbf])
    B.flush(block)
    snap6 = B.snap()
    B.barrier(snap6)
    st6.close()
    if PHASE_LIMIT <= 6:
        B.finalize(block)
        sem_stack.close()
        return nc

    st7 = contextlib.ExitStack()
    s_wbb, b_wbb = [], []
    for i in range(2):
        h_, bb = sb(f"wbb{i}", [128, 32 * 128], BF16, st7)
        s_wbb.append(h_)
        b_wbb.append(bb)
    s_hrb, b_shrb = [], []
    for i in range(2):
        h_, bb = sb(f"hrb{i}", [128, 512], F32, st7)
        s_hrb.append(h_)
        b_shrb.append(bb)
    s_z2e, b_sz2e = [], []
    for i in range(2):
        h_, bb = sb(f"z2e{i}", [128, 512], F32, st7)
        s_z2e.append(h_)
        b_sz2e.append(bb)
    p_ff = [ps(f"pff{i}", st7) for i in range(3)]
    p_e = ps("pe", st7)

    for tt in range(2):
        for kt in range(32):
            B.op("tensor", lambda q, kt=kt, tt=tt: q.matmul(
                p_e[0][0:40, 0:512], s_enc[:, kt * 40:(kt + 1) * 40],
                s_hbf[:, kt * T + tt * 512:kt * T + (tt + 1) * 512],
                start=(kt == 0), stop=(kt == 31)),
                reads=[b_senc, b_shbf], writes=[p_e[1]])
        B.op("vector", lambda q, tt=tt: q.tensor_copy(
            out=s_e01[0:40, tt * 512:(tt + 1) * 512], in_=p_e[0][0:40, 0:512]),
            reads=[p_e[1]], writes=[b_se01])
    for oc in range(32):
        wbb, bwbb = s_wbb[oc % 2], b_wbb[oc % 2]
        dma("gpsimd", wbb[:, :].rearrange("p (a b) -> p a b", a=32),
            w_baseT[oc].rearrange("a p b -> p a b"),
            reads=[b_w_base], writes=[bwbb], group="g_sw%d" % (oc % 2))
        for tt in range(2):
            pp, bpp = p_ff[(oc * 2 + tt) % 3]
            for kt in range(32):
                B.op("tensor", lambda q, pp=pp, wbb=wbb, kt=kt, tt=tt: q.matmul(
                    pp[:, 0:512], wbb[:, kt * 128:(kt + 1) * 128],
                    s_hbf[:, kt * T + tt * 512:kt * T + (tt + 1) * 512],
                    start=(kt == 0), stop=False),
                    reads=[bwbb, b_shbf], writes=[bpp])
            B.op("tensor", lambda q, pp=pp, oc=oc, tt=tt: q.matmul(
                pp[:, 0:512], s_fwt[:, oc * 128:(oc + 1) * 128],
                s_e01[0:40, tt * 512:(tt + 1) * 512], start=False, stop=True),
                reads=[b_sfwt, b_se01], writes=[bpp])
            hr, bhr = s_hrb[tt], b_shrb[tt]
            dma("sync", hr[:, :], d_h[oc * 128:(oc + 1) * 128, tt * 512:(tt + 1) * 512],
                reads=[b_dh], writes=[bhr])
            ze, bze = s_z2e[tt], b_sz2e[tt]
            B.op("vector", lambda q, pp=pp, hr=hr, ze=ze, oc=oc: q.scalar_tensor_tensor(
                out=ze[:, :], in0=pp[:, 0:512], scalar=s_bbc[:, oc:oc + 1], in1=hr[:, :],
                op0=ALU.add, op1=ALU.add), reads=[bpp, b_sbbc, bhr], writes=[bze])
            dma("scalar", d_z2[oc * 128:(oc + 1) * 128, tt * 512:(tt + 1) * 512],
                ze[:, :], reads=[bze], writes=[b_dz2], group="g_sp")
    B.flush(block)
    snap7 = B.snap()
    B.barrier(snap7)
    st7.close()
    stHB.close()
    if PHASE_LIMIT <= 7:
        B.finalize(block)
        sem_stack.close()
        return nc

    st8 = contextlib.ExitStack()
    s_z2, b_sz2, s_zb2, b_szb2, s_zq2, b_szq2, s_otm, b_sotm = [], [], [], [], [], [], [], []
    s_mr2, b_smr2, s_qr2, b_sqr2, s_vr2, b_svr2 = [], [], [], [], [], []
    for i in range(2):
        t_, bb = sb(f"z2fm{i}", [128, D], F32, st8)
        s_z2.append(t_); b_sz2.append(bb)
        t_, bb = sb(f"zbf2{i}", [128, D], BF16, st8)
        s_zb2.append(t_); b_szb2.append(bb)
        t_, bb = sb(f"zsq2{i}", [128, D], BF16, st8)
        s_zq2.append(t_); b_szq2.append(bb)
        t_, bb = sb(f"otm{i}", [128, D], F32, st8)
        s_otm.append(t_); b_sotm.append(bb)
        t_, bb = sb(f"mr2{i}", [1, 128], F32, st8)
        s_mr2.append(t_); b_smr2.append(bb)
        t_, bb = sb(f"qr2{i}", [1, 128], F32, st8)
        s_qr2.append(t_); b_sqr2.append(bb)
        t_, bb = sb(f"vr2{i}", [1, 128], F32, st8)
        s_vr2.append(t_); b_svr2.append(bb)
    p_tx2 = [ps("ptx20", st8), ps("ptx21", st8)]
    p_m2 = ps("pm2", st8)
    p_q2 = ps("pq2", st8)
    p_s12 = [ps("ps12a", st8), ps("ps12b", st8)]
    p_s22 = [ps("ps22a", st8), ps("ps22b", st8)]

    for tt in range(8):
        i = tt % 2
        dma("sync", s_z2[i][:, :].rearrange("p (a t) -> p a t", a=32),
            d_z2.rearrange("(a p) t -> p a t", p=128)[:, :, tt * 128:(tt + 1) * 128],
            reads=[b_dz2], writes=[b_sz2[i]])
        B.op("scalar", lambda q, i=i: q.activation(out=s_zb2[i][:, :], in_=s_z2[i][:, :],
                                                   func=AF.Copy),
             reads=[b_sz2[i]], writes=[b_szb2[i]])
        B.op("scalar", lambda q, i=i: q.activation(out=s_zq2[i][:, :], in_=s_z2[i][:, :],
                                                   func=AF.Square),
             reads=[b_sz2[i]], writes=[b_szq2[i]])
        for dc in range(32):
            B.op("tensor", lambda q, dc=dc, i=i: q.matmul(
                p_m2[0][0:1, 0:128], s_invd[:, :], s_zb2[i][:, dc * 128:(dc + 1) * 128],
                start=(dc == 0), stop=(dc == 31)), reads=[b_sinvd, b_szb2[i]],
                writes=[p_m2[1]])
        for dc in range(32):
            B.op("tensor", lambda q, dc=dc, i=i: q.matmul(
                p_q2[0][0:1, 0:128], s_invd[:, :], s_zq2[i][:, dc * 128:(dc + 1) * 128],
                start=(dc == 0), stop=(dc == 31)), reads=[b_sinvd, b_szq2[i]],
                writes=[p_q2[1]])
        B.op("vector", lambda q, i=i: q.tensor_copy(out=s_mr2[i][:, :], in_=p_m2[0][0:1, 0:128]),
             reads=[p_m2[1]], writes=[b_smr2[i]])
        B.op("vector", lambda q, i=i: q.tensor_copy(out=s_qr2[i][:, :], in_=p_q2[0][0:1, 0:128]),
             reads=[p_q2[1]], writes=[b_sqr2[i]])
        B.op("vector", lambda q, i=i: q.tensor_mul(out=s_vr2[i][:, :], in0=s_mr2[i][:, :],
                                                   in1=s_mr2[i][:, :]),
             reads=[b_smr2[i]], writes=[b_svr2[i]])
        B.op("vector", lambda q, i=i: q.tensor_sub(out=s_vr2[i][:, :], in0=s_qr2[i][:, :],
                                                   in1=s_vr2[i][:, :]),
             reads=[b_sqr2[i], b_svr2[i]], writes=[b_svr2[i]])
        B.op("scalar", lambda q, i=i: q.activation(out=s_vr2[i][:, :], in_=s_vr2[i][:, :],
                                                   func=AF.Ln, bias=s_epsc[0:1, :]),
             reads=[b_svr2[i], b_sepsc], writes=[b_svr2[i]])
        B.op("scalar", lambda q, i=i: q.activation(out=s_vr2[i][:, :], in_=s_vr2[i][:, :],
                                                   func=AF.Exp, scale=-0.5),
             reads=[b_svr2[i]], writes=[b_svr2[i]])
        B.op("vector", lambda q, i=i: q.tensor_mul(out=s_mr2[i][:, :], in0=s_mr2[i][:, :],
                                                   in1=s_vr2[i][:, :]),
             reads=[b_smr2[i], b_svr2[i]], writes=[b_smr2[i]])
        B.op("tensor", lambda q, i=i: q.matmul(p_s12[i][0][:, 0:128], s_ones1[:, :],
                                               s_vr2[i][0:1, :], start=True, stop=True),
             reads=[b_ones1, b_svr2[i]], writes=[p_s12[i][1]])
        B.op("tensor", lambda q, i=i: q.matmul(p_s22[i][0][:, 0:128], s_ones1[:, :],
                                               s_mr2[i][0:1, :], start=True, stop=True),
             reads=[b_ones1, b_smr2[i]], writes=[p_s22[i][1]])
        for dc in range(32):
            B.op("vector", lambda q, dc=dc, i=i: q.tensor_tensor(
                out=s_z2[i][:, dc * 128:(dc + 1) * 128],
                in0=s_z2[i][:, dc * 128:(dc + 1) * 128],
                in1=p_s12[i][0][:, 0:128], op=ALU.mult),
                reads=[b_sz2[i], p_s12[i][1]], writes=[b_sz2[i]])
        B.op("vector", lambda q, i=i: q.tensor_tensor(
            out=s_z2[i][:, :].rearrange("p (a t) -> p a t", a=32),
            in0=s_z2[i][:, :].rearrange("p (a t) -> p a t", a=32),
            in1=p_s22[i][0][:, 0:128].unsqueeze(1).broadcast_to([128, 32, 128]),
            op=ALU.subtract), reads=[b_sz2[i], p_s22[i][1]], writes=[b_sz2[i]])
        for dc in range(32):
            ptx, bptx = p_tx2[dc % 2]
            B.op("tensor", lambda q, ptx=ptx, dc=dc, i=i: q.transpose(
                ptx[:, 0:128], s_z2[i][:, dc * 128:(dc + 1) * 128], s_ident[:, :]),
                reads=[b_sz2[i], b_ident], writes=[bptx])
            B.op("vector", lambda q, ptx=ptx, dc=dc, i=i: q.tensor_copy(
                out=s_otm[i][:, dc * 128:(dc + 1) * 128], in_=ptx[:, 0:128]),
                reads=[bptx], writes=[b_sotm[i]])
        dma("sync", out_ext[tt * 128:(tt + 1) * 128, :], s_otm[i][:, :],
            reads=[b_sotm[i]], writes=[b_out], group="g_sp")
    if KDEBUG:
        for nm, srcap, bsrc in [("dbg_x", d_x, b_dx), ("dbg_sg", d_sg, b_dsg),
                                ("dbg_b", d_b, b_db), ("dbg_attn", d_attn, b_dattn),
                                ("dbg_h", d_h, b_dh), ("dbg_cum", d_cum, b_dcum),
                                ("dbg_g", d_g, b_dg), ("dbg_z2", d_z2, b_dz2),
                                ("dbg_agb", d_agbo, b_agbo), ("dbg_agd", d_agdo, b_agdo)]:
            dma("sync", dbg[nm][0], srcap, reads=[bsrc], writes=[dbg[nm][1]], group="g_sp")
    B.op("gpsimd", lambda q: q.nop(), reads=[b_out], writes=[])
    B.flush(block)
    st8.close()

    B.finalize(block)
    sem_stack.close()
    stack_all.close()
    return nc


_NC_CACHE = None


def kernel(**inputs):
    global _NC_CACHE
    sh, per_core = _host_prep(inputs)
    if _NC_CACHE is None:
        _NC_CACHE = build_program()
    nc = _NC_CACHE
    in_maps = []
    for k in range(NCORES):
        m = dict(sh)
        m.update(per_core[k])
        in_maps.append(m)
    res = run_bass_kernel_spmd(nc, in_maps, core_ids=list(range(NCORES)))
    out = np.concatenate([np.asarray(res.results[k]["out"]) for k in range(NCORES)], axis=0)
    return out.reshape(1, L, D).astype(np.float32)


# version 11
# speedup vs baseline: 1.0616x; 1.0238x over previous
import contextlib
import math
import os
import numpy as np
import ml_dtypes
import concourse.bass as bass
import concourse.mybir as mybir
from concourse.bass_utils import run_bass_kernel_spmd

F32 = mybir.dt.float32
BF16 = mybir.dt.bfloat16
U8 = mybir.dt.uint8
AF = mybir.ActivationFunctionType
ALU = mybir.AluOpType
AX = mybir.AxisListType

NCORES = 8
D = 4096
L = 8192
T = 1024
HALO = 3
TH = T + HALO
THP = 1040
CS = 128
NCH = T // CS
H = 128
PH = 64
NST = 128
G = 8
INTER = H * PH
CONV = INTER + 2 * G * NST
PROJ = INTER + CONV + H
GS = INTER // G
EPS = 1e-5
SCALING = 0.5

NOC = PROJ // 128
OC_GATE0, OC_X0, OC_B0, OC_C0, OC_DT = 0, 64, 128, 136, 144
SPANS_XBC = [(0, 512), (512, 1024), (1024, 1027)]
SPANS_GD = [(3, 515), (515, 1027)]

ENGS = ("sync", "scalar", "vector", "tensor", "gpsimd")
GROUPS = ("g_const", "g_w0", "g_w1", "g_sp", "g_rd", "g_cc",
          "g_xc", "g_sw0", "g_sw1", "g_spg", "g_rdg")
SELF_SYNC = {"sync": False, "scalar": False, "vector": False, "tensor": False,
             "gpsimd": False}


class Buf:
    __slots__ = ("name", "ready", "readers")

    def __init__(self, name):
        self.name = name
        self.ready = None
        self.readers = []


class EngRec:
    __slots__ = ("name", "cum", "ops", "last_wait", "sem")

    def __init__(self, name):
        self.name = name
        self.cum = 0
        self.ops = []
        self.last_wait = {}
        self.sem = None


class Builder:
    def __init__(self, nc):
        self.nc = nc
        self.engs = {n: EngRec(n) for n in ENGS}
        self.gcum = {g: 0 for g in GROUPS}
        self.gsem = {}

    def buf(self, name):
        return Buf(name)

    def op(self, eng, closure, reads=(), writes=(), inc=1, group=None):
        e = self.engs[eng]
        waits = {}

        def need(tok):
            if tok is None:
                return
            s, v = tok
            if s in self.gcum:
                v = self.gcum[s]
                if v == 0:
                    return
            elif s == eng and not SELF_SYNC[eng]:
                return
            if e.last_wait.get(s, 0) >= v:
                return
            if waits.get(s, 0) < v:
                waits[s] = v

        for b in reads:
            need(b.ready)
        for b in writes:
            need(b.ready)
            for t in b.readers:
                need(t)
        for s, v in waits.items():
            e.last_wait[s] = v
        if group is not None:
            self.gcum[group] += inc
            tok = (group, self.gcum[group])
            e.ops.append((waits, closure, (group, inc)))
            e.cum += 0
        else:
            tok = (eng, e.cum + inc)
            e.ops.append((waits, closure, inc))
            e.cum += inc
        for b in reads:
            b.readers.append(tok)
        for b in writes:
            b.ready = tok
            b.readers = []
        return tok

    def snap(self):
        d = {n: self.engs[n].cum for n in ENGS}
        d.update({g: self.gcum[g] for g in GROUPS})
        return d

    def barrier(self, snap=None):
        if snap is None:
            snap = self.snap()
        for n in ENGS:
            e = self.engs[n]
            waits = {}
            for m, v in snap.items():
                if m == n or v == 0:
                    continue
                if e.last_wait.get(m, 0) >= v:
                    continue
                waits[m] = v
                e.last_wait[m] = v
            e.ops.append((waits, lambda q: q.nop(), 1))
            e.cum += 1

    def flush(self, block):
        return

    def finalize(self, block):
        waited = {n: set() for n in ENGS}
        for n in ENGS:
            for w, _, _ in self.engs[n].ops:
                for s, v in w.items():
                    if s in waited:
                        waited[s].add(v)
        CHUNK = 3000
        for n in ENGS:
            e = self.engs[n]
            if not e.ops:
                continue
            ops = e.ops
            e.ops = []
            sems = {m: self.engs[m].sem for m in ENGS}
            sems.update(self.gsem)
            mysem = e.sem
            wset = waited[n]
            nops = len(ops)
            state = {"pending": 0, "cum": 0}
            for lo in range(0, nops, CHUNK):
                hi = min(lo + CHUNK, nops)

                def body(q, lo=lo, hi=hi, ops=ops, sems=sems, mysem=mysem,
                         wset=wset, nops=nops, state=state, n=n):
                    for i in range(lo, hi):
                        waits, closure, inc = ops[i]
                        for s, v in sorted(waits.items()):
                            q.wait_ge(sems[s], v)
                        ins = closure(q)
                        if isinstance(inc, tuple):
                            ins.then_inc(sems[inc[0]], inc[1])
                            continue
                        state["pending"] += inc
                        state["cum"] += inc
                        dense = n != "tensor"
                        if (dense or state["cum"] in wset or i == nops - 1
                                or i == hi - 1):
                            ins.then_inc(mysem, state["pending"])
                            state["pending"] = 0

                getattr(block, n)(body)


def _build_FT_bf16():
    n = D
    t = np.arange(n)[:, None].astype(np.float64)
    F = np.zeros((n, n), np.float64)
    F[:, 0] = 1.0 / math.sqrt(n)
    maxf = (n - 1) // 2
    f = np.arange(1, maxf + 1)[None, :].astype(np.float64)
    ang = 2.0 * np.pi * f * t / n
    F[:, 1:1 + 2 * maxf:2] = math.sqrt(2.0 / n) * np.cos(ang)
    F[:, 2:2 + 2 * maxf:2] = math.sqrt(2.0 / n) * np.sin(ang)
    F[:, n - 1] = ((-1.0) ** np.arange(n)) / math.sqrt(n)
    return np.ascontiguousarray(F.T).astype(ml_dtypes.bfloat16)


_FT_CACHE = None


def _host_prep(inputs):
    global _FT_CACHE
    x = np.asarray(inputs["L0_attn_input_states"], np.float32).reshape(L, D)
    w_in = np.asarray(inputs["L0_attn_in_proj_w"], np.float32)
    conv_w = np.asarray(inputs["L0_attn_conv_w"], np.float32)
    conv_b = np.asarray(inputs["L0_attn_conv_b"], np.float32)
    dt_bias = np.asarray(inputs["L0_attn_dt_bias"], np.float32)
    A_log = np.asarray(inputs["L0_attn_A_log"], np.float32)
    Dp = np.asarray(inputs["L0_attn_D"], np.float32)
    norm_w = np.asarray(inputs["L0_attn_norm_w"], np.float32)
    out_w = np.asarray(inputs["L0_attn_out_w"], np.float32)
    w_base = np.asarray(inputs["L0_ffn_W_base"], np.float32)
    b_base = np.asarray(inputs["L0_ffn_b_base"], np.float32)
    enc0 = np.asarray(inputs["L0_ffn_enc0"], np.float32)
    enc1 = np.asarray(inputs["L0_ffn_enc1"], np.float32)
    c0 = np.asarray(inputs["L0_ffn_coefs0"], np.float32)
    c1 = np.asarray(inputs["L0_ffn_coefs1"], np.float32)
    m0 = np.asarray(inputs["L0_ffn_mask0"])
    m1 = np.asarray(inputs["L0_ffn_mask1"])

    sh = {}
    wiT = np.ascontiguousarray(w_in.T)
    sh["w_in"] = np.ascontiguousarray(wiT.reshape(32, 128, NOC, 128).transpose(2, 0, 1, 3))
    woT = np.ascontiguousarray(out_w.T)
    sh["w_out"] = np.ascontiguousarray(woT.reshape(64, 128, 32, 128).transpose(2, 0, 1, 3))
    wbT = np.ascontiguousarray(w_base.T)
    sh["w_base"] = np.ascontiguousarray(wbT.reshape(32, 128, 32, 128).transpose(2, 0, 1, 3))
    if _FT_CACHE is None:
        _FT_CACHE = _build_FT_bf16()
    sh["ft"] = _FT_CACHE
    sh["enc"] = np.ascontiguousarray(np.concatenate([enc0, enc1], axis=0).T)
    sh["c01"] = np.ascontiguousarray(np.concatenate(
        [c0.transpose(1, 2, 0).reshape(D, 96),
         c1.transpose(1, 2, 0).reshape(D, 24)], axis=1))
    sh["m01"] = np.ascontiguousarray(np.concatenate(
        [m0.transpose(1, 2, 0).reshape(D, 96),
         m1.transpose(1, 2, 0).reshape(D, 24)], axis=1)).view(np.uint8)
    sh["cw"] = np.ascontiguousarray(conv_w.reshape(80, 128, 4).transpose(1, 0, 2).reshape(128, 320))
    sh["cb"] = np.ascontiguousarray(conv_b.reshape(80, 128).T)
    sh["dtb"] = np.ascontiguousarray(dt_bias.reshape(128, 1))
    sh["alog"] = np.ascontiguousarray(A_log.reshape(128, 1))
    dexp = np.empty((128, 64), np.float32)
    for oc in range(64):
        dexp[0:64, oc] = Dp[2 * oc]
        dexp[64:128, oc] = Dp[2 * oc + 1]
    sh["dexp"] = dexp
    sh["nwc"] = np.ascontiguousarray(norm_w.reshape(64, 128).T)
    sh["bbc"] = np.ascontiguousarray(b_base.reshape(32, 128).T)
    sh["ident"] = np.eye(128, dtype=np.float32)
    sh["ones1"] = np.ones((1, 128), np.float32)
    sh["onest"] = np.ones((128, CS), np.float32)
    sh["tril"] = np.ascontiguousarray(
        (np.arange(CS)[None, :] >= np.arange(128)[:, None]).astype(np.float32))
    sh["invgs"] = np.full((128, 1), 1.0 / GS, np.float32).astype(ml_dtypes.bfloat16)
    sh["invd"] = np.full((128, 1), 1.0 / D, np.float32).astype(ml_dtypes.bfloat16)
    sh["epsc"] = np.full((128, 1), EPS, np.float32)
    sh["onec"] = np.ones((128, 1), np.float32)

    per_core = []
    for k in range(NCORES):
        d = {}
        s = k * T
        if k == 0:
            xs = np.concatenate([np.zeros((HALO, D), np.float32), x[0:T]], axis=0)
        else:
            xs = x[s - HALO:s + T]
        d["x"] = np.ascontiguousarray(xs)
        pmt = np.zeros((8, 8), np.float32)
        for m in range(8):
            for j in range(8):
                if j < m < k:
                    pmt[m, j] = 1.0
        d["pmt"] = pmt
        zm = np.zeros((128, 8), np.float32)
        zm[:, 0:k] = 1.0
        d["zmask"] = zm
        per_core.append(d)
    return sh, per_core


PHASE_LIMIT = int(os.environ.get("KPHASES", "99"))
P4SUB = int(os.environ.get("KP4SUB", "99"))
P4OPS = int(os.environ.get("KP4OPS", "99"))
P4Y = int(os.environ.get("KP4Y", "2"))


def build_program():
    nc = bass.Bass("TRN2", target_bir_lowering=False, debug=False)
    B = Builder(nc)

    def ein(name, shape, dt):
        return nc.dram_tensor(name, shape, dt, kind="ExternalInput").ap(), B.buf("d:" + name)

    x_in, b_x_in = ein("x", [TH, D], F32)
    w_in, b_w_in = ein("w_in", [NOC, 32, 128, 128], F32)
    w_out, b_w_out = ein("w_out", [32, 64, 128, 128], F32)
    w_baseT, b_w_base = ein("w_base", [32, 32, 128, 128], F32)
    ft_in, b_ft = ein("ft", [D, D], BF16)
    enc_in, b_enc = ein("enc", [D, 40], F32)
    c01_in, b_c01 = ein("c01", [D, 120], F32)
    m01_in, b_m01 = ein("m01", [D, 120], U8)
    cw_in, b_cw = ein("cw", [128, 320], F32)
    cb_in, b_cb = ein("cb", [128, 80], F32)
    dtb_in, b_dtb = ein("dtb", [128, 1], F32)
    alog_in, b_alog = ein("alog", [128, 1], F32)
    dexp_in, b_dexp = ein("dexp", [128, 64], F32)
    nwc_in, b_nwc = ein("nwc", [128, 64], F32)
    bbc_in, b_bbc = ein("bbc", [128, 32], F32)
    ident_in, b_ident_in = ein("ident", [128, 128], F32)
    ones1_in, b_ones1_in = ein("ones1", [1, 128], F32)
    onest_in, b_onest_in = ein("onest", [128, CS], F32)
    tril_in, b_tril_in = ein("tril", [128, CS], F32)
    invgs_in, b_invgs_in = ein("invgs", [128, 1], BF16)
    invd_in, b_invd_in = ein("invd", [128, 1], BF16)
    epsc_in, b_epsc_in = ein("epsc", [128, 1], F32)
    onec_in, b_onec_in = ein("onec", [128, 1], F32)
    pmt_in, b_pmt_in = ein("pmt", [8, 8], F32)
    zmask_in, b_zmask_in = ein("zmask", [128, 8], F32)

    out_ext = nc.dram_tensor("out", [T, D], F32, kind="ExternalOutput").ap()
    b_out = B.buf("d:out")
    KDEBUG = os.environ.get("KDEBUG", "") != ""
    dbg = {}
    if KDEBUG:
        for nm, shape, dt in [("dbg_x", [INTER, T], BF16), ("dbg_sg", [INTER, T], BF16),
                              ("dbg_b", [G * NST, T], BF16), ("dbg_attn", [D, T], F32),
                              ("dbg_h", [D, T], F32), ("dbg_cum", [NCH, H, CS], F32),
                              ("dbg_g", [INTER, T], BF16), ("dbg_z2", [D, T], F32),
                              ("dbg_agb", [NCORES * NST, INTER], BF16),
                              ("dbg_agd", [NCORES, H], F32)]:
            dbg[nm] = (nc.dram_tensor(nm, shape, dt, kind="ExternalOutput").ap(),
                       B.buf("d:" + nm))

    def idram(name, shape, dt):
        return nc.dram_tensor(name, shape, dt).ap(), B.buf("i:" + name)

    d_xbf, b_xbf = idram("d_xbf", [THP, D], BF16)
    d_x, b_dx = idram("d_x", [INTER, T], BF16)
    d_b, b_db = idram("d_b", [G * NST, T], BF16)
    d_sg, b_dsg = idram("d_sg", [INTER, T], BF16)
    d_g, b_dg = idram("d_g", [INTER, T], BF16)
    d_attn, b_dattn = idram("d_attn", [D, T], F32)
    d_h, b_dh = idram("d_h", [D, T], F32)
    d_z2, b_dz2 = idram("d_z2", [D, T], F32)
    d_cum, b_dcum = idram("d_cum", [NCH, H, CS], F32)
    d_agbi, b_agbi = idram("d_agbi", [NST, INTER], BF16)
    d_agbo, b_agbo = idram("d_agbo", [NCORES * NST, INTER], BF16)
    d_agdi, b_agdi = idram("d_agdi", [1, H], F32)
    d_agdo, b_agdo = idram("d_agdo", [NCORES, H], F32)

    stack_all = contextlib.ExitStack()

    def sb(name, shape, dt, st=None):
        hdl = (st or stack_all).enter_context(nc.sbuf_tensor("sb_" + name, shape, dt))
        return hdl, B.buf("s:" + name)

    def ps(name, st=None):
        hdl = (st or stack_all).enter_context(nc.psum_tensor("ps_" + name, [128, 512], F32))
        return hdl, B.buf("p:" + name)

    s_ident, b_ident = sb("ident", [128, 128], F32)
    s_ones1, b_ones1 = sb("ones1", [1, 128], F32)
    s_onest, b_onest = sb("onest", [128, CS], F32)
    s_tril, b_tril = sb("tril", [128, CS], F32)
    s_cw, b_scw = sb("cw", [128, 320], F32)
    s_cb, b_scb = sb("cb", [128, 80], F32)
    s_dtb, b_sdtb = sb("dtb", [128, 1], F32)
    s_alog, b_salog = sb("alog", [128, 1], F32)
    s_dexp, b_sdexp = sb("dexp", [128, 64], F32)
    s_nwc, b_snwc = sb("nwc", [128, 64], F32)
    s_bbc, b_sbbc = sb("bbc", [128, 32], F32)
    s_invgs, b_sinvgs = sb("invgs", [128, 1], BF16)
    s_invd, b_sinvd = sb("invd", [128, 1], BF16)
    s_epsc, b_sepsc = sb("epsc", [128, 1], F32)
    s_onec, b_sonec = sb("onec", [128, 1], F32)
    s_pmt, b_spmt = sb("pmt", [8, 8], F32)
    s_zmask, b_szmask = sb("zmask", [128, 8], F32)
    s_negA, b_negA = sb("negA", [128, 1], F32)
    s_row, b_srow = sb("rowtmp", [1, 128], F32)
    s_enc, b_senc = sb("encbf", [128, 32 * 40], BF16)
    s_fwt, b_sfwt = sb("fwt", [40, D], BF16)
    s_e01, b_se01 = sb("e01", [40, T], BF16)

    sem_stack = contextlib.ExitStack()
    for n in ENGS:
        B.engs[n].sem = sem_stack.enter_context(nc.semaphore("sem_" + n))
    for g in GROUPS:
        B.gsem[g] = sem_stack.enter_context(nc.semaphore("sem_" + g))
    block = sem_stack.enter_context(nc.Block())

    def dma(eng, out_ap, in_ap, reads, writes, group="g_rd", **kw):
        return B.op(eng, lambda q, o=out_ap, i=in_ap, kw=kw: q.dma_start(out=o, in_=i, **kw),
                    reads=reads, writes=writes, inc=16, group=group)

    st14 = contextlib.ExitStack()
    s_dt, b_sdt = sb("dtraw", [128, TH], F32, st14)
    s_cumT, b_scumT = sb("cumT", [128, NCH * CS], F32, st14)
    s_ncumT, b_sncumT = sb("ncumT", [128, NCH * CS], F32, st14)
    s_dtpT, b_sdtpT = sb("dtpT", [128, NCH * CS], BF16, st14)
    s_dlastT, b_sdlastT = sb("dlastT", [128, NCH * CS], BF16, st14)
    s_fT, b_sfT = sb("fT", [128, CS], BF16, st14)
    s_cdrow, b_scdrow = sb("cdrow", [1, NCH * H], F32, st14)
    s_B, b_sB = sb("Bfm", [128, G * T], BF16, st14)
    s_C, b_sC = sb("Cfm", [128, G * T], BF16, st14)
    st1 = contextlib.ExitStack()
    s_xfm, b_xfm = [], []
    for kt in range(32):
        h_, bb = sb(f"xfm{kt}", [128, THP], BF16, st1)
        s_xfm.append(h_)
        b_xfm.append(bb)
    s_wbuf, b_wbuf = [], []
    for i in range(2):
        h_, bb = sb(f"win{i}", [128, 32 * 128], BF16, st1)
        s_wbuf.append(h_)
        b_wbuf.append(bb)
    p_ip = [ps(f"pip{i}", st1) for i in range(3)]
    s_pre, b_pre = [], []
    s_cvt, b_cvt = [], []
    s_evo, b_evo = [], []
    for i in range(2):
        h_, bb = sb(f"pre{i}", [128, TH], BF16, st1)
        s_pre.append(h_)
        b_pre.append(bb)
        h_, bb = sb(f"cvt{i}", [128, T], BF16, st1)
        s_cvt.append(h_)
        b_cvt.append(bb)
        h_, bb = sb(f"evo{i}", [128, TH], BF16, st1)
        s_evo.append(h_)
        b_evo.append(bb)

    for s_t, d_ap, b_s, b_d in [
        (s_ident, ident_in, b_ident, b_ident_in), (s_ones1, ones1_in, b_ones1, b_ones1_in),
        (s_onest, onest_in, b_onest, b_onest_in), (s_tril, tril_in, b_tril, b_tril_in),
        (s_cw, cw_in, b_scw, b_cw), (s_cb, cb_in, b_scb, b_cb),
        (s_dtb, dtb_in, b_sdtb, b_dtb), (s_alog, alog_in, b_salog, b_alog),
        (s_dexp, dexp_in, b_sdexp, b_dexp), (s_nwc, nwc_in, b_snwc, b_nwc),
        (s_bbc, bbc_in, b_sbbc, b_bbc), (s_invgs, invgs_in, b_sinvgs, b_invgs_in),
        (s_invd, invd_in, b_sinvd, b_invd_in), (s_epsc, epsc_in, b_sepsc, b_epsc_in),
        (s_onec, onec_in, b_sonec, b_onec_in), (s_pmt, pmt_in, b_spmt, b_pmt_in),
        (s_zmask, zmask_in, b_szmask, b_zmask_in),
    ]:
        dma("sync", s_t[:, :], d_ap, reads=[b_d], writes=[b_s], group="g_const")

    B.op("scalar", lambda q: q.activation(out=s_negA[:, :], in_=s_alog[:, :], func=AF.Exp),
         reads=[b_salog], writes=[b_negA])
    B.op("vector", lambda q: q.tensor_scalar(out=s_negA[:, :], in0=s_negA[:, :],
                                             scalar1=-1.0, scalar2=None, op0=ALU.mult),
         reads=[b_negA], writes=[b_negA])

    dma("gpsimd", d_xbf[0:TH, :], x_in, reads=[b_x_in], writes=[b_xbf], group="g_xc")
    for kt in range(32):
        dma("sync", s_xfm[kt][:, :], d_xbf[:, kt * 128:(kt + 1) * 128],
            reads=[b_xbf], writes=[b_xfm[kt]], transpose=True, group="g_const")

    for oc in range(NOC):
        wb, bwb = s_wbuf[oc % 2], b_wbuf[oc % 2]
        dma("gpsimd", wb[:, :].rearrange("p (a b) -> p a b", a=32),
            w_in[oc].rearrange("a p b -> p a b"),
            reads=[b_w_in], writes=[bwb], group="g_sw%d" % (oc % 2))
        par = oc % 2
        spans = SPANS_XBC if (OC_X0 <= oc < OC_DT) else SPANS_GD
        for ti, (t0, t1) in enumerate(spans):
            pp, bpp = p_ip[(oc * 3 + ti) % 3]
            n = t1 - t0
            for k in range(32):
                B.op("tensor",
                     lambda q, pp=pp, wb=wb, k=k, t0=t0, t1=t1: q.matmul(
                         pp[:, 0:(t1 - t0)], wb[:, k * 128:(k + 1) * 128],
                         s_xfm[k][:, t0:t1], start=(k == 0), stop=(k == 31)),
                     reads=[bwb, b_xfm[k]], writes=[bpp])
            if oc < OC_X0:
                B.op("scalar", lambda q, pp=pp, t0=t0, n=n, par=par: q.activation(
                    out=s_evo[par][:, t0:t0 + n], in_=pp[:, 0:n], func=AF.Silu),
                    reads=[bpp], writes=[b_evo[par]])
            elif oc == OC_DT:
                B.op("scalar", lambda q, pp=pp, t0=t0, n=n: q.activation(
                    out=s_dt[:, t0:t0 + n], in_=pp[:, 0:n], func=AF.Copy),
                    reads=[bpp], writes=[b_sdt])
            else:
                B.op("scalar", lambda q, pp=pp, t0=t0, n=n, par=par: q.activation(
                    out=s_pre[par][:, t0:t0 + n], in_=pp[:, 0:n], func=AF.Copy),
                    reads=[bpp], writes=[b_pre[par]])
        if oc < OC_X0:
            dma("sync", d_sg[oc * 128:(oc + 1) * 128, :], s_evo[par][:, HALO:TH],
                reads=[b_evo[par]], writes=[b_dsg], group="g_sp")
        elif oc == OC_DT:
            pass
        else:
            co = oc - OC_X0
            B.op("vector", lambda q, co=co, par=par: q.tensor_scalar(
                out=s_cvt[par][:, :], in0=s_pre[par][:, 3:3 + T],
                scalar1=s_cw[:, 4 * co + 3:4 * co + 4], scalar2=None, op0=ALU.mult),
                reads=[b_pre[par], b_scw], writes=[b_cvt[par]])
            for kk in (2, 1, 0):
                B.op("vector", lambda q, co=co, kk=kk, par=par: q.scalar_tensor_tensor(
                    out=s_cvt[par][:, :], in0=s_pre[par][:, kk:kk + T],
                    scalar=s_cw[:, 4 * co + kk:4 * co + kk + 1], in1=s_cvt[par][:, :],
                    op0=ALU.mult, op1=ALU.add),
                    reads=[b_pre[par], b_scw, b_cvt[par]], writes=[b_cvt[par]])
            B.op("scalar", lambda q, co=co, par=par: q.activation(
                out=s_evo[par][:, 0:T], in_=s_cvt[par][:, :], func=AF.Silu,
                bias=s_cb[:, co:co + 1]), reads=[b_cvt[par], b_scb], writes=[b_evo[par]])
            if oc < OC_B0:
                dma("sync", d_x[co * 128:(co + 1) * 128, :], s_evo[par][:, 0:T],
                    reads=[b_evo[par]], writes=[b_dx], group="g_sp")
            elif oc < OC_C0:
                g = oc - OC_B0
                B.op("vector", lambda q, g=g, par=par: q.tensor_copy(
                    out=s_B[:, g * T:(g + 1) * T], in_=s_evo[par][:, 0:T]),
                    reads=[b_evo[par]], writes=[b_sB])
                dma("sync", d_b[g * 128:(g + 1) * 128, :], s_evo[par][:, 0:T],
                    reads=[b_evo[par]], writes=[b_db], group="g_sp")
            else:
                g = oc - OC_C0
                B.op("vector", lambda q, g=g, par=par: q.tensor_copy(
                    out=s_C[:, g * T:(g + 1) * T], in_=s_evo[par][:, 0:T]),
                    reads=[b_evo[par]], writes=[b_sC])

    B.flush(block)
    snap1 = B.snap()
    B.barrier(snap1)
    st1.close()
    if PHASE_LIMIT <= 1:
        B.finalize(block)
        sem_stack.close()
        return nc

    st2 = contextlib.ExitStack()
    s_dtp, b_sdtp = sb("dtp", [128, T], F32, st2)
    s_dA, b_sdA = sb("dA", [128, T], F32, st2)
    s_brun, b_brun = sb("brun", [128, INTER], F32, st2)
    s_bigT, b_bigT = sb("bigT", [128, INTER], BF16, st2)
    s_BT, b_sBT = sb("BT", [128, G * NST], BF16, st2)
    s_cum, b_scum = sb("cumc", [128, CS], F32, st2)
    s_tmpA, b_tmpA = sb("tmpA", [128, CS], F32, st2)
    s_tmpB, b_tmpB = sb("tmpB", [128, 4], F32, st2)
    p_tr0 = ps("ptr0", st2)
    p_tr1 = ps("ptr1", st2)
    p_row = ps("prow", st2)
    p_cdb = ps("pcdb", st2)
    p_st = [ps("pst0", st2), ps("pst1", st2)]
    p_dtot = ps("pdtot", st2)

    B.op("vector", lambda q: q.tensor_scalar(
        out=s_dtp[:, :], in0=s_dt[:, HALO:TH], scalar1=s_dtb[:, :], scalar2=None,
        op0=ALU.add), reads=[b_sdt, b_sdtb], writes=[b_sdtp])
    B.op("scalar", lambda q: q.activation(out=s_dA[:, :], in_=s_dtp[:, :], func=AF.Abs),
         reads=[b_sdtp], writes=[b_sdA])
    B.op("scalar", lambda q: q.activation(out=s_dA[:, :], in_=s_dA[:, :], func=AF.Exp,
                                          scale=-1.0), reads=[b_sdA], writes=[b_sdA])
    B.op("scalar", lambda q: q.activation(out=s_dA[:, :], in_=s_dA[:, :], func=AF.Ln,
                                          bias=s_onec[:, :]),
         reads=[b_sdA, b_sonec], writes=[b_sdA])
    B.op("vector", lambda q: q.tensor_scalar(
        out=s_dtp[:, :], in0=s_dtp[:, :], scalar1=0.0, scalar2=None, op0=ALU.max),
        reads=[b_sdtp], writes=[b_sdtp])
    B.op("vector", lambda q: q.tensor_add(out=s_dtp[:, :], in0=s_dtp[:, :], in1=s_dA[:, :]),
         reads=[b_sdtp, b_sdA], writes=[b_sdtp])
    B.op("vector", lambda q: q.tensor_scalar(
        out=s_dA[:, :], in0=s_dtp[:, :], scalar1=s_negA[:, :], scalar2=None, op0=ALU.mult),
        reads=[b_sdtp, b_negA], writes=[b_sdA])
    B.op("vector", lambda q: q.memset(s_brun[:, :], 0.0), reads=[], writes=[b_brun])

    for c in range(NCH):
        c0, c1 = c * CS, (c + 1) * CS
        B.op("vector", lambda q, c0=c0, c1=c1: q.tensor_tensor_scan(
            out=s_cum[:, :], data0=s_onest[:, :], data1=s_dA[:, c0:c1],
            initial=0.0, op0=ALU.mult, op1=ALU.add), reads=[b_onest, b_sdA], writes=[b_scum])
        dma("sync", d_cum[c], s_cum[:, :], reads=[b_scum], writes=[b_dcum], group="g_sp")
        B.op("tensor", lambda q: q.transpose(p_tr0[0][:, 0:CS], s_cum[:, :], s_ident[:, :]),
             reads=[b_scum, b_ident], writes=[p_tr0[1]])
        B.op("vector", lambda q, c0=c0, c1=c1: q.tensor_copy(
            out=s_cumT[:, c0:c1], in_=p_tr0[0][:, 0:CS]),
            reads=[p_tr0[1]], writes=[b_scumT])
        B.op("vector", lambda q, c0=c0, c1=c1: q.tensor_scalar(
            out=s_ncumT[:, c0:c1], in0=s_cumT[:, c0:c1], scalar1=-1.0,
            scalar2=None, op0=ALU.mult), reads=[b_scumT], writes=[b_sncumT])
        B.op("scalar", lambda q: q.activation(
            out=s_tmpA[:, :], in_=s_cum[:, :], func=AF.Exp, scale=-1.0,
            bias=s_cum[:, CS - 1:CS]), reads=[b_scum], writes=[b_tmpA])
        B.op("tensor", lambda q, c0=c0, c1=c1: q.transpose(
            p_tr1[0][:, 0:CS], s_dtp[:, c0:c1], s_ident[:, :]),
            reads=[b_sdtp, b_ident], writes=[p_tr1[1]])
        B.op("vector", lambda q, c0=c0, c1=c1: q.tensor_copy(
            out=s_dtpT[:, c0:c1], in_=p_tr1[0][:, 0:CS]),
            reads=[p_tr1[1]], writes=[b_sdtpT])
        B.op("tensor", lambda q: q.transpose(p_tr1[0][:, 0:CS], s_tmpA[:, :], s_ident[:, :]),
             reads=[b_tmpA, b_ident], writes=[p_tr1[1]])
        B.op("vector", lambda q, c0=c0, c1=c1: q.tensor_copy(
            out=s_dlastT[:, c0:c1], in_=p_tr1[0][:, 0:CS]),
            reads=[p_tr1[1]], writes=[b_sdlastT])
        B.op("vector", lambda q, c0=c0, c1=c1: q.reduce_sum(
            out=s_tmpB[:, 0:1], in_=s_dA[:, c0:c1], axis=AX.X),
            reads=[b_sdA], writes=[b_tmpB])
        B.op("tensor", lambda q, c=c: q.matmul(
            p_dtot[0][0:1, 0:128], s_tmpB[:, 0:1], s_ident[:, :],
            start=(c == 0), stop=(c == NCH - 1)),
            reads=[b_tmpB, b_ident], writes=[p_dtot[1]])
        B.op("scalar", lambda q: q.activation(out=s_tmpB[:, 1:2], in_=s_tmpB[:, 0:1],
                                              func=AF.Exp), reads=[b_tmpB], writes=[b_tmpB])
        B.op("tensor", lambda q: q.matmul(
            p_row[0][0:1, 0:128], s_tmpB[:, 1:2], s_ident[:, :], start=True, stop=True),
            reads=[b_tmpB, b_ident], writes=[p_row[1]])
        B.op("vector", lambda q, c=c: q.tensor_copy(
            out=s_cdrow[:, c * H:(c + 1) * H], in_=p_row[0][0:1, 0:128]),
            reads=[p_row[1]], writes=[b_scdrow])
        if c > 0:
            B.op("tensor", lambda q, c=c: q.matmul(
                p_cdb[0][:, 0:128], s_ones1[:, :], s_cdrow[0:1, c * H:(c + 1) * H],
                start=True, stop=True), reads=[b_ones1, b_scdrow], writes=[p_cdb[1]])
            B.op("vector", lambda q: q.tensor_tensor(
                out=s_brun[:, :].rearrange("p (h a) -> p h a", h=H),
                in0=s_brun[:, :].rearrange("p (h a) -> p h a", h=H),
                in1=p_cdb[0][:, 0:128].unsqueeze(2).broadcast_to([128, H, PH]),
                op=ALU.mult), reads=[b_brun, p_cdb[1]], writes=[b_brun])
        dma("sync", s_bigT[:, :], d_x[:, c0:c1], reads=[b_dx], writes=[b_bigT], transpose=True)
        B.op("vector", lambda q, c0=c0, c1=c1: q.tensor_mul(
            out=s_fT[:, :], in0=s_dtpT[:, c0:c1], in1=s_dlastT[:, c0:c1]),
            reads=[b_sdtpT, b_sdlastT], writes=[b_sfT])
        B.op("vector", lambda q: q.tensor_tensor(
            out=s_bigT[:, :].rearrange("p (h a) -> p h a", h=H),
            in0=s_bigT[:, :].rearrange("p (h a) -> p h a", h=H),
            in1=s_fT[:, :].unsqueeze(2).broadcast_to([128, H, PH]),
            op=ALU.mult), reads=[b_bigT, b_sfT], writes=[b_bigT])
        dma("sync", s_BT[:, :], d_b[:, c0:c1], reads=[b_db], writes=[b_sBT], transpose=True)
        for hb in range(16):
            pst, bst = p_st[hb % 2]
            for hh in range(8):
                h = hb * 8 + hh
                g = h // 16
                B.op("tensor", lambda q, pst=pst, g=g, h=h, hh=hh: q.matmul(
                    pst[:, hh * PH:(hh + 1) * PH], s_BT[:, g * NST:(g + 1) * NST],
                    s_bigT[:, h * PH:(h + 1) * PH], start=True, stop=True),
                    reads=[b_sBT, b_bigT], writes=[bst])
            B.op("vector", lambda q, pst=pst, hb=hb: q.tensor_tensor(
                out=s_brun[:, hb * 512:(hb + 1) * 512], in0=s_brun[:, hb * 512:(hb + 1) * 512],
                in1=pst[:, :], op=ALU.add), reads=[bst, b_brun], writes=[b_brun])

    dma("gpsimd", d_agbi, s_brun[:, :], reads=[b_brun], writes=[b_agbi], group="g_spg")
    B.op("vector", lambda q: q.tensor_copy(out=s_row[:, :], in_=p_dtot[0][0:1, 0:128]),
         reads=[p_dtot[1]], writes=[b_srow])
    dma("sync", d_agdi, s_row[:, :], reads=[b_srow], writes=[b_agdi], group="g_sp")
    snap2 = B.snap()
    B.op("gpsimd", lambda q: q.collective_compute(
        "AllGather", ALU.bypass, replica_groups=[list(range(NCORES))],
        ins=[d_agbi], outs=[d_agbo]), reads=[b_agbi], writes=[b_agbo], inc=1,
        group="g_cc")
    B.op("gpsimd", lambda q: q.collective_compute(
        "AllGather", ALU.bypass, replica_groups=[list(range(NCORES))],
        ins=[d_agdi], outs=[d_agdo]), reads=[b_agdi], writes=[b_agdo], inc=1,
        group="g_cc")
    B.flush(block)
    B.barrier(snap2)
    st2.close()
    if PHASE_LIMIT <= 2:
        B.finalize(block)
        sem_stack.close()
        return nc

    st3 = contextlib.ExitStack()
    s_ftb, b_ftb = [], []
    for i in range(2):
        h_, bb = sb(f"ftb{i}", [128, D], BF16, st3)
        s_ftb.append(h_)
        b_ftb.append(bb)
    s_c01, b_sc01 = [], []
    s_m01f, b_sm01f = [], []
    for i in range(2):
        h_, bb = sb(f"c01_{i}", [128, 120], F32, st3)
        s_c01.append(h_)
        b_sc01.append(bb)
        h_, bb = sb(f"m01_{i}", [128, 120], F32, st3)
        s_m01f.append(h_)
        b_sm01f.append(bb)
    s_wm, b_swm = sb("wm", [128, 120], F32, st3)
    s_wt, b_swt = sb("wt", [128, 40], F32, st3)
    s_w01, b_sw01 = sb("w01", [128, 32 * 40], BF16, st3)
    p_fwt = [ps(f"pfwt{i}", st3) for i in range(8)]

    dma("gpsimd", s_enc[:, :].rearrange("p (a r) -> p a r", a=32),
        enc_in.rearrange("(a p) r -> p a r", p=128), reads=[b_enc], writes=[b_senc],
        group="g_rdg")
    for kt in range(32):
        i2 = kt % 2
        dma("sync", s_c01[i2][:, :], c01_in[kt * 128:(kt + 1) * 128, :],
            reads=[b_c01], writes=[b_sc01[i2]])
        dma("gpsimd", s_m01f[i2][:, :], m01_in[kt * 128:(kt + 1) * 128, :],
            reads=[b_m01], writes=[b_sm01f[i2]], group="g_rdg")
        B.op("vector", lambda q, i2=i2: q.tensor_mul(
            out=s_wm[:, :], in0=s_c01[i2][:, :], in1=s_m01f[i2][:, :]),
            reads=[b_sc01[i2], b_sm01f[i2]], writes=[b_swm])
        wmv = s_wm[:, :].rearrange("p (r e) -> p r e", r=40)
        B.op("vector", lambda q, wmv=wmv: q.tensor_add(
            out=s_wt[:, :].unsqueeze(2), in0=wmv[:, :, 0:1], in1=wmv[:, :, 1:2]),
            reads=[b_swm], writes=[b_swt])
        B.op("vector", lambda q, wmv=wmv, kt=kt: q.tensor_add(
            out=s_w01[:, kt * 40:(kt + 1) * 40].unsqueeze(2),
            in0=s_wt[:, :].unsqueeze(2), in1=wmv[:, :, 2:3]),
            reads=[b_swt, b_swm], writes=[b_sw01])
    for kt in range(32):
        fb, bfb = s_ftb[kt % 2], b_ftb[kt % 2]
        dma("sync", fb[:, :], ft_in[kt * 128:(kt + 1) * 128, :], reads=[b_ft], writes=[bfb],
            group="g_w%d" % (kt % 2))
        for ot in range(8):
            pf, bpf = p_fwt[ot]
            B.op("tensor", lambda q, pf=pf, fb=fb, kt=kt, ot=ot: q.matmul(
                pf[0:40, 0:512], s_w01[:, kt * 40:(kt + 1) * 40],
                fb[:, ot * 512:(ot + 1) * 512], start=(kt == 0), stop=(kt == 31)),
                reads=[b_sw01, bfb], writes=[bpf])
    for ot in range(8):
        pf, bpf = p_fwt[ot]
        B.op("scalar", lambda q, pf=pf, ot=ot: q.activation(
            out=s_fwt[:, ot * 512:(ot + 1) * 512], in_=pf[0:40, 0:512],
            func=AF.Copy, scale=SCALING), reads=[bpf], writes=[b_sfwt])
    B.flush(block)
    snap3 = B.snap()
    B.barrier(snap3)
    st3.close()
    if PHASE_LIMIT <= 3:
        B.finalize(block)
        sem_stack.close()
        return nc

    st4 = contextlib.ExitStack()
    s_prev, b_prev = sb("prev", [128, INTER], F32, st4)
    s_prevb, b_prevb = sb("prevb", [128, INTER], BF16, st4)
    s_big, b_big = sb("big", [128, INTER], BF16, st4)
    s_bigH, b_bigH = sb("bigH", [128, INTER], BF16, st4)
    s_BT4, b_sBT4 = sb("BT4", [128, G * NST], BF16, st4)
    s_sgc, b_ssgc = sb("sgc", [128, INTER], BF16, st4)
    s_xfc, b_sxfc = sb("xfc", [128, INTER], BF16, st4)
    s_scm, b_sscm = sb("scm", [128, G * CS], BF16, st4)
    s_cumq, s_E4, s_S4, s_ce4, s_ecb, s_gq, s_sbc = [], [], [], [], [], [], []
    b_scumq, b_sE4, b_sS4, b_sce4, b_secb, b_sgq, b_ssbc = [], [], [], [], [], [], []
    for i in range(2):
        t_, bb = sb(f"cumq{i}", [1, 4 * CS], F32, st4)
        s_cumq.append(t_); b_scumq.append(bb)
        t_, bb = sb(f"E4{i}", [128, 512], BF16, st4)
        s_E4.append(t_); b_sE4.append(bb)
        t_, bb = sb(f"S4{i}", [128, 512], BF16, st4)
        s_S4.append(t_); b_sS4.append(bb)
        t_, bb = sb(f"ce4{i}", [128, 512], BF16, st4)
        s_ce4.append(t_); b_sce4.append(bb)
        t_, bb = sb(f"ecb{i}", [128, 512], BF16, st4)
        s_ecb.append(t_); b_secb.append(bb)
        t_, bb = sb(f"gq{i}", [128, 2 * CS], BF16, st4)
        s_gq.append(t_); b_sgq.append(bb)
        t_, bb = sb(f"sbc{i}", [128, CS], F32, st4)
        s_sbc.append(t_); b_ssbc.append(bb)
    s_wz, b_swz = sb("wz", [128, 8], F32, st4)
    s_dtg, b_sdtg = sb("dtg", [8, 128], F32, st4)
    s_yt, b_syt = sb("yt", [128, 128], F32, st4)
    s_g0, b_sg0 = [], []
    for i in range(2):
        t_, bb = sb(f"g0_{i}", [128, 8 * CS], BF16, st4)
        s_g0.append(t_); b_sg0.append(bb)
    p_sc = [ps("psc0", st4), ps("psc1", st4)]
    p_bc = [ps("pbc0", st4), ps("pbc1", st4)]
    p_y = [ps("py0", st4), ps("py1", st4)]
    p_v = ps("pv", st4)
    p_sb2 = ps("psb", st4)
    var_ps = [p_v, p_sc[0]]
    sbc_ps = [p_sb2, p_sc[1]]

    dma("sync", s_dtg[:, :], d_agdo, reads=[b_agdo], writes=[b_sdtg])
    B.op("tensor", lambda q: q.matmul(p_bc[0][0][:, 0:8], s_dtg[:, :], s_pmt[:, :],
                                      start=True, stop=True),
         reads=[b_sdtg, b_spmt], writes=[p_bc[0][1]])
    B.op("scalar", lambda q: q.activation(out=s_wz[:, :], in_=p_bc[0][0][:, 0:8],
                                          func=AF.Exp), reads=[p_bc[0][1]], writes=[b_swz])
    B.op("vector", lambda q: q.tensor_mul(out=s_wz[:, :], in0=s_wz[:, :],
                                          in1=s_zmask[:, :]),
         reads=[b_swz, b_szmask], writes=[b_swz])
    B.op("vector", lambda q: q.memset(s_prev[:, :], 0.0), reads=[], writes=[b_prev])
    for j in range(NCORES - 1):
        dma("sync", s_bigH[:, :], d_agbo[j * NST:(j + 1) * NST, :],
            reads=[b_agbo], writes=[b_bigH])
        B.op("tensor", lambda q, j=j: q.matmul(
            p_bc[1][0][0:1, 0:128],
            s_wz[:, j:j + 1], s_ident[:, :], start=True, stop=True),
            reads=[b_swz, b_ident], writes=[p_bc[1][1]])
        B.op("vector", lambda q: q.tensor_copy(out=s_row[:, :], in_=p_bc[1][0][0:1, 0:128]),
             reads=[p_bc[1][1]], writes=[b_srow])
        B.op("tensor", lambda q: q.matmul(
            p_bc[0][0][:, 0:128], s_ones1[:, :], s_row[0:1, :], start=True, stop=True),
            reads=[b_ones1, b_srow], writes=[p_bc[0][1]])
        B.op("vector", lambda q: q.tensor_tensor(
            out=s_big[:, :].rearrange("p (h a) -> p h a", h=H),
            in0=s_bigH[:, :].rearrange("p (h a) -> p h a", h=H),
            in1=p_bc[0][0][:, 0:128].unsqueeze(2).broadcast_to([128, H, PH]),
            op=ALU.mult), reads=[b_bigH, p_bc[0][1]], writes=[b_big])
        B.op("vector", lambda q: q.tensor_tensor(
            out=s_prev[:, :], in0=s_prev[:, :], in1=s_big[:, :], op=ALU.add),
            reads=[b_prev, b_big], writes=[b_prev])
    B.op("vector", lambda q: q.tensor_copy(out=s_prevb[:, :], in_=s_prev[:, :]),
         reads=[b_prev], writes=[b_prevb])

    for c in range(NCH):
        c0, c1 = c * CS, (c + 1) * CS
        dma("sync", s_big[:, :], d_x[:, c0:c1], reads=[b_dx], writes=[b_big], transpose=True)
        B.op("vector", lambda q, c0=c0, c1=c1: q.tensor_tensor(
            out=s_big[:, :].rearrange("p (h a) -> p h a", h=H),
            in0=s_big[:, :].rearrange("p (h a) -> p h a", h=H),
            in1=s_dtpT[:, c0:c1].unsqueeze(2).broadcast_to([128, H, PH]),
            op=ALU.mult), reads=[b_big, b_sdtpT], writes=[b_big])
        B.op("vector", lambda q, c0=c0, c1=c1: q.tensor_tensor(
            out=s_bigH[:, :].rearrange("p (h a) -> p h a", h=H),
            in0=s_big[:, :].rearrange("p (h a) -> p h a", h=H),
            in1=s_dlastT[:, c0:c1].unsqueeze(2).broadcast_to([128, H, PH]),
            op=ALU.mult), reads=[b_big, b_sdlastT], writes=[b_bigH])
        dma("sync", s_BT4[:, :], d_b[:, c0:c1], reads=[b_db], writes=[b_sBT4], transpose=True)
        dma("sync", s_sgc[:, :].rearrange("p (a t) -> p a t", a=64),
            d_sg.rearrange("(a p) t -> p a t", p=128)[:, :, c0:c1],
            reads=[b_dsg], writes=[b_ssgc])
        dma("sync", s_xfc[:, :].rearrange("p (a t) -> p a t", a=64),
            d_x.rearrange("(a p) t -> p a t", p=128)[:, :, c0:c1],
            reads=[b_dx], writes=[b_sxfc])
        for g in range(G):
            pg, bpg = p_sc[g // 4]
            B.op("tensor", lambda q, pg=pg, g=g, c0=c0, c1=c1: q.matmul(
                pg[:, (g % 4) * CS:(g % 4 + 1) * CS], s_B[:, g * T + c0:g * T + c1],
                s_C[:, g * T + c0:g * T + c1], start=True, stop=True),
                reads=[b_sB, b_sC], writes=[bpg])
        for g in range(G):
            pg, bpg = p_sc[g // 4]
            B.op("vector", lambda q, pg=pg, g=g: q.tensor_tensor(
                out=s_scm[:, g * CS:(g + 1) * CS], in0=pg[:, (g % 4) * CS:(g % 4 + 1) * CS],
                in1=s_tril[:, :], op=ALU.mult), reads=[bpg, b_tril], writes=[b_sscm])
        def emit_decay(hb, c0, c1):
            bpar = hb % 2
            g = hb // 4
            h0 = hb * 4
            dma("sync", s_cumq[bpar][:, :],
                d_cum[c, h0:h0 + 4, :].rearrange("a t -> (a t)").unsqueeze(0),
                reads=[b_dcum], writes=[b_scumq[bpar]])
            pb, bpb = p_bc[bpar]
            B.op("tensor", lambda q, pb=pb, bpar=bpar: q.matmul(
                pb[:, 0:512], s_ones1[:, :],
                s_cumq[bpar][0:1, 0:512], start=True, stop=True),
                reads=[b_ones1, b_scumq[bpar]], writes=[bpb])
            B.op("scalar", lambda q, pb=pb, bpar=bpar: q.activation(
                out=s_ecb[bpar][:, :], in_=pb[:, 0:512], func=AF.Exp),
                reads=[bpb], writes=[b_secb[bpar]])
            for hh in range(4):
                B.op("scalar", lambda q, pb=pb, c0=c0, h0=h0, hh=hh, bpar=bpar:
                     q.activation(
                         out=s_E4[bpar][:, hh * CS:(hh + 1) * CS],
                         in_=pb[:, hh * CS:(hh + 1) * CS], func=AF.Exp,
                         bias=s_ncumT[:, c0 + h0 + hh:c0 + h0 + hh + 1]),
                     reads=[bpb, b_sncumT], writes=[b_sE4[bpar]])
            B.op("vector", lambda q, bpar=bpar: q.tensor_scalar(
                out=s_E4[bpar][:, :], in0=s_E4[bpar][:, :], scalar1=1.0,
                scalar2=None, op0=ALU.min),
                reads=[b_sE4[bpar]], writes=[b_sE4[bpar]])
            B.op("vector", lambda q, g=g, bpar=bpar: q.tensor_tensor(
                out=s_S4[bpar][:, :].rearrange("p (a t) -> p a t", a=4),
                in0=s_E4[bpar][:, :].rearrange("p (a t) -> p a t", a=4),
                in1=s_scm[:, g * CS:(g + 1) * CS].unsqueeze(1).broadcast_to([128, 4, CS]),
                op=ALU.mult), reads=[b_sE4[bpar], b_sscm], writes=[b_sS4[bpar]])
            B.op("vector", lambda q, g=g, c0=c0, c1=c1, bpar=bpar: q.tensor_tensor(
                out=s_ce4[bpar][:, :].rearrange("p (a t) -> p a t", a=4),
                in0=s_C[:, g * T + c0:g * T + c1].unsqueeze(1).broadcast_to([128, 4, CS]),
                in1=s_ecb[bpar][:, :].rearrange("p (a t) -> p a t", a=4),
                op=ALU.mult), reads=[b_sC, b_secb[bpar]], writes=[b_sce4[bpar]])

        def emit_y(hb):
            bpar = hb % 2
            h0 = hb * 4
            for hh in range(4):
                h = h0 + hh
                pr = h // 2
                py, bpy = p_y[pr % 2]
                po = (h % 2) * PH
                B.op("tensor", lambda q, py=py, po=po, h=h, hh=hh, bpar=bpar: q.matmul(
                    py[po:po + PH, 0:CS], s_big[:, h * PH:(h + 1) * PH],
                    s_S4[bpar][:, hh * CS:(hh + 1) * CS], start=True, stop=False),
                    reads=[b_big, b_sS4[bpar]], writes=[bpy])
                B.op("tensor", lambda q, py=py, po=po, h=h, hh=hh, bpar=bpar: q.matmul(
                    py[po:po + PH, 0:CS], s_prevb[:, h * PH:(h + 1) * PH],
                    s_ce4[bpar][:, hh * CS:(hh + 1) * CS], start=False, stop=True),
                    reads=[b_prevb, b_sce4[bpar]], writes=[bpy])

        def emit_gating(hb):
            h0 = hb * 4
            hpar = hb % 2
            gg0 = (h0 // 2) // 8
            for pri in range(2):
                pr = h0 // 2 + pri
                gg = pr // 8
                gp = gg % 2
                py2, bpy2 = p_y[pr % 2]
                B.op("vector", lambda q, py2=py2, pr=pr: q.scalar_tensor_tensor(
                    out=s_yt[:, :], in0=s_xfc[:, pr * CS:(pr + 1) * CS],
                    scalar=s_dexp[:, pr:pr + 1], in1=py2[:, 0:CS],
                    op0=ALU.mult, op1=ALU.add),
                    reads=[b_sxfc, b_sdexp, bpy2], writes=[b_syt])
                B.op("vector", lambda q, pr=pr, gp=gp: q.tensor_tensor(
                    out=s_g0[gp][:, (pr % 8) * CS:(pr % 8 + 1) * CS], in0=s_yt[:, :],
                    in1=s_sgc[:, pr * CS:(pr + 1) * CS], op=ALU.mult),
                    reads=[b_syt, b_ssgc], writes=[b_sg0[gp]])
            pr0 = h0 // 2
            gp0 = gg0 % 2
            sl = (pr0 % 8) * CS
            B.op("scalar", lambda q, gp0=gp0, sl=sl, hpar=hpar: q.activation(
                out=s_gq[hpar][:, :], in_=s_g0[gp0][:, sl:sl + 2 * CS],
                func=AF.Square), reads=[b_sg0[gp0]], writes=[b_sgq[hpar]])
            for pri in range(2):
                pr = h0 // 2 + pri
                gg = pr // 8
                gp = gg % 2
                pv, bpv = var_ps[gg % 2]
                B.op("tensor", lambda q, pr=pr, pv=pv, pri=pri, hpar=hpar: q.matmul(
                    pv[0:1, 0:CS], s_invgs[:, :],
                    s_gq[hpar][:, pri * CS:(pri + 1) * CS],
                    start=(pr % 8 == 0), stop=(pr % 8 == 7)),
                    reads=[b_sgq[hpar], b_sinvgs], writes=[bpv])
                if pr % 8 == 7:
                    psb, bpsb = sbc_ps[gg % 2]
                    B.op("scalar", lambda q, pv=pv: q.activation(
                        out=s_row[:, :], in_=pv[0:1, 0:CS], func=AF.Ln,
                        bias=s_epsc[0:1, :]), reads=[bpv, b_sepsc], writes=[b_srow])
                    B.op("scalar", lambda q: q.activation(
                        out=s_row[:, :], in_=s_row[:, :], func=AF.Exp, scale=-0.5),
                        reads=[b_srow], writes=[b_srow])
                    B.op("tensor", lambda q, psb=psb: q.matmul(
                        psb[:, 0:CS], s_ones1[:, :], s_row[0:1, :],
                        start=True, stop=True),
                        reads=[b_ones1, b_srow], writes=[bpsb])
                    B.op("scalar", lambda q, psb=psb, gp=gp: q.activation(
                        out=s_sbc[gp][:, :], in_=psb[:, 0:CS], func=AF.Copy),
                        reads=[bpsb], writes=[b_ssbc[gp]])
                    for p8 in range(8):
                        prr = gg * 8 + p8
                        B.op("vector", lambda q, p8=p8, prr=prr, gp=gp: q.scalar_tensor_tensor(
                            out=s_sgc[:, prr * CS:(prr + 1) * CS],
                            in0=s_g0[gp][:, p8 * CS:(p8 + 1) * CS],
                            scalar=s_nwc[:, prr:prr + 1], in1=s_sbc[gp][:, :],
                            op0=ALU.mult, op1=ALU.mult),
                            reads=[b_sg0[gp], b_snwc, b_ssbc[gp]], writes=[b_ssgc])

        for idx in range(32 + 2):
            if 0 <= idx - 2 < 32:
                emit_gating(idx - 2)
            if 0 <= idx - 1 < 32:
                emit_y(idx - 1)
            if idx < 32:
                emit_decay(idx, c0, c1)
        dma("scalar", d_g.rearrange("(a p) t -> p a t", p=128)[:, :, c0:c1],
            s_sgc[:, :].rearrange("p (a t) -> p a t", a=64),
            reads=[b_ssgc], writes=[b_dg], group="g_sp")
        B.op("tensor", lambda q, c=c: q.matmul(
            p_bc[0][0][:, 0:128],
            s_ones1[:, :], s_cdrow[0:1, c * H:(c + 1) * H], start=True, stop=True),
            reads=[b_ones1, b_scdrow], writes=[p_bc[0][1]])
        B.op("vector", lambda q: q.tensor_tensor(
            out=s_prev[:, :].rearrange("p (h a) -> p h a", h=H),
            in0=s_prev[:, :].rearrange("p (h a) -> p h a", h=H),
            in1=p_bc[0][0][:, 0:128].unsqueeze(2).broadcast_to([128, H, PH]),
            op=ALU.mult), reads=[b_prev, p_bc[0][1]], writes=[b_prev])
        for hb in range(16):
            pst, bst = p_sc[hb % 2]
            for hh in range(8):
                h = hb * 8 + hh
                g = h // 16
                B.op("tensor", lambda q, pst=pst, g=g, h=h, hh=hh: q.matmul(
                    pst[:, hh * PH:(hh + 1) * PH], s_BT4[:, g * NST:(g + 1) * NST],
                    s_bigH[:, h * PH:(h + 1) * PH], start=True, stop=True),
                    reads=[b_sBT4, b_bigH], writes=[bst])
            B.op("vector", lambda q, pst=pst, hb=hb: q.tensor_tensor(
                out=s_prev[:, hb * 512:(hb + 1) * 512],
                in0=s_prev[:, hb * 512:(hb + 1) * 512],
                in1=pst[:, :], op=ALU.add), reads=[bst, b_prev], writes=[b_prev])
        if c < NCH - 1:
            B.op("vector", lambda q: q.tensor_copy(out=s_prevb[:, :], in_=s_prev[:, :]),
                 reads=[b_prev], writes=[b_prevb])
    B.flush(block)
    snap4 = B.snap()
    B.barrier(snap4)
    st4.close()
    st14.close()
    if PHASE_LIMIT <= 4:
        B.finalize(block)
        sem_stack.close()
        return nc

    st5 = contextlib.ExitStack()
    s_gall, b_gall = sb("gall", [128, 64 * T], BF16, st5)
    s_owb, b_owb = [], []
    for i in range(2):
        h_, bb = sb(f"owb{i}", [128, 64 * 128], BF16, st5)
        s_owb.append(h_)
        b_owb.append(bb)
    s_aev, b_aev = [], []
    for i in range(2):
        h_, bb = sb(f"aev{i}", [128, 512], F32, st5)
        s_aev.append(h_)
        b_aev.append(bb)
    p_op = [ps(f"pop{i}", st5) for i in range(3)]

    dma("sync", s_gall[:, :].rearrange("p (a t) -> p a t", a=64),
        d_g.rearrange("(a p) t -> p a t", p=128), reads=[b_dg], writes=[b_gall])
    for oc in range(32):
        owb, bowb = s_owb[oc % 2], b_owb[oc % 2]
        dma("gpsimd", owb[:, :].rearrange("p (a b) -> p a b", a=64),
            w_out[oc].rearrange("a p b -> p a b"),
            reads=[b_w_out], writes=[bowb], group="g_sw%d" % (oc % 2))
        for tt in range(2):
            pp, bpp = p_op[(oc * 2 + tt) % 3]
            for kt in range(64):
                B.op("tensor", lambda q, pp=pp, owb=owb, kt=kt, tt=tt: q.matmul(
                    pp[:, 0:512], owb[:, kt * 128:(kt + 1) * 128],
                    s_gall[:, kt * T + tt * 512:kt * T + (tt + 1) * 512],
                    start=(kt == 0), stop=(kt == 63)),
                    reads=[bowb, b_gall], writes=[bpp])
            ae, bae = s_aev[tt], b_aev[tt]
            B.op("vector", lambda q, pp=pp, ae=ae: q.tensor_copy(
                out=ae[:, :], in_=pp[:, 0:512]), reads=[bpp], writes=[bae])
            dma("scalar", d_attn[oc * 128:(oc + 1) * 128, tt * 512:(tt + 1) * 512],
                ae[:, :], reads=[bae], writes=[b_dattn], group="g_sp")
    B.flush(block)
    snap5 = B.snap()
    B.barrier(snap5)
    st5.close()
    if PHASE_LIMIT <= 5:
        B.finalize(block)
        sem_stack.close()
        return nc

    stHB = contextlib.ExitStack()
    s_hbf, b_shbf = sb("hbf", [128, 32 * T], BF16, stHB)
    st6 = contextlib.ExitStack()
    s_xtm, b_sxtm = sb("xtm", [128, D], F32, st6)
    s_arb, b_sarb, s_zb, b_szb, s_zq, b_szq, s_hfm, b_shfm = [], [], [], [], [], [], [], []
    s_mrow, b_smrow, s_qrow, b_sqrow, s_vrow, b_svrow = [], [], [], [], [], []
    for i in range(2):
        t_, bb = sb(f"arb{i}", [128, D], F32, st6)
        s_arb.append(t_); b_sarb.append(bb)
        t_, bb = sb(f"zbf{i}", [128, D], BF16, st6)
        s_zb.append(t_); b_szb.append(bb)
        t_, bb = sb(f"zsq{i}", [128, D], BF16, st6)
        s_zq.append(t_); b_szq.append(bb)
        t_, bb = sb(f"hfm{i}", [128, D], F32, st6)
        s_hfm.append(t_); b_shfm.append(bb)
        t_, bb = sb(f"mrow{i}", [1, 128], F32, st6)
        s_mrow.append(t_); b_smrow.append(bb)
        t_, bb = sb(f"qrow{i}", [1, 128], F32, st6)
        s_qrow.append(t_); b_sqrow.append(bb)
        t_, bb = sb(f"vrow{i}", [1, 128], F32, st6)
        s_vrow.append(t_); b_svrow.append(bb)
    p_tx = [ps("ptx0", st6), ps("ptx1", st6)]
    p_m = ps("pm", st6)
    p_q = ps("pq", st6)
    p_s1 = [ps("ps1a", st6), ps("ps1b", st6)]
    p_s2 = [ps("ps2a", st6), ps("ps2b", st6)]

    def ln_rows(i, pm, pq, ps1, ps2):
        B.op("vector", lambda q: q.tensor_copy(out=s_mrow[i][:, :], in_=pm[0][0:1, 0:128]),
             reads=[pm[1]], writes=[b_smrow[i]])
        B.op("vector", lambda q: q.tensor_copy(out=s_qrow[i][:, :], in_=pq[0][0:1, 0:128]),
             reads=[pq[1]], writes=[b_sqrow[i]])
        B.op("vector", lambda q: q.tensor_mul(out=s_vrow[i][:, :], in0=s_mrow[i][:, :],
                                              in1=s_mrow[i][:, :]),
             reads=[b_smrow[i]], writes=[b_svrow[i]])
        B.op("vector", lambda q: q.tensor_sub(out=s_vrow[i][:, :], in0=s_qrow[i][:, :],
                                              in1=s_vrow[i][:, :]),
             reads=[b_sqrow[i], b_svrow[i]], writes=[b_svrow[i]])
        B.op("scalar", lambda q: q.activation(out=s_vrow[i][:, :], in_=s_vrow[i][:, :],
                                              func=AF.Ln, bias=s_epsc[0:1, :]),
             reads=[b_svrow[i], b_sepsc], writes=[b_svrow[i]])
        B.op("scalar", lambda q: q.activation(out=s_vrow[i][:, :], in_=s_vrow[i][:, :],
                                              func=AF.Exp, scale=-0.5),
             reads=[b_svrow[i]], writes=[b_svrow[i]])
        B.op("vector", lambda q: q.tensor_mul(out=s_mrow[i][:, :], in0=s_mrow[i][:, :],
                                              in1=s_vrow[i][:, :]),
             reads=[b_smrow[i], b_svrow[i]], writes=[b_smrow[i]])
        B.op("tensor", lambda q: q.matmul(ps1[0][:, 0:128], s_ones1[:, :],
                                          s_vrow[i][0:1, :], start=True, stop=True),
             reads=[b_ones1, b_svrow[i]], writes=[ps1[1]])
        B.op("tensor", lambda q: q.matmul(ps2[0][:, 0:128], s_ones1[:, :],
                                          s_mrow[i][0:1, :], start=True, stop=True),
             reads=[b_ones1, b_smrow[i]], writes=[ps2[1]])

    for tt in range(8):
        i = tt % 2
        dma("sync", s_xtm[:, :], x_in[HALO + tt * 128:HALO + (tt + 1) * 128, :],
            reads=[b_x_in], writes=[b_sxtm])
        dma("sync", s_arb[i][:, :].rearrange("p (a t) -> p a t", a=32),
            d_attn.rearrange("(a p) t -> p a t", p=128)[:, :, tt * 128:(tt + 1) * 128],
            reads=[b_dattn], writes=[b_sarb[i]])
        for dc in range(32):
            ptx, bptx = p_tx[dc % 2]
            B.op("tensor", lambda q, ptx=ptx, dc=dc: q.transpose(
                ptx[:, 0:128], s_xtm[:, dc * 128:(dc + 1) * 128], s_ident[:, :]),
                reads=[b_sxtm, b_ident], writes=[bptx])
            B.op("vector", lambda q, ptx=ptx, dc=dc, i=i: q.tensor_tensor(
                out=s_arb[i][:, dc * 128:(dc + 1) * 128], in0=ptx[:, 0:128],
                in1=s_arb[i][:, dc * 128:(dc + 1) * 128], op=ALU.add),
                reads=[bptx, b_sarb[i]], writes=[b_sarb[i]])
        B.op("scalar", lambda q, i=i: q.activation(out=s_zb[i][:, :], in_=s_arb[i][:, :],
                                                   func=AF.Copy),
             reads=[b_sarb[i]], writes=[b_szb[i]])
        B.op("scalar", lambda q, i=i: q.activation(out=s_zq[i][:, :], in_=s_arb[i][:, :],
                                                   func=AF.Square),
             reads=[b_sarb[i]], writes=[b_szq[i]])
        for dc in range(32):
            B.op("tensor", lambda q, dc=dc, i=i: q.matmul(
                p_m[0][0:1, 0:128], s_invd[:, :], s_zb[i][:, dc * 128:(dc + 1) * 128],
                start=(dc == 0), stop=(dc == 31)), reads=[b_sinvd, b_szb[i]], writes=[p_m[1]])
        for dc in range(32):
            B.op("tensor", lambda q, dc=dc, i=i: q.matmul(
                p_q[0][0:1, 0:128], s_invd[:, :], s_zq[i][:, dc * 128:(dc + 1) * 128],
                start=(dc == 0), stop=(dc == 31)), reads=[b_sinvd, b_szq[i]], writes=[p_q[1]])
        ln_rows(i, p_m, p_q, p_s1[i], p_s2[i])
        for dc in range(32):
            B.op("vector", lambda q, dc=dc, i=i: q.tensor_tensor(
                out=s_hfm[i][:, dc * 128:(dc + 1) * 128],
                in0=s_arb[i][:, dc * 128:(dc + 1) * 128],
                in1=p_s1[i][0][:, 0:128], op=ALU.mult),
                reads=[b_sarb[i], p_s1[i][1]], writes=[b_shfm[i]])
        B.op("vector", lambda q, i=i: q.tensor_tensor(
            out=s_hfm[i][:, :].rearrange("p (a t) -> p a t", a=32),
            in0=s_hfm[i][:, :].rearrange("p (a t) -> p a t", a=32),
            in1=p_s2[i][0][:, 0:128].unsqueeze(1).broadcast_to([128, 32, 128]),
            op=ALU.subtract), reads=[b_shfm[i], p_s2[i][1]], writes=[b_shfm[i]])
        dma("scalar", d_h.rearrange("(a p) t -> p a t", p=128)[:, :, tt * 128:(tt + 1) * 128],
            s_hfm[i][:, :].rearrange("p (a t) -> p a t", a=32),
            reads=[b_shfm[i]], writes=[b_dh], group="g_sp")
        B.op("scalar", lambda q, tt=tt, i=i: q.activation(
            out=s_hbf[:, :].rearrange("p (a t) -> p a t", a=32)[:, :, tt * 128:(tt + 1) * 128],
            in_=s_hfm[i][:, :].rearrange("p (a t) -> p a t", a=32), func=AF.Copy),
            reads=[b_shfm[i]], writes=[b_shbf])
    B.flush(block)
    snap6 = B.snap()
    B.barrier(snap6)
    st6.close()
    if PHASE_LIMIT <= 6:
        B.finalize(block)
        sem_stack.close()
        return nc

    st7 = contextlib.ExitStack()
    s_wbb, b_wbb = [], []
    for i in range(2):
        h_, bb = sb(f"wbb{i}", [128, 32 * 128], BF16, st7)
        s_wbb.append(h_)
        b_wbb.append(bb)
    s_hrb, b_shrb = [], []
    for i in range(2):
        h_, bb = sb(f"hrb{i}", [128, 512], F32, st7)
        s_hrb.append(h_)
        b_shrb.append(bb)
    s_z2e, b_sz2e = [], []
    for i in range(2):
        h_, bb = sb(f"z2e{i}", [128, 512], F32, st7)
        s_z2e.append(h_)
        b_sz2e.append(bb)
    p_ff = [ps(f"pff{i}", st7) for i in range(3)]
    p_e = ps("pe", st7)

    for tt in range(2):
        for kt in range(32):
            B.op("tensor", lambda q, kt=kt, tt=tt: q.matmul(
                p_e[0][0:40, 0:512], s_enc[:, kt * 40:(kt + 1) * 40],
                s_hbf[:, kt * T + tt * 512:kt * T + (tt + 1) * 512],
                start=(kt == 0), stop=(kt == 31)),
                reads=[b_senc, b_shbf], writes=[p_e[1]])
        B.op("vector", lambda q, tt=tt: q.tensor_copy(
            out=s_e01[0:40, tt * 512:(tt + 1) * 512], in_=p_e[0][0:40, 0:512]),
            reads=[p_e[1]], writes=[b_se01])
    for oc in range(32):
        wbb, bwbb = s_wbb[oc % 2], b_wbb[oc % 2]
        dma("gpsimd", wbb[:, :].rearrange("p (a b) -> p a b", a=32),
            w_baseT[oc].rearrange("a p b -> p a b"),
            reads=[b_w_base], writes=[bwbb], group="g_sw%d" % (oc % 2))
        for tt in range(2):
            pp, bpp = p_ff[(oc * 2 + tt) % 3]
            for kt in range(32):
                B.op("tensor", lambda q, pp=pp, wbb=wbb, kt=kt, tt=tt: q.matmul(
                    pp[:, 0:512], wbb[:, kt * 128:(kt + 1) * 128],
                    s_hbf[:, kt * T + tt * 512:kt * T + (tt + 1) * 512],
                    start=(kt == 0), stop=False),
                    reads=[bwbb, b_shbf], writes=[bpp])
            B.op("tensor", lambda q, pp=pp, oc=oc, tt=tt: q.matmul(
                pp[:, 0:512], s_fwt[:, oc * 128:(oc + 1) * 128],
                s_e01[0:40, tt * 512:(tt + 1) * 512], start=False, stop=True),
                reads=[b_sfwt, b_se01], writes=[bpp])
            hr, bhr = s_hrb[tt], b_shrb[tt]
            dma("sync", hr[:, :], d_h[oc * 128:(oc + 1) * 128, tt * 512:(tt + 1) * 512],
                reads=[b_dh], writes=[bhr])
            ze, bze = s_z2e[tt], b_sz2e[tt]
            B.op("vector", lambda q, pp=pp, hr=hr, ze=ze, oc=oc: q.scalar_tensor_tensor(
                out=ze[:, :], in0=pp[:, 0:512], scalar=s_bbc[:, oc:oc + 1], in1=hr[:, :],
                op0=ALU.add, op1=ALU.add), reads=[bpp, b_sbbc, bhr], writes=[bze])
            dma("scalar", d_z2[oc * 128:(oc + 1) * 128, tt * 512:(tt + 1) * 512],
                ze[:, :], reads=[bze], writes=[b_dz2], group="g_sp")
    B.flush(block)
    snap7 = B.snap()
    B.barrier(snap7)
    st7.close()
    stHB.close()
    if PHASE_LIMIT <= 7:
        B.finalize(block)
        sem_stack.close()
        return nc

    st8 = contextlib.ExitStack()
    s_z2, b_sz2, s_zb2, b_szb2, s_zq2, b_szq2, s_otm, b_sotm = [], [], [], [], [], [], [], []
    s_mr2, b_smr2, s_qr2, b_sqr2, s_vr2, b_svr2 = [], [], [], [], [], []
    for i in range(2):
        t_, bb = sb(f"z2fm{i}", [128, D], F32, st8)
        s_z2.append(t_); b_sz2.append(bb)
        t_, bb = sb(f"zbf2{i}", [128, D], BF16, st8)
        s_zb2.append(t_); b_szb2.append(bb)
        t_, bb = sb(f"zsq2{i}", [128, D], BF16, st8)
        s_zq2.append(t_); b_szq2.append(bb)
        t_, bb = sb(f"otm{i}", [128, D], F32, st8)
        s_otm.append(t_); b_sotm.append(bb)
        t_, bb = sb(f"mr2{i}", [1, 128], F32, st8)
        s_mr2.append(t_); b_smr2.append(bb)
        t_, bb = sb(f"qr2{i}", [1, 128], F32, st8)
        s_qr2.append(t_); b_sqr2.append(bb)
        t_, bb = sb(f"vr2{i}", [1, 128], F32, st8)
        s_vr2.append(t_); b_svr2.append(bb)
    p_tx2 = [ps("ptx20", st8), ps("ptx21", st8)]
    p_m2 = ps("pm2", st8)
    p_q2 = ps("pq2", st8)
    p_s12 = [ps("ps12a", st8), ps("ps12b", st8)]
    p_s22 = [ps("ps22a", st8), ps("ps22b", st8)]

    for tt in range(8):
        i = tt % 2
        dma("sync", s_z2[i][:, :].rearrange("p (a t) -> p a t", a=32),
            d_z2.rearrange("(a p) t -> p a t", p=128)[:, :, tt * 128:(tt + 1) * 128],
            reads=[b_dz2], writes=[b_sz2[i]])
        B.op("scalar", lambda q, i=i: q.activation(out=s_zb2[i][:, :], in_=s_z2[i][:, :],
                                                   func=AF.Copy),
             reads=[b_sz2[i]], writes=[b_szb2[i]])
        B.op("scalar", lambda q, i=i: q.activation(out=s_zq2[i][:, :], in_=s_z2[i][:, :],
                                                   func=AF.Square),
             reads=[b_sz2[i]], writes=[b_szq2[i]])
        for dc in range(32):
            B.op("tensor", lambda q, dc=dc, i=i: q.matmul(
                p_m2[0][0:1, 0:128], s_invd[:, :], s_zb2[i][:, dc * 128:(dc + 1) * 128],
                start=(dc == 0), stop=(dc == 31)), reads=[b_sinvd, b_szb2[i]],
                writes=[p_m2[1]])
        for dc in range(32):
            B.op("tensor", lambda q, dc=dc, i=i: q.matmul(
                p_q2[0][0:1, 0:128], s_invd[:, :], s_zq2[i][:, dc * 128:(dc + 1) * 128],
                start=(dc == 0), stop=(dc == 31)), reads=[b_sinvd, b_szq2[i]],
                writes=[p_q2[1]])
        B.op("vector", lambda q, i=i: q.tensor_copy(out=s_mr2[i][:, :], in_=p_m2[0][0:1, 0:128]),
             reads=[p_m2[1]], writes=[b_smr2[i]])
        B.op("vector", lambda q, i=i: q.tensor_copy(out=s_qr2[i][:, :], in_=p_q2[0][0:1, 0:128]),
             reads=[p_q2[1]], writes=[b_sqr2[i]])
        B.op("vector", lambda q, i=i: q.tensor_mul(out=s_vr2[i][:, :], in0=s_mr2[i][:, :],
                                                   in1=s_mr2[i][:, :]),
             reads=[b_smr2[i]], writes=[b_svr2[i]])
        B.op("vector", lambda q, i=i: q.tensor_sub(out=s_vr2[i][:, :], in0=s_qr2[i][:, :],
                                                   in1=s_vr2[i][:, :]),
             reads=[b_sqr2[i], b_svr2[i]], writes=[b_svr2[i]])
        B.op("scalar", lambda q, i=i: q.activation(out=s_vr2[i][:, :], in_=s_vr2[i][:, :],
                                                   func=AF.Ln, bias=s_epsc[0:1, :]),
             reads=[b_svr2[i], b_sepsc], writes=[b_svr2[i]])
        B.op("scalar", lambda q, i=i: q.activation(out=s_vr2[i][:, :], in_=s_vr2[i][:, :],
                                                   func=AF.Exp, scale=-0.5),
             reads=[b_svr2[i]], writes=[b_svr2[i]])
        B.op("vector", lambda q, i=i: q.tensor_mul(out=s_mr2[i][:, :], in0=s_mr2[i][:, :],
                                                   in1=s_vr2[i][:, :]),
             reads=[b_smr2[i], b_svr2[i]], writes=[b_smr2[i]])
        B.op("tensor", lambda q, i=i: q.matmul(p_s12[i][0][:, 0:128], s_ones1[:, :],
                                               s_vr2[i][0:1, :], start=True, stop=True),
             reads=[b_ones1, b_svr2[i]], writes=[p_s12[i][1]])
        B.op("tensor", lambda q, i=i: q.matmul(p_s22[i][0][:, 0:128], s_ones1[:, :],
                                               s_mr2[i][0:1, :], start=True, stop=True),
             reads=[b_ones1, b_smr2[i]], writes=[p_s22[i][1]])
        for dc in range(32):
            B.op("vector", lambda q, dc=dc, i=i: q.tensor_tensor(
                out=s_z2[i][:, dc * 128:(dc + 1) * 128],
                in0=s_z2[i][:, dc * 128:(dc + 1) * 128],
                in1=p_s12[i][0][:, 0:128], op=ALU.mult),
                reads=[b_sz2[i], p_s12[i][1]], writes=[b_sz2[i]])
        B.op("vector", lambda q, i=i: q.tensor_tensor(
            out=s_z2[i][:, :].rearrange("p (a t) -> p a t", a=32),
            in0=s_z2[i][:, :].rearrange("p (a t) -> p a t", a=32),
            in1=p_s22[i][0][:, 0:128].unsqueeze(1).broadcast_to([128, 32, 128]),
            op=ALU.subtract), reads=[b_sz2[i], p_s22[i][1]], writes=[b_sz2[i]])
        for dc in range(32):
            ptx, bptx = p_tx2[dc % 2]
            B.op("tensor", lambda q, ptx=ptx, dc=dc, i=i: q.transpose(
                ptx[:, 0:128], s_z2[i][:, dc * 128:(dc + 1) * 128], s_ident[:, :]),
                reads=[b_sz2[i], b_ident], writes=[bptx])
            B.op("vector", lambda q, ptx=ptx, dc=dc, i=i: q.tensor_copy(
                out=s_otm[i][:, dc * 128:(dc + 1) * 128], in_=ptx[:, 0:128]),
                reads=[bptx], writes=[b_sotm[i]])
        dma("sync", out_ext[tt * 128:(tt + 1) * 128, :], s_otm[i][:, :],
            reads=[b_sotm[i]], writes=[b_out], group="g_sp")
    if KDEBUG:
        for nm, srcap, bsrc in [("dbg_x", d_x, b_dx), ("dbg_sg", d_sg, b_dsg),
                                ("dbg_b", d_b, b_db), ("dbg_attn", d_attn, b_dattn),
                                ("dbg_h", d_h, b_dh), ("dbg_cum", d_cum, b_dcum),
                                ("dbg_g", d_g, b_dg), ("dbg_z2", d_z2, b_dz2),
                                ("dbg_agb", d_agbo, b_agbo), ("dbg_agd", d_agdo, b_agdo)]:
            dma("sync", dbg[nm][0], srcap, reads=[bsrc], writes=[dbg[nm][1]], group="g_sp")
    B.op("gpsimd", lambda q: q.nop(), reads=[b_out], writes=[])
    B.flush(block)
    st8.close()

    B.finalize(block)
    sem_stack.close()
    stack_all.close()
    return nc


_NC_CACHE = None


def kernel(**inputs):
    global _NC_CACHE
    sh, per_core = _host_prep(inputs)
    if _NC_CACHE is None:
        _NC_CACHE = build_program()
    nc = _NC_CACHE
    in_maps = []
    for k in range(NCORES):
        m = dict(sh)
        m.update(per_core[k])
        in_maps.append(m)
    res = run_bass_kernel_spmd(nc, in_maps, core_ids=list(range(NCORES)))
    out = np.concatenate([np.asarray(res.results[k]["out"]) for k in range(NCORES)], axis=0)
    return out.reshape(1, L, D).astype(np.float32)


# version 13
# speedup vs baseline: 1.2127x; 1.1424x over previous
import contextlib
import math
import os
import numpy as np
import ml_dtypes
import concourse.bass as bass
import concourse.mybir as mybir
from concourse.bass_utils import run_bass_kernel_spmd

F32 = mybir.dt.float32
BF16 = mybir.dt.bfloat16
U8 = mybir.dt.uint8
AF = mybir.ActivationFunctionType
ALU = mybir.AluOpType
AX = mybir.AxisListType

NCORES = 8
D = 4096
L = 8192
T = 1024
HALO = 3
TH = T + HALO
THP = 1040
CS = 128
NCH = T // CS
H = 128
PH = 64
NST = 128
G = 8
INTER = H * PH
CONV = INTER + 2 * G * NST
PROJ = INTER + CONV + H
GS = INTER // G
EPS = 1e-5
SCALING = 0.5

NOC = PROJ // 128
OC_GATE0, OC_X0, OC_B0, OC_C0, OC_DT = 0, 64, 128, 136, 144
SPANS_XBC = [(0, 512), (512, 1024), (1024, 1027)]
SPANS_GD = [(3, 515), (515, 1027)]

ENGS = ("sync", "scalar", "vector", "tensor", "gpsimd")
GROUPS = ("g_const", "g_w0", "g_w1", "g_sp", "g_rd", "g_cc",
          "g_xc", "g_sw0", "g_sw1", "g_spg", "g_rdg")
SELF_SYNC = {"sync": False, "scalar": False, "vector": False, "tensor": False,
             "gpsimd": False}


class Buf:
    __slots__ = ("name", "ready", "readers")

    def __init__(self, name):
        self.name = name
        self.ready = None
        self.readers = []


class EngRec:
    __slots__ = ("name", "cum", "ops", "last_wait", "sem")

    def __init__(self, name):
        self.name = name
        self.cum = 0
        self.ops = []
        self.last_wait = {}
        self.sem = None


class Builder:
    def __init__(self, nc):
        self.nc = nc
        self.engs = {n: EngRec(n) for n in ENGS}
        self.gcum = {g: 0 for g in GROUPS}
        self.gsem = {}

    def buf(self, name):
        return Buf(name)

    def op(self, eng, closure, reads=(), writes=(), inc=1, group=None):
        e = self.engs[eng]
        waits = {}

        def need(tok):
            if tok is None:
                return
            s, v = tok
            if s in self.gcum:
                v = self.gcum[s]
                if v == 0:
                    return
            elif s == eng and not SELF_SYNC[eng]:
                return
            if e.last_wait.get(s, 0) >= v:
                return
            if waits.get(s, 0) < v:
                waits[s] = v

        for b in reads:
            need(b.ready)
        for b in writes:
            need(b.ready)
            for t in b.readers:
                need(t)
        for s, v in waits.items():
            e.last_wait[s] = v
        if group is not None:
            self.gcum[group] += inc
            tok = (group, self.gcum[group])
            e.ops.append((waits, closure, (group, inc)))
            e.cum += 0
        else:
            tok = (eng, e.cum + inc)
            e.ops.append((waits, closure, inc))
            e.cum += inc
        for b in reads:
            b.readers.append(tok)
        for b in writes:
            b.ready = tok
            b.readers = []
        return tok

    def snap(self):
        d = {n: self.engs[n].cum for n in ENGS}
        d.update({g: self.gcum[g] for g in GROUPS})
        return d

    def barrier(self, snap=None):
        if snap is None:
            snap = self.snap()
        for n in ENGS:
            e = self.engs[n]
            waits = {}
            for m, v in snap.items():
                if m == n or v == 0:
                    continue
                if e.last_wait.get(m, 0) >= v:
                    continue
                waits[m] = v
                e.last_wait[m] = v
            e.ops.append((waits, lambda q: q.nop(), 1))
            e.cum += 1

    def flush(self, block):
        return

    def finalize(self, block):
        waited = {n: set() for n in ENGS}
        for n in ENGS:
            for w, _, _ in self.engs[n].ops:
                for s, v in w.items():
                    if s in waited:
                        waited[s].add(v)
        CHUNK = 3000
        for n in ENGS:
            e = self.engs[n]
            if not e.ops:
                continue
            ops = e.ops
            e.ops = []
            sems = {m: self.engs[m].sem for m in ENGS}
            sems.update(self.gsem)
            mysem = e.sem
            wset = waited[n]
            nops = len(ops)
            state = {"pending": 0, "cum": 0}
            for lo in range(0, nops, CHUNK):
                hi = min(lo + CHUNK, nops)

                def body(q, lo=lo, hi=hi, ops=ops, sems=sems, mysem=mysem,
                         wset=wset, nops=nops, state=state, n=n):
                    for i in range(lo, hi):
                        waits, closure, inc = ops[i]
                        for s, v in sorted(waits.items()):
                            q.wait_ge(sems[s], v)
                        ins = closure(q)
                        if isinstance(inc, tuple):
                            ins.then_inc(sems[inc[0]], inc[1])
                            continue
                        state["pending"] += inc
                        state["cum"] += inc
                        dense = n != "tensor"
                        if (dense or state["cum"] in wset or i == nops - 1
                                or i == hi - 1):
                            ins.then_inc(mysem, state["pending"])
                            state["pending"] = 0

                getattr(block, n)(body)


def _build_FT_bf16():
    n = D
    t = np.arange(n)[:, None].astype(np.float64)
    F = np.zeros((n, n), np.float64)
    F[:, 0] = 1.0 / math.sqrt(n)
    maxf = (n - 1) // 2
    f = np.arange(1, maxf + 1)[None, :].astype(np.float64)
    ang = 2.0 * np.pi * f * t / n
    F[:, 1:1 + 2 * maxf:2] = math.sqrt(2.0 / n) * np.cos(ang)
    F[:, 2:2 + 2 * maxf:2] = math.sqrt(2.0 / n) * np.sin(ang)
    F[:, n - 1] = ((-1.0) ** np.arange(n)) / math.sqrt(n)
    return np.ascontiguousarray(F.T).astype(ml_dtypes.bfloat16)


_FT_CACHE = None


def _host_prep(inputs):
    global _FT_CACHE
    x = np.asarray(inputs["L0_attn_input_states"], np.float32).reshape(L, D)
    w_in = np.asarray(inputs["L0_attn_in_proj_w"], np.float32)
    conv_w = np.asarray(inputs["L0_attn_conv_w"], np.float32)
    conv_b = np.asarray(inputs["L0_attn_conv_b"], np.float32)
    dt_bias = np.asarray(inputs["L0_attn_dt_bias"], np.float32)
    A_log = np.asarray(inputs["L0_attn_A_log"], np.float32)
    Dp = np.asarray(inputs["L0_attn_D"], np.float32)
    norm_w = np.asarray(inputs["L0_attn_norm_w"], np.float32)
    out_w = np.asarray(inputs["L0_attn_out_w"], np.float32)
    w_base = np.asarray(inputs["L0_ffn_W_base"], np.float32)
    b_base = np.asarray(inputs["L0_ffn_b_base"], np.float32)
    enc0 = np.asarray(inputs["L0_ffn_enc0"], np.float32)
    enc1 = np.asarray(inputs["L0_ffn_enc1"], np.float32)
    c0 = np.asarray(inputs["L0_ffn_coefs0"], np.float32)
    c1 = np.asarray(inputs["L0_ffn_coefs1"], np.float32)
    m0 = np.asarray(inputs["L0_ffn_mask0"])
    m1 = np.asarray(inputs["L0_ffn_mask1"])

    sh = {}
    wiT = np.ascontiguousarray(w_in.T)
    sh["w_in"] = np.ascontiguousarray(wiT.reshape(32, 128, NOC, 128).transpose(2, 0, 1, 3))
    woT = np.ascontiguousarray(out_w.T)
    sh["w_out"] = np.ascontiguousarray(woT.reshape(64, 128, 32, 128).transpose(2, 0, 1, 3))
    wbT = np.ascontiguousarray(w_base.T)
    sh["w_base"] = np.ascontiguousarray(wbT.reshape(32, 128, 32, 128).transpose(2, 0, 1, 3))
    if _FT_CACHE is None:
        _FT_CACHE = _build_FT_bf16()
    sh["ft"] = _FT_CACHE
    sh["enc"] = np.ascontiguousarray(np.concatenate([enc0, enc1], axis=0).T)
    sh["c01"] = np.ascontiguousarray(np.concatenate(
        [c0.transpose(1, 2, 0).reshape(D, 96),
         c1.transpose(1, 2, 0).reshape(D, 24)], axis=1))
    sh["m01"] = np.ascontiguousarray(np.concatenate(
        [m0.transpose(1, 2, 0).reshape(D, 96),
         m1.transpose(1, 2, 0).reshape(D, 24)], axis=1)).view(np.uint8)
    sh["cw"] = np.ascontiguousarray(conv_w.reshape(80, 128, 4).transpose(1, 0, 2).reshape(128, 320))
    sh["cb"] = np.ascontiguousarray(conv_b.reshape(80, 128).T)
    sh["dtb"] = np.ascontiguousarray(dt_bias.reshape(128, 1))
    sh["alog"] = np.ascontiguousarray(A_log.reshape(128, 1))
    dexp = np.empty((128, 64), np.float32)
    for oc in range(64):
        dexp[0:64, oc] = Dp[2 * oc]
        dexp[64:128, oc] = Dp[2 * oc + 1]
    sh["dexp"] = dexp
    sh["nwc"] = np.ascontiguousarray(norm_w.reshape(64, 128).T)
    sh["bbc"] = np.ascontiguousarray(b_base.reshape(32, 128).T)
    sh["ident"] = np.eye(128, dtype=np.float32)
    sh["ones1"] = np.ones((1, 128), np.float32)
    sh["onest"] = np.ones((128, CS), np.float32)
    sh["tril"] = np.ascontiguousarray(
        (np.arange(CS)[None, :] >= np.arange(128)[:, None]).astype(np.float32))
    sh["invgs"] = np.full((128, 1), 1.0 / GS, np.float32).astype(ml_dtypes.bfloat16)
    sh["invd"] = np.full((128, 1), 1.0 / D, np.float32).astype(ml_dtypes.bfloat16)
    sh["epsc"] = np.full((128, 1), EPS, np.float32)
    sh["onec"] = np.ones((128, 1), np.float32)

    per_core = []
    for k in range(NCORES):
        d = {}
        s = k * T
        if k == 0:
            xs = np.concatenate([np.zeros((HALO, D), np.float32), x[0:T]], axis=0)
        else:
            xs = x[s - HALO:s + T]
        d["x"] = np.ascontiguousarray(xs)
        pmt = np.zeros((8, 8), np.float32)
        for m in range(8):
            for j in range(8):
                if j < m < k:
                    pmt[m, j] = 1.0
        d["pmt"] = pmt
        zm = np.zeros((128, 8), np.float32)
        zm[:, 0:k] = 1.0
        d["zmask"] = zm
        per_core.append(d)
    return sh, per_core


PHASE_LIMIT = int(os.environ.get("KPHASES", "99"))
P4SUB = int(os.environ.get("KP4SUB", "99"))
P4OPS = int(os.environ.get("KP4OPS", "99"))
P4Y = int(os.environ.get("KP4Y", "2"))


def build_program():
    nc = bass.Bass("TRN2", target_bir_lowering=False, debug=False)
    B = Builder(nc)

    def ein(name, shape, dt):
        return nc.dram_tensor(name, shape, dt, kind="ExternalInput").ap(), B.buf("d:" + name)

    x_in, b_x_in = ein("x", [TH, D], F32)
    w_in, b_w_in = ein("w_in", [NOC, 32, 128, 128], F32)
    w_out, b_w_out = ein("w_out", [32, 64, 128, 128], F32)
    w_baseT, b_w_base = ein("w_base", [32, 32, 128, 128], F32)
    ft_in, b_ft = ein("ft", [D, D], BF16)
    enc_in, b_enc = ein("enc", [D, 40], F32)
    c01_in, b_c01 = ein("c01", [D, 120], F32)
    m01_in, b_m01 = ein("m01", [D, 120], U8)
    cw_in, b_cw = ein("cw", [128, 320], F32)
    cb_in, b_cb = ein("cb", [128, 80], F32)
    dtb_in, b_dtb = ein("dtb", [128, 1], F32)
    alog_in, b_alog = ein("alog", [128, 1], F32)
    dexp_in, b_dexp = ein("dexp", [128, 64], F32)
    nwc_in, b_nwc = ein("nwc", [128, 64], F32)
    bbc_in, b_bbc = ein("bbc", [128, 32], F32)
    ident_in, b_ident_in = ein("ident", [128, 128], F32)
    ones1_in, b_ones1_in = ein("ones1", [1, 128], F32)
    onest_in, b_onest_in = ein("onest", [128, CS], F32)
    tril_in, b_tril_in = ein("tril", [128, CS], F32)
    invgs_in, b_invgs_in = ein("invgs", [128, 1], BF16)
    invd_in, b_invd_in = ein("invd", [128, 1], BF16)
    epsc_in, b_epsc_in = ein("epsc", [128, 1], F32)
    onec_in, b_onec_in = ein("onec", [128, 1], F32)
    pmt_in, b_pmt_in = ein("pmt", [8, 8], F32)
    zmask_in, b_zmask_in = ein("zmask", [128, 8], F32)

    out_ext = nc.dram_tensor("out", [T, D], F32, kind="ExternalOutput").ap()
    b_out = B.buf("d:out")
    KDEBUG = os.environ.get("KDEBUG", "") != ""
    dbg = {}
    if KDEBUG:
        for nm, shape, dt in [("dbg_x", [INTER, T], BF16), ("dbg_sg", [INTER, T], BF16),
                              ("dbg_b", [G * NST, T], BF16), ("dbg_attn", [D, T], F32),
                              ("dbg_h", [D, T], F32), ("dbg_cum", [NCH, H, CS], F32),
                              ("dbg_g", [INTER, T], BF16), ("dbg_z2", [D, T], F32),
                              ("dbg_agb", [NCORES * NST, INTER], BF16),
                              ("dbg_agd", [NCORES, H], F32)]:
            dbg[nm] = (nc.dram_tensor(nm, shape, dt, kind="ExternalOutput").ap(),
                       B.buf("d:" + nm))

    def idram(name, shape, dt):
        return nc.dram_tensor(name, shape, dt).ap(), B.buf("i:" + name)

    d_xbf, b_xbf = idram("d_xbf", [THP, D], BF16)
    d_x, b_dx = idram("d_x", [INTER, T], BF16)
    d_b, b_db = idram("d_b", [G * NST, T], BF16)
    d_sg, b_dsg = idram("d_sg", [INTER, T], BF16)
    d_g, b_dg = idram("d_g", [INTER, T], BF16)
    d_attn, b_dattn = idram("d_attn", [D, T], F32)
    d_h, b_dh = idram("d_h", [D, T], F32)
    d_z2, b_dz2 = idram("d_z2", [D, T], F32)
    d_cum, b_dcum = idram("d_cum", [NCH, H, CS], F32)
    d_agbi, b_agbi = idram("d_agbi", [NST, INTER], BF16)
    d_agbo, b_agbo = idram("d_agbo", [NCORES * NST, INTER], BF16)
    d_agdi, b_agdi = idram("d_agdi", [1, H], F32)
    d_agdo, b_agdo = idram("d_agdo", [NCORES, H], F32)

    stack_all = contextlib.ExitStack()

    def sb(name, shape, dt, st=None):
        hdl = (st or stack_all).enter_context(nc.sbuf_tensor("sb_" + name, shape, dt))
        return hdl, B.buf("s:" + name)

    def ps(name, st=None):
        hdl = (st or stack_all).enter_context(nc.psum_tensor("ps_" + name, [128, 512], F32))
        return hdl, B.buf("p:" + name)

    s_ident, b_ident = sb("ident", [128, 128], F32)
    s_ones1, b_ones1 = sb("ones1", [1, 128], F32)
    s_onest, b_onest = sb("onest", [128, CS], F32)
    s_tril, b_tril = sb("tril", [128, CS], F32)
    s_cw, b_scw = sb("cw", [128, 320], F32)
    s_cb, b_scb = sb("cb", [128, 80], F32)
    s_dtb, b_sdtb = sb("dtb", [128, 1], F32)
    s_alog, b_salog = sb("alog", [128, 1], F32)
    s_dexp, b_sdexp = sb("dexp", [128, 64], F32)
    s_nwc, b_snwc = sb("nwc", [128, 64], F32)
    s_bbc, b_sbbc = sb("bbc", [128, 32], F32)
    s_invgs, b_sinvgs = sb("invgs", [128, 1], BF16)
    s_invd, b_sinvd = sb("invd", [128, 1], BF16)
    s_epsc, b_sepsc = sb("epsc", [128, 1], F32)
    s_onec, b_sonec = sb("onec", [128, 1], F32)
    s_pmt, b_spmt = sb("pmt", [8, 8], F32)
    s_zmask, b_szmask = sb("zmask", [128, 8], F32)
    s_negA, b_negA = sb("negA", [128, 1], F32)
    s_row, b_srow = sb("rowtmp", [1, 128], F32)
    s_enc, b_senc = sb("encbf", [128, 32 * 40], BF16)
    s_fwt, b_sfwt = sb("fwt", [40, D], BF16)
    s_e01, b_se01 = sb("e01", [40, T], BF16)

    sem_stack = contextlib.ExitStack()
    for n in ENGS:
        B.engs[n].sem = sem_stack.enter_context(nc.semaphore("sem_" + n))
    for g in GROUPS:
        B.gsem[g] = sem_stack.enter_context(nc.semaphore("sem_" + g))
    block = sem_stack.enter_context(nc.Block())

    def dma(eng, out_ap, in_ap, reads, writes, group="g_rd", **kw):
        return B.op(eng, lambda q, o=out_ap, i=in_ap, kw=kw: q.dma_start(out=o, in_=i, **kw),
                    reads=reads, writes=writes, inc=16, group=group)

    st14 = contextlib.ExitStack()
    s_dt, b_sdt = sb("dtraw", [128, TH], F32, st14)
    s_cumT, b_scumT = sb("cumT", [128, NCH * CS], F32, st14)
    s_ncumT, b_sncumT = sb("ncumT", [128, NCH * CS], F32, st14)
    s_dtpT, b_sdtpT = sb("dtpT", [128, NCH * CS], BF16, st14)
    s_dlastT, b_sdlastT = sb("dlastT", [128, NCH * CS], BF16, st14)
    s_fT, b_sfT = sb("fT", [128, CS], BF16, st14)
    s_cdrow, b_scdrow = sb("cdrow", [1, NCH * H], F32, st14)
    s_B, b_sB = sb("Bfm", [128, G * T], BF16, st14)
    s_C, b_sC = sb("Cfm", [128, G * T], BF16, st14)
    st1 = contextlib.ExitStack()
    s_xfm, b_xfm = [], []
    for kt in range(32):
        h_, bb = sb(f"xfm{kt}", [128, THP], BF16, st1)
        s_xfm.append(h_)
        b_xfm.append(bb)
    s_wbuf, b_wbuf = [], []
    for i in range(2):
        h_, bb = sb(f"win{i}", [128, 32 * 128], BF16, st1)
        s_wbuf.append(h_)
        b_wbuf.append(bb)
    p_ip = [ps(f"pip{i}", st1) for i in range(3)]
    s_pre, b_pre = [], []
    s_cvt, b_cvt = [], []
    s_evo, b_evo = [], []
    for i in range(2):
        h_, bb = sb(f"pre{i}", [128, TH], BF16, st1)
        s_pre.append(h_)
        b_pre.append(bb)
        h_, bb = sb(f"cvt{i}", [128, T], BF16, st1)
        s_cvt.append(h_)
        b_cvt.append(bb)
        h_, bb = sb(f"evo{i}", [128, TH], BF16, st1)
        s_evo.append(h_)
        b_evo.append(bb)

    for s_t, d_ap, b_s, b_d in [
        (s_ident, ident_in, b_ident, b_ident_in), (s_ones1, ones1_in, b_ones1, b_ones1_in),
        (s_onest, onest_in, b_onest, b_onest_in), (s_tril, tril_in, b_tril, b_tril_in),
        (s_cw, cw_in, b_scw, b_cw), (s_cb, cb_in, b_scb, b_cb),
        (s_dtb, dtb_in, b_sdtb, b_dtb), (s_alog, alog_in, b_salog, b_alog),
        (s_dexp, dexp_in, b_sdexp, b_dexp), (s_nwc, nwc_in, b_snwc, b_nwc),
        (s_bbc, bbc_in, b_sbbc, b_bbc), (s_invgs, invgs_in, b_sinvgs, b_invgs_in),
        (s_invd, invd_in, b_sinvd, b_invd_in), (s_epsc, epsc_in, b_sepsc, b_epsc_in),
        (s_onec, onec_in, b_sonec, b_onec_in), (s_pmt, pmt_in, b_spmt, b_pmt_in),
        (s_zmask, zmask_in, b_szmask, b_zmask_in),
    ]:
        dma("sync", s_t[:, :], d_ap, reads=[b_d], writes=[b_s], group="g_const")

    B.op("scalar", lambda q: q.activation(out=s_negA[:, :], in_=s_alog[:, :], func=AF.Exp),
         reads=[b_salog], writes=[b_negA])
    B.op("vector", lambda q: q.tensor_scalar(out=s_negA[:, :], in0=s_negA[:, :],
                                             scalar1=-1.0, scalar2=None, op0=ALU.mult),
         reads=[b_negA], writes=[b_negA])

    dma("gpsimd", d_xbf[0:TH, :], x_in, reads=[b_x_in], writes=[b_xbf], group="g_xc")
    for kt in range(32):
        dma("sync", s_xfm[kt][:, :], d_xbf[:, kt * 128:(kt + 1) * 128],
            reads=[b_xbf], writes=[b_xfm[kt]], transpose=True, group="g_const")

    for oc in range(NOC):
        wb, bwb = s_wbuf[oc % 2], b_wbuf[oc % 2]
        dma("gpsimd", wb[:, :].rearrange("p (a b) -> p a b", a=32),
            w_in[oc].rearrange("a p b -> p a b"),
            reads=[b_w_in], writes=[bwb], group="g_sw%d" % (oc % 2))
        par = oc % 2
        spans = SPANS_XBC if (OC_X0 <= oc < OC_DT) else SPANS_GD
        for ti, (t0, t1) in enumerate(spans):
            pp, bpp = p_ip[(oc * 3 + ti) % 3]
            n = t1 - t0
            for k in range(32):
                B.op("tensor",
                     lambda q, pp=pp, wb=wb, k=k, t0=t0, t1=t1: q.matmul(
                         pp[:, 0:(t1 - t0)], wb[:, k * 128:(k + 1) * 128],
                         s_xfm[k][:, t0:t1], start=(k == 0), stop=(k == 31)),
                     reads=[bwb, b_xfm[k]], writes=[bpp])
            if oc < OC_X0:
                B.op("scalar", lambda q, pp=pp, t0=t0, n=n, par=par: q.activation(
                    out=s_evo[par][:, t0:t0 + n], in_=pp[:, 0:n], func=AF.Silu),
                    reads=[bpp], writes=[b_evo[par]])
            elif oc == OC_DT:
                B.op("scalar", lambda q, pp=pp, t0=t0, n=n: q.activation(
                    out=s_dt[:, t0:t0 + n], in_=pp[:, 0:n], func=AF.Copy),
                    reads=[bpp], writes=[b_sdt])
            else:
                B.op("scalar", lambda q, pp=pp, t0=t0, n=n, par=par: q.activation(
                    out=s_pre[par][:, t0:t0 + n], in_=pp[:, 0:n], func=AF.Copy),
                    reads=[bpp], writes=[b_pre[par]])
        if oc < OC_X0:
            dma("sync", d_sg[oc * 128:(oc + 1) * 128, :], s_evo[par][:, HALO:TH],
                reads=[b_evo[par]], writes=[b_dsg], group="g_sp")
        elif oc == OC_DT:
            pass
        else:
            co = oc - OC_X0
            B.op("vector", lambda q, co=co, par=par: q.tensor_scalar(
                out=s_cvt[par][:, :], in0=s_pre[par][:, 3:3 + T],
                scalar1=s_cw[:, 4 * co + 3:4 * co + 4], scalar2=None, op0=ALU.mult),
                reads=[b_pre[par], b_scw], writes=[b_cvt[par]])
            for kk in (2, 1, 0):
                B.op("vector", lambda q, co=co, kk=kk, par=par: q.scalar_tensor_tensor(
                    out=s_cvt[par][:, :], in0=s_pre[par][:, kk:kk + T],
                    scalar=s_cw[:, 4 * co + kk:4 * co + kk + 1], in1=s_cvt[par][:, :],
                    op0=ALU.mult, op1=ALU.add),
                    reads=[b_pre[par], b_scw, b_cvt[par]], writes=[b_cvt[par]])
            B.op("scalar", lambda q, co=co, par=par: q.activation(
                out=s_evo[par][:, 0:T], in_=s_cvt[par][:, :], func=AF.Silu,
                bias=s_cb[:, co:co + 1]), reads=[b_cvt[par], b_scb], writes=[b_evo[par]])
            if oc < OC_B0:
                dma("sync", d_x[co * 128:(co + 1) * 128, :], s_evo[par][:, 0:T],
                    reads=[b_evo[par]], writes=[b_dx], group="g_sp")
            elif oc < OC_C0:
                g = oc - OC_B0
                B.op("vector", lambda q, g=g, par=par: q.tensor_copy(
                    out=s_B[:, g * T:(g + 1) * T], in_=s_evo[par][:, 0:T]),
                    reads=[b_evo[par]], writes=[b_sB])
                dma("sync", d_b[g * 128:(g + 1) * 128, :], s_evo[par][:, 0:T],
                    reads=[b_evo[par]], writes=[b_db], group="g_sp")
            else:
                g = oc - OC_C0
                B.op("vector", lambda q, g=g, par=par: q.tensor_copy(
                    out=s_C[:, g * T:(g + 1) * T], in_=s_evo[par][:, 0:T]),
                    reads=[b_evo[par]], writes=[b_sC])

    B.flush(block)
    snap1 = B.snap()
    B.barrier(snap1)
    st1.close()
    if PHASE_LIMIT <= 1:
        B.finalize(block)
        sem_stack.close()
        return nc

    st2 = contextlib.ExitStack()
    s_dtp, b_sdtp = sb("dtp", [128, T], F32, st2)
    s_dA, b_sdA = sb("dA", [128, T], F32, st2)
    s_brun, b_brun = sb("brun", [128, INTER], F32, st2)
    s_bigT, b_bigT = sb("bigT", [128, INTER], BF16, st2)
    s_BT, b_sBT = sb("BT", [128, G * NST], BF16, st2)
    s_cum, b_scum = sb("cumc", [128, CS], F32, st2)
    s_tmpA, b_tmpA = sb("tmpA", [128, CS], F32, st2)
    s_tmpB, b_tmpB = sb("tmpB", [128, 4], F32, st2)
    p_tr0 = ps("ptr0", st2)
    p_tr1 = ps("ptr1", st2)
    p_row = ps("prow", st2)
    p_cdb = ps("pcdb", st2)
    p_st = [ps("pst0", st2), ps("pst1", st2)]
    p_dtot = ps("pdtot", st2)

    B.op("vector", lambda q: q.tensor_scalar(
        out=s_dtp[:, :], in0=s_dt[:, HALO:TH], scalar1=s_dtb[:, :], scalar2=None,
        op0=ALU.add), reads=[b_sdt, b_sdtb], writes=[b_sdtp])
    B.op("scalar", lambda q: q.activation(out=s_dA[:, :], in_=s_dtp[:, :], func=AF.Abs),
         reads=[b_sdtp], writes=[b_sdA])
    B.op("scalar", lambda q: q.activation(out=s_dA[:, :], in_=s_dA[:, :], func=AF.Exp,
                                          scale=-1.0), reads=[b_sdA], writes=[b_sdA])
    B.op("scalar", lambda q: q.activation(out=s_dA[:, :], in_=s_dA[:, :], func=AF.Ln,
                                          bias=s_onec[:, :]),
         reads=[b_sdA, b_sonec], writes=[b_sdA])
    B.op("vector", lambda q: q.tensor_scalar(
        out=s_dtp[:, :], in0=s_dtp[:, :], scalar1=0.0, scalar2=None, op0=ALU.max),
        reads=[b_sdtp], writes=[b_sdtp])
    B.op("vector", lambda q: q.tensor_add(out=s_dtp[:, :], in0=s_dtp[:, :], in1=s_dA[:, :]),
         reads=[b_sdtp, b_sdA], writes=[b_sdtp])
    B.op("vector", lambda q: q.tensor_scalar(
        out=s_dA[:, :], in0=s_dtp[:, :], scalar1=s_negA[:, :], scalar2=None, op0=ALU.mult),
        reads=[b_sdtp, b_negA], writes=[b_sdA])
    B.op("vector", lambda q: q.memset(s_brun[:, :], 0.0), reads=[], writes=[b_brun])

    for c in range(NCH):
        c0, c1 = c * CS, (c + 1) * CS
        B.op("vector", lambda q, c0=c0, c1=c1: q.tensor_tensor_scan(
            out=s_cum[:, :], data0=s_onest[:, :], data1=s_dA[:, c0:c1],
            initial=0.0, op0=ALU.mult, op1=ALU.add), reads=[b_onest, b_sdA], writes=[b_scum])
        dma("sync", d_cum[c], s_cum[:, :], reads=[b_scum], writes=[b_dcum], group="g_sp")
        B.op("tensor", lambda q: q.transpose(p_tr0[0][:, 0:CS], s_cum[:, :], s_ident[:, :]),
             reads=[b_scum, b_ident], writes=[p_tr0[1]])
        B.op("vector", lambda q, c0=c0, c1=c1: q.tensor_copy(
            out=s_cumT[:, c0:c1], in_=p_tr0[0][:, 0:CS]),
            reads=[p_tr0[1]], writes=[b_scumT])
        B.op("vector", lambda q, c0=c0, c1=c1: q.tensor_scalar(
            out=s_ncumT[:, c0:c1], in0=s_cumT[:, c0:c1], scalar1=-1.0,
            scalar2=None, op0=ALU.mult), reads=[b_scumT], writes=[b_sncumT])
        B.op("scalar", lambda q: q.activation(
            out=s_tmpA[:, :], in_=s_cum[:, :], func=AF.Exp, scale=-1.0,
            bias=s_cum[:, CS - 1:CS]), reads=[b_scum], writes=[b_tmpA])
        B.op("tensor", lambda q, c0=c0, c1=c1: q.transpose(
            p_tr1[0][:, 0:CS], s_dtp[:, c0:c1], s_ident[:, :]),
            reads=[b_sdtp, b_ident], writes=[p_tr1[1]])
        B.op("vector", lambda q, c0=c0, c1=c1: q.tensor_copy(
            out=s_dtpT[:, c0:c1], in_=p_tr1[0][:, 0:CS]),
            reads=[p_tr1[1]], writes=[b_sdtpT])
        B.op("tensor", lambda q: q.transpose(p_tr1[0][:, 0:CS], s_tmpA[:, :], s_ident[:, :]),
             reads=[b_tmpA, b_ident], writes=[p_tr1[1]])
        B.op("vector", lambda q, c0=c0, c1=c1: q.tensor_copy(
            out=s_dlastT[:, c0:c1], in_=p_tr1[0][:, 0:CS]),
            reads=[p_tr1[1]], writes=[b_sdlastT])
        B.op("vector", lambda q, c0=c0, c1=c1: q.reduce_sum(
            out=s_tmpB[:, 0:1], in_=s_dA[:, c0:c1], axis=AX.X),
            reads=[b_sdA], writes=[b_tmpB])
        B.op("tensor", lambda q, c=c: q.matmul(
            p_dtot[0][0:1, 0:128], s_tmpB[:, 0:1], s_ident[:, :],
            start=(c == 0), stop=(c == NCH - 1)),
            reads=[b_tmpB, b_ident], writes=[p_dtot[1]])
        B.op("scalar", lambda q: q.activation(out=s_tmpB[:, 1:2], in_=s_tmpB[:, 0:1],
                                              func=AF.Exp), reads=[b_tmpB], writes=[b_tmpB])
        B.op("tensor", lambda q: q.matmul(
            p_row[0][0:1, 0:128], s_tmpB[:, 1:2], s_ident[:, :], start=True, stop=True),
            reads=[b_tmpB, b_ident], writes=[p_row[1]])
        B.op("vector", lambda q, c=c: q.tensor_copy(
            out=s_cdrow[:, c * H:(c + 1) * H], in_=p_row[0][0:1, 0:128]),
            reads=[p_row[1]], writes=[b_scdrow])
        if c > 0:
            B.op("tensor", lambda q, c=c: q.matmul(
                p_cdb[0][:, 0:128], s_ones1[:, :], s_cdrow[0:1, c * H:(c + 1) * H],
                start=True, stop=True), reads=[b_ones1, b_scdrow], writes=[p_cdb[1]])
            B.op("vector", lambda q: q.tensor_tensor(
                out=s_brun[:, :].rearrange("p (h a) -> p h a", h=H),
                in0=s_brun[:, :].rearrange("p (h a) -> p h a", h=H),
                in1=p_cdb[0][:, 0:128].unsqueeze(2).broadcast_to([128, H, PH]),
                op=ALU.mult), reads=[b_brun, p_cdb[1]], writes=[b_brun])
        dma("sync", s_bigT[:, :], d_x[:, c0:c1], reads=[b_dx], writes=[b_bigT], transpose=True)
        B.op("vector", lambda q, c0=c0, c1=c1: q.tensor_mul(
            out=s_fT[:, :], in0=s_dtpT[:, c0:c1], in1=s_dlastT[:, c0:c1]),
            reads=[b_sdtpT, b_sdlastT], writes=[b_sfT])
        B.op("vector", lambda q: q.tensor_tensor(
            out=s_bigT[:, :].rearrange("p (h a) -> p h a", h=H),
            in0=s_bigT[:, :].rearrange("p (h a) -> p h a", h=H),
            in1=s_fT[:, :].unsqueeze(2).broadcast_to([128, H, PH]),
            op=ALU.mult), reads=[b_bigT, b_sfT], writes=[b_bigT])
        dma("sync", s_BT[:, :], d_b[:, c0:c1], reads=[b_db], writes=[b_sBT], transpose=True)
        for hb in range(16):
            pst, bst = p_st[hb % 2]
            for hh in range(8):
                h = hb * 8 + hh
                g = h // 16
                B.op("tensor", lambda q, pst=pst, g=g, h=h, hh=hh: q.matmul(
                    pst[:, hh * PH:(hh + 1) * PH], s_BT[:, g * NST:(g + 1) * NST],
                    s_bigT[:, h * PH:(h + 1) * PH], start=True, stop=True),
                    reads=[b_sBT, b_bigT], writes=[bst])
            B.op("vector", lambda q, pst=pst, hb=hb: q.tensor_tensor(
                out=s_brun[:, hb * 512:(hb + 1) * 512], in0=s_brun[:, hb * 512:(hb + 1) * 512],
                in1=pst[:, :], op=ALU.add), reads=[bst, b_brun], writes=[b_brun])

    dma("gpsimd", d_agbi, s_brun[:, :], reads=[b_brun], writes=[b_agbi], group="g_spg")
    B.op("vector", lambda q: q.tensor_copy(out=s_row[:, :], in_=p_dtot[0][0:1, 0:128]),
         reads=[p_dtot[1]], writes=[b_srow])
    dma("sync", d_agdi, s_row[:, :], reads=[b_srow], writes=[b_agdi], group="g_sp")
    snap2 = B.snap()
    B.op("gpsimd", lambda q: q.collective_compute(
        "AllGather", ALU.bypass, replica_groups=[list(range(NCORES))],
        ins=[d_agbi], outs=[d_agbo]), reads=[b_agbi], writes=[b_agbo], inc=1,
        group="g_cc")
    B.op("gpsimd", lambda q: q.collective_compute(
        "AllGather", ALU.bypass, replica_groups=[list(range(NCORES))],
        ins=[d_agdi], outs=[d_agdo]), reads=[b_agdi], writes=[b_agdo], inc=1,
        group="g_cc")
    B.flush(block)
    B.barrier(snap2)
    st2.close()
    if PHASE_LIMIT <= 2:
        B.finalize(block)
        sem_stack.close()
        return nc

    st3 = contextlib.ExitStack()
    s_ftb, b_ftb = [], []
    for i in range(2):
        h_, bb = sb(f"ftb{i}", [128, D], BF16, st3)
        s_ftb.append(h_)
        b_ftb.append(bb)
    s_c01, b_sc01 = [], []
    s_m01f, b_sm01f = [], []
    for i in range(2):
        h_, bb = sb(f"c01_{i}", [128, 120], F32, st3)
        s_c01.append(h_)
        b_sc01.append(bb)
        h_, bb = sb(f"m01_{i}", [128, 120], F32, st3)
        s_m01f.append(h_)
        b_sm01f.append(bb)
    s_wm, b_swm = sb("wm", [128, 120], F32, st3)
    s_wt, b_swt = sb("wt", [128, 40], F32, st3)
    s_w01, b_sw01 = sb("w01", [128, 32 * 40], BF16, st3)
    p_fwt = [ps(f"pfwt{i}", st3) for i in range(8)]

    dma("gpsimd", s_enc[:, :].rearrange("p (a r) -> p a r", a=32),
        enc_in.rearrange("(a p) r -> p a r", p=128), reads=[b_enc], writes=[b_senc],
        group="g_rdg")
    for kt in range(32):
        i2 = kt % 2
        dma("sync", s_c01[i2][:, :], c01_in[kt * 128:(kt + 1) * 128, :],
            reads=[b_c01], writes=[b_sc01[i2]])
        dma("gpsimd", s_m01f[i2][:, :], m01_in[kt * 128:(kt + 1) * 128, :],
            reads=[b_m01], writes=[b_sm01f[i2]], group="g_rdg")
        B.op("vector", lambda q, i2=i2: q.tensor_mul(
            out=s_wm[:, :], in0=s_c01[i2][:, :], in1=s_m01f[i2][:, :]),
            reads=[b_sc01[i2], b_sm01f[i2]], writes=[b_swm])
        wmv = s_wm[:, :].rearrange("p (r e) -> p r e", r=40)
        B.op("vector", lambda q, wmv=wmv: q.tensor_add(
            out=s_wt[:, :].unsqueeze(2), in0=wmv[:, :, 0:1], in1=wmv[:, :, 1:2]),
            reads=[b_swm], writes=[b_swt])
        B.op("vector", lambda q, wmv=wmv, kt=kt: q.tensor_add(
            out=s_w01[:, kt * 40:(kt + 1) * 40].unsqueeze(2),
            in0=s_wt[:, :].unsqueeze(2), in1=wmv[:, :, 2:3]),
            reads=[b_swt, b_swm], writes=[b_sw01])
    for kt in range(32):
        fb, bfb = s_ftb[kt % 2], b_ftb[kt % 2]
        dma("sync", fb[:, :], ft_in[kt * 128:(kt + 1) * 128, :], reads=[b_ft], writes=[bfb],
            group="g_w%d" % (kt % 2))
        for ot in range(8):
            pf, bpf = p_fwt[ot]
            B.op("tensor", lambda q, pf=pf, fb=fb, kt=kt, ot=ot: q.matmul(
                pf[0:40, 0:512], s_w01[:, kt * 40:(kt + 1) * 40],
                fb[:, ot * 512:(ot + 1) * 512], start=(kt == 0), stop=(kt == 31)),
                reads=[b_sw01, bfb], writes=[bpf])
    for ot in range(8):
        pf, bpf = p_fwt[ot]
        B.op("scalar", lambda q, pf=pf, ot=ot: q.activation(
            out=s_fwt[:, ot * 512:(ot + 1) * 512], in_=pf[0:40, 0:512],
            func=AF.Copy, scale=SCALING), reads=[bpf], writes=[b_sfwt])
    B.flush(block)
    snap3 = B.snap()
    B.barrier(snap3)
    st3.close()
    if PHASE_LIMIT <= 3:
        B.finalize(block)
        sem_stack.close()
        return nc

    st4 = contextlib.ExitStack()
    s_prev, b_prev = sb("prev", [128, INTER], F32, st4)
    s_prevb, b_prevb = sb("prevb", [128, INTER], BF16, st4)
    s_big, b_big = sb("big", [128, INTER], BF16, st4)
    s_bigH, b_bigH = sb("bigH", [128, INTER], BF16, st4)
    s_BT4, b_sBT4 = sb("BT4", [128, G * NST], BF16, st4)
    s_sgc, b_ssgc = sb("sgc", [128, INTER], BF16, st4)
    s_xfc, b_sxfc = sb("xfc", [128, INTER], BF16, st4)
    s_scm, b_sscm = sb("scm", [128, G * CS], BF16, st4)
    s_cumq, s_E4, s_S4, s_ce4, s_ecb, s_gq, s_sbc = [], [], [], [], [], [], []
    b_scumq, b_sE4, b_sS4, b_sce4, b_secb, b_sgq, b_ssbc = [], [], [], [], [], [], []
    for i in range(2):
        t_, bb = sb(f"cumq{i}", [1, 4 * CS], F32, st4)
        s_cumq.append(t_); b_scumq.append(bb)
        t_, bb = sb(f"E4{i}", [128, 512], BF16, st4)
        s_E4.append(t_); b_sE4.append(bb)
        t_, bb = sb(f"S4{i}", [128, 512], BF16, st4)
        s_S4.append(t_); b_sS4.append(bb)
        t_, bb = sb(f"ce4{i}", [128, 512], BF16, st4)
        s_ce4.append(t_); b_sce4.append(bb)
        t_, bb = sb(f"ecb{i}", [128, 512], BF16, st4)
        s_ecb.append(t_); b_secb.append(bb)
        t_, bb = sb(f"gq{i}", [128, 2 * CS], BF16, st4)
        s_gq.append(t_); b_sgq.append(bb)
        t_, bb = sb(f"sbc{i}", [128, CS], F32, st4)
        s_sbc.append(t_); b_ssbc.append(bb)
    s_wz, b_swz = sb("wz", [128, 8], F32, st4)
    s_dtg, b_sdtg = sb("dtg", [8, 128], F32, st4)
    s_yt, b_syt = sb("yt", [128, 128], F32, st4)
    s_g0, b_sg0 = [], []
    for i in range(2):
        t_, bb = sb(f"g0_{i}", [128, 8 * CS], BF16, st4)
        s_g0.append(t_); b_sg0.append(bb)
    p_sc = [ps("psc0", st4), ps("psc1", st4)]
    p_bc = [ps("pbc0", st4), ps("pbc1", st4)]
    p_y = [ps("py0", st4), ps("py1", st4)]
    p_v = ps("pv", st4)
    p_sb2 = ps("psb", st4)
    p_y = p_y + p_sc

    dma("sync", s_dtg[:, :], d_agdo, reads=[b_agdo], writes=[b_sdtg])
    B.op("tensor", lambda q: q.matmul(p_bc[0][0][:, 0:8], s_dtg[:, :], s_pmt[:, :],
                                      start=True, stop=True),
         reads=[b_sdtg, b_spmt], writes=[p_bc[0][1]])
    B.op("scalar", lambda q: q.activation(out=s_wz[:, :], in_=p_bc[0][0][:, 0:8],
                                          func=AF.Exp), reads=[p_bc[0][1]], writes=[b_swz])
    B.op("vector", lambda q: q.tensor_mul(out=s_wz[:, :], in0=s_wz[:, :],
                                          in1=s_zmask[:, :]),
         reads=[b_swz, b_szmask], writes=[b_swz])
    B.op("vector", lambda q: q.memset(s_prev[:, :], 0.0), reads=[], writes=[b_prev])
    for j in range(NCORES - 1):
        dma("sync", s_bigH[:, :], d_agbo[j * NST:(j + 1) * NST, :],
            reads=[b_agbo], writes=[b_bigH])
        B.op("tensor", lambda q, j=j: q.matmul(
            p_bc[1][0][0:1, 0:128],
            s_wz[:, j:j + 1], s_ident[:, :], start=True, stop=True),
            reads=[b_swz, b_ident], writes=[p_bc[1][1]])
        B.op("vector", lambda q: q.tensor_copy(out=s_row[:, :], in_=p_bc[1][0][0:1, 0:128]),
             reads=[p_bc[1][1]], writes=[b_srow])
        B.op("tensor", lambda q: q.matmul(
            p_bc[0][0][:, 0:128], s_ones1[:, :], s_row[0:1, :], start=True, stop=True),
            reads=[b_ones1, b_srow], writes=[p_bc[0][1]])
        B.op("vector", lambda q: q.tensor_tensor(
            out=s_big[:, :].rearrange("p (h a) -> p h a", h=H),
            in0=s_bigH[:, :].rearrange("p (h a) -> p h a", h=H),
            in1=p_bc[0][0][:, 0:128].unsqueeze(2).broadcast_to([128, H, PH]),
            op=ALU.mult), reads=[b_bigH, p_bc[0][1]], writes=[b_big])
        B.op("vector", lambda q: q.tensor_tensor(
            out=s_prev[:, :], in0=s_prev[:, :], in1=s_big[:, :], op=ALU.add),
            reads=[b_prev, b_big], writes=[b_prev])
    B.op("vector", lambda q: q.tensor_copy(out=s_prevb[:, :], in_=s_prev[:, :]),
         reads=[b_prev], writes=[b_prevb])

    for c in range(NCH):
        c0, c1 = c * CS, (c + 1) * CS
        dma("sync", s_big[:, :], d_x[:, c0:c1], reads=[b_dx], writes=[b_big], transpose=True)
        B.op("vector", lambda q, c0=c0, c1=c1: q.tensor_tensor(
            out=s_big[:, :].rearrange("p (h a) -> p h a", h=H),
            in0=s_big[:, :].rearrange("p (h a) -> p h a", h=H),
            in1=s_dtpT[:, c0:c1].unsqueeze(2).broadcast_to([128, H, PH]),
            op=ALU.mult), reads=[b_big, b_sdtpT], writes=[b_big])
        B.op("vector", lambda q, c0=c0, c1=c1: q.tensor_tensor(
            out=s_bigH[:, :].rearrange("p (h a) -> p h a", h=H),
            in0=s_big[:, :].rearrange("p (h a) -> p h a", h=H),
            in1=s_dlastT[:, c0:c1].unsqueeze(2).broadcast_to([128, H, PH]),
            op=ALU.mult), reads=[b_big, b_sdlastT], writes=[b_bigH])
        dma("sync", s_BT4[:, :], d_b[:, c0:c1], reads=[b_db], writes=[b_sBT4], transpose=True)
        dma("sync", s_sgc[:, :].rearrange("p (a t) -> p a t", a=64),
            d_sg.rearrange("(a p) t -> p a t", p=128)[:, :, c0:c1],
            reads=[b_dsg], writes=[b_ssgc])
        dma("sync", s_xfc[:, :].rearrange("p (a t) -> p a t", a=64),
            d_x.rearrange("(a p) t -> p a t", p=128)[:, :, c0:c1],
            reads=[b_dx], writes=[b_sxfc])
        for g in range(G):
            pg, bpg = p_sc[g // 4]
            B.op("tensor", lambda q, pg=pg, g=g, c0=c0, c1=c1: q.matmul(
                pg[:, (g % 4) * CS:(g % 4 + 1) * CS], s_B[:, g * T + c0:g * T + c1],
                s_C[:, g * T + c0:g * T + c1], start=True, stop=True),
                reads=[b_sB, b_sC], writes=[bpg])
        for g in range(G):
            pg, bpg = p_sc[g // 4]
            B.op("vector", lambda q, pg=pg, g=g: q.tensor_tensor(
                out=s_scm[:, g * CS:(g + 1) * CS], in0=pg[:, (g % 4) * CS:(g % 4 + 1) * CS],
                in1=s_tril[:, :], op=ALU.mult), reads=[bpg, b_tril], writes=[b_sscm])
        def emit_bcast(hb):
            bpar = hb % 2
            h0 = hb * 4
            dma("sync", s_cumq[bpar][:, :],
                d_cum[c, h0:h0 + 4, :].rearrange("a t -> (a t)").unsqueeze(0),
                reads=[b_dcum], writes=[b_scumq[bpar]])
            pb, bpb = p_bc[bpar]
            B.op("tensor", lambda q, pb=pb, bpar=bpar: q.matmul(
                pb[:, 0:512], s_ones1[:, :],
                s_cumq[bpar][0:1, 0:512], start=True, stop=True),
                reads=[b_ones1, b_scumq[bpar]], writes=[bpb])

        def emit_dact(hb, c0=None):
            bpar = hb % 2
            h0 = hb * 4
            pb, bpb = p_bc[bpar]
            B.op("scalar", lambda q, pb=pb, bpar=bpar: q.activation(
                out=s_ecb[bpar][:, :], in_=pb[:, 0:512], func=AF.Exp),
                reads=[bpb], writes=[b_secb[bpar]])
            for hh in range(4):
                B.op("scalar", lambda q, pb=pb, h0=h0, hh=hh, bpar=bpar:
                     q.activation(
                         out=s_E4[bpar][:, hh * CS:(hh + 1) * CS],
                         in_=pb[:, hh * CS:(hh + 1) * CS], func=AF.Exp,
                         bias=s_ncumT[:, c0 + h0 + hh:c0 + h0 + hh + 1]),
                     reads=[bpb, b_sncumT], writes=[b_sE4[bpar]])

        def emit_ddve(hb, c0=None, c1=None):
            bpar = hb % 2
            g = hb // 4
            B.op("vector", lambda q, bpar=bpar: q.tensor_scalar(
                out=s_E4[bpar][:, :], in0=s_E4[bpar][:, :], scalar1=1.0,
                scalar2=None, op0=ALU.min),
                reads=[b_sE4[bpar]], writes=[b_sE4[bpar]])
            B.op("vector", lambda q, g=g, bpar=bpar: q.tensor_tensor(
                out=s_S4[bpar][:, :].rearrange("p (a t) -> p a t", a=4),
                in0=s_E4[bpar][:, :].rearrange("p (a t) -> p a t", a=4),
                in1=s_scm[:, g * CS:(g + 1) * CS].unsqueeze(1).broadcast_to([128, 4, CS]),
                op=ALU.mult), reads=[b_sE4[bpar], b_sscm], writes=[b_sS4[bpar]])
            B.op("vector", lambda q, g=g, bpar=bpar: q.tensor_tensor(
                out=s_ce4[bpar][:, :].rearrange("p (a t) -> p a t", a=4),
                in0=s_C[:, g * T + c0:g * T + c1].unsqueeze(1).broadcast_to([128, 4, CS]),
                in1=s_ecb[bpar][:, :].rearrange("p (a t) -> p a t", a=4),
                op=ALU.mult), reads=[b_sC, b_secb[bpar]], writes=[b_sce4[bpar]])

        def emit_y(hb):
            bpar = hb % 2
            h0 = hb * 4
            for hh in range(4):
                h = h0 + hh
                pr = h // 2
                py, bpy = p_y[pr % 4]
                po = (h % 2) * PH
                B.op("tensor", lambda q, py=py, po=po, h=h, hh=hh, bpar=bpar: q.matmul(
                    py[po:po + PH, 0:CS], s_big[:, h * PH:(h + 1) * PH],
                    s_S4[bpar][:, hh * CS:(hh + 1) * CS], start=True, stop=False),
                    reads=[b_big, b_sS4[bpar]], writes=[bpy])
                B.op("tensor", lambda q, py=py, po=po, h=h, hh=hh, bpar=bpar: q.matmul(
                    py[po:po + PH, 0:CS], s_prevb[:, h * PH:(h + 1) * PH],
                    s_ce4[bpar][:, hh * CS:(hh + 1) * CS], start=False, stop=True),
                    reads=[b_prevb, b_sce4[bpar]], writes=[bpy])

        def emit_gdve(hb):
            h0 = hb * 4
            for pri in range(2):
                pr = h0 // 2 + pri
                gg = pr // 8
                gp = gg % 2
                py2, bpy2 = p_y[pr % 4]
                B.op("vector", lambda q, py2=py2, pr=pr: q.scalar_tensor_tensor(
                    out=s_yt[:, :], in0=s_xfc[:, pr * CS:(pr + 1) * CS],
                    scalar=s_dexp[:, pr:pr + 1], in1=py2[:, 0:CS],
                    op0=ALU.mult, op1=ALU.add),
                    reads=[b_sxfc, b_sdexp, bpy2], writes=[b_syt])
                B.op("vector", lambda q, pr=pr, gp=gp: q.tensor_tensor(
                    out=s_g0[gp][:, (pr % 8) * CS:(pr % 8 + 1) * CS], in0=s_yt[:, :],
                    in1=s_sgc[:, pr * CS:(pr + 1) * CS], op=ALU.mult),
                    reads=[b_syt, b_ssgc], writes=[b_sg0[gp]])

        def emit_gact(hb):
            h0 = hb * 4
            hpar = hb % 2
            pr0 = h0 // 2
            gp0 = (pr0 // 8) % 2
            sl = (pr0 % 8) * CS
            B.op("scalar", lambda q, gp0=gp0, sl=sl, hpar=hpar: q.activation(
                out=s_gq[hpar][:, :], in_=s_g0[gp0][:, sl:sl + 2 * CS],
                func=AF.Square), reads=[b_sg0[gp0]], writes=[b_sgq[hpar]])

        def emit_var(hb):
            h0 = hb * 4
            hpar = hb % 2
            for pri in range(2):
                pr = h0 // 2 + pri
                B.op("tensor", lambda q, pr=pr, pri=pri, hpar=hpar: q.matmul(
                    p_v[0][0:1, 0:CS], s_invgs[:, :],
                    s_gq[hpar][:, pri * CS:(pri + 1) * CS],
                    start=(pr % 8 == 0), stop=(pr % 8 == 7)),
                    reads=[b_sgq[hpar], b_sinvgs], writes=[p_v[1]])

        def emit_final(hb):
            if (hb + 1) % 4 != 0:
                return
            gg = hb // 4
            gp = gg % 2
            B.op("scalar", lambda q: q.activation(
                out=s_row[:, :], in_=p_v[0][0:1, 0:CS], func=AF.Ln,
                bias=s_epsc[0:1, :]), reads=[p_v[1], b_sepsc], writes=[b_srow])
            B.op("scalar", lambda q: q.activation(
                out=s_row[:, :], in_=s_row[:, :], func=AF.Exp, scale=-0.5),
                reads=[b_srow], writes=[b_srow])
            B.op("tensor", lambda q: q.matmul(
                p_sb2[0][:, 0:CS], s_ones1[:, :], s_row[0:1, :],
                start=True, stop=True),
                reads=[b_ones1, b_srow], writes=[p_sb2[1]])
            B.op("scalar", lambda q, gp=gp: q.activation(
                out=s_sbc[gp][:, :], in_=p_sb2[0][:, 0:CS], func=AF.Copy),
                reads=[p_sb2[1]], writes=[b_ssbc[gp]])
            for p8 in range(8):
                prr = gg * 8 + p8
                B.op("vector", lambda q, p8=p8, prr=prr, gp=gp: q.scalar_tensor_tensor(
                    out=s_sgc[:, prr * CS:(prr + 1) * CS],
                    in0=s_g0[gp][:, p8 * CS:(p8 + 1) * CS],
                    scalar=s_nwc[:, prr:prr + 1], in1=s_sbc[gp][:, :],
                    op0=ALU.mult, op1=ALU.mult),
                    reads=[b_sg0[gp], b_snwc, b_ssbc[gp]], writes=[b_ssgc])

        for idx in range(32 + 7):
            if idx < 32:
                emit_bcast(idx)
            if 0 <= idx - 1 < 32:
                emit_dact(idx - 1, c0)
            if 0 <= idx - 2 < 32:
                emit_ddve(idx - 2, c0, c1)
            if 0 <= idx - 4 < 32:
                emit_gdve(idx - 4)
            if 0 <= idx - 5 < 32:
                emit_gact(idx - 5)
            if 0 <= idx - 3 < 32:
                emit_y(idx - 3)
            if 0 <= idx - 7 < 32:
                emit_final(idx - 7)
            if 0 <= idx - 6 < 32:
                emit_var(idx - 6)
        dma("scalar", d_g.rearrange("(a p) t -> p a t", p=128)[:, :, c0:c1],
            s_sgc[:, :].rearrange("p (a t) -> p a t", a=64),
            reads=[b_ssgc], writes=[b_dg], group="g_sp")
        B.op("tensor", lambda q, c=c: q.matmul(
            p_bc[0][0][:, 0:128],
            s_ones1[:, :], s_cdrow[0:1, c * H:(c + 1) * H], start=True, stop=True),
            reads=[b_ones1, b_scdrow], writes=[p_bc[0][1]])
        B.op("vector", lambda q: q.tensor_tensor(
            out=s_prev[:, :].rearrange("p (h a) -> p h a", h=H),
            in0=s_prev[:, :].rearrange("p (h a) -> p h a", h=H),
            in1=p_bc[0][0][:, 0:128].unsqueeze(2).broadcast_to([128, H, PH]),
            op=ALU.mult), reads=[b_prev, p_bc[0][1]], writes=[b_prev])
        for hb in range(16):
            pst, bst = p_sc[hb % 2]
            for hh in range(8):
                h = hb * 8 + hh
                g = h // 16
                B.op("tensor", lambda q, pst=pst, g=g, h=h, hh=hh: q.matmul(
                    pst[:, hh * PH:(hh + 1) * PH], s_BT4[:, g * NST:(g + 1) * NST],
                    s_bigH[:, h * PH:(h + 1) * PH], start=True, stop=True),
                    reads=[b_sBT4, b_bigH], writes=[bst])
            B.op("vector", lambda q, pst=pst, hb=hb: q.tensor_tensor(
                out=s_prev[:, hb * 512:(hb + 1) * 512],
                in0=s_prev[:, hb * 512:(hb + 1) * 512],
                in1=pst[:, :], op=ALU.add), reads=[bst, b_prev], writes=[b_prev])
        if c < NCH - 1:
            B.op("vector", lambda q: q.tensor_copy(out=s_prevb[:, :], in_=s_prev[:, :]),
                 reads=[b_prev], writes=[b_prevb])
    B.flush(block)
    snap4 = B.snap()
    B.barrier(snap4)
    st4.close()
    st14.close()
    if PHASE_LIMIT <= 4:
        B.finalize(block)
        sem_stack.close()
        return nc

    st5 = contextlib.ExitStack()
    s_gall, b_gall = sb("gall", [128, 64 * T], BF16, st5)
    s_owb, b_owb = [], []
    for i in range(2):
        h_, bb = sb(f"owb{i}", [128, 64 * 128], BF16, st5)
        s_owb.append(h_)
        b_owb.append(bb)
    s_aev, b_aev = [], []
    for i in range(2):
        h_, bb = sb(f"aev{i}", [128, 512], F32, st5)
        s_aev.append(h_)
        b_aev.append(bb)
    p_op = [ps(f"pop{i}", st5) for i in range(3)]

    dma("sync", s_gall[:, :].rearrange("p (a t) -> p a t", a=64),
        d_g.rearrange("(a p) t -> p a t", p=128), reads=[b_dg], writes=[b_gall])
    for oc in range(32):
        owb, bowb = s_owb[oc % 2], b_owb[oc % 2]
        dma("gpsimd", owb[:, :].rearrange("p (a b) -> p a b", a=64),
            w_out[oc].rearrange("a p b -> p a b"),
            reads=[b_w_out], writes=[bowb], group="g_sw%d" % (oc % 2))
        for tt in range(2):
            pp, bpp = p_op[(oc * 2 + tt) % 3]
            for kt in range(64):
                B.op("tensor", lambda q, pp=pp, owb=owb, kt=kt, tt=tt: q.matmul(
                    pp[:, 0:512], owb[:, kt * 128:(kt + 1) * 128],
                    s_gall[:, kt * T + tt * 512:kt * T + (tt + 1) * 512],
                    start=(kt == 0), stop=(kt == 63)),
                    reads=[bowb, b_gall], writes=[bpp])
            ae, bae = s_aev[tt], b_aev[tt]
            B.op("vector", lambda q, pp=pp, ae=ae: q.tensor_copy(
                out=ae[:, :], in_=pp[:, 0:512]), reads=[bpp], writes=[bae])
            dma("scalar", d_attn[oc * 128:(oc + 1) * 128, tt * 512:(tt + 1) * 512],
                ae[:, :], reads=[bae], writes=[b_dattn], group="g_sp")
    B.flush(block)
    snap5 = B.snap()
    B.barrier(snap5)
    st5.close()
    if PHASE_LIMIT <= 5:
        B.finalize(block)
        sem_stack.close()
        return nc

    stHB = contextlib.ExitStack()
    s_hbf, b_shbf = sb("hbf", [128, 32 * T], BF16, stHB)
    st6 = contextlib.ExitStack()
    s_xtm, b_sxtm = sb("xtm", [128, D], F32, st6)
    s_arb, b_sarb, s_zb, b_szb, s_zq, b_szq, s_hfm, b_shfm = [], [], [], [], [], [], [], []
    s_mrow, b_smrow, s_qrow, b_sqrow, s_vrow, b_svrow = [], [], [], [], [], []
    for i in range(2):
        t_, bb = sb(f"arb{i}", [128, D], F32, st6)
        s_arb.append(t_); b_sarb.append(bb)
        t_, bb = sb(f"zbf{i}", [128, D], BF16, st6)
        s_zb.append(t_); b_szb.append(bb)
        t_, bb = sb(f"zsq{i}", [128, D], BF16, st6)
        s_zq.append(t_); b_szq.append(bb)
        t_, bb = sb(f"hfm{i}", [128, D], F32, st6)
        s_hfm.append(t_); b_shfm.append(bb)
        t_, bb = sb(f"mrow{i}", [1, 128], F32, st6)
        s_mrow.append(t_); b_smrow.append(bb)
        t_, bb = sb(f"qrow{i}", [1, 128], F32, st6)
        s_qrow.append(t_); b_sqrow.append(bb)
        t_, bb = sb(f"vrow{i}", [1, 128], F32, st6)
        s_vrow.append(t_); b_svrow.append(bb)
    p_tx = [ps("ptx0", st6), ps("ptx1", st6)]
    p_m = ps("pm", st6)
    p_q = ps("pq", st6)
    p_s1 = [ps("ps1a", st6), ps("ps1b", st6)]
    p_s2 = [ps("ps2a", st6), ps("ps2b", st6)]

    def ln_rows(i, pm, pq, ps1, ps2):
        B.op("vector", lambda q: q.tensor_copy(out=s_mrow[i][:, :], in_=pm[0][0:1, 0:128]),
             reads=[pm[1]], writes=[b_smrow[i]])
        B.op("vector", lambda q: q.tensor_copy(out=s_qrow[i][:, :], in_=pq[0][0:1, 0:128]),
             reads=[pq[1]], writes=[b_sqrow[i]])
        B.op("vector", lambda q: q.tensor_mul(out=s_vrow[i][:, :], in0=s_mrow[i][:, :],
                                              in1=s_mrow[i][:, :]),
             reads=[b_smrow[i]], writes=[b_svrow[i]])
        B.op("vector", lambda q: q.tensor_sub(out=s_vrow[i][:, :], in0=s_qrow[i][:, :],
                                              in1=s_vrow[i][:, :]),
             reads=[b_sqrow[i], b_svrow[i]], writes=[b_svrow[i]])
        B.op("scalar", lambda q: q.activation(out=s_vrow[i][:, :], in_=s_vrow[i][:, :],
                                              func=AF.Ln, bias=s_epsc[0:1, :]),
             reads=[b_svrow[i], b_sepsc], writes=[b_svrow[i]])
        B.op("scalar", lambda q: q.activation(out=s_vrow[i][:, :], in_=s_vrow[i][:, :],
                                              func=AF.Exp, scale=-0.5),
             reads=[b_svrow[i]], writes=[b_svrow[i]])
        B.op("vector", lambda q: q.tensor_mul(out=s_mrow[i][:, :], in0=s_mrow[i][:, :],
                                              in1=s_vrow[i][:, :]),
             reads=[b_smrow[i], b_svrow[i]], writes=[b_smrow[i]])
        B.op("tensor", lambda q: q.matmul(ps1[0][:, 0:128], s_ones1[:, :],
                                          s_vrow[i][0:1, :], start=True, stop=True),
             reads=[b_ones1, b_svrow[i]], writes=[ps1[1]])
        B.op("tensor", lambda q: q.matmul(ps2[0][:, 0:128], s_ones1[:, :],
                                          s_mrow[i][0:1, :], start=True, stop=True),
             reads=[b_ones1, b_smrow[i]], writes=[ps2[1]])

    for tt in range(8):
        i = tt % 2
        dma("sync", s_xtm[:, :], x_in[HALO + tt * 128:HALO + (tt + 1) * 128, :],
            reads=[b_x_in], writes=[b_sxtm])
        dma("sync", s_arb[i][:, :].rearrange("p (a t) -> p a t", a=32),
            d_attn.rearrange("(a p) t -> p a t", p=128)[:, :, tt * 128:(tt + 1) * 128],
            reads=[b_dattn], writes=[b_sarb[i]])
        for dc in range(32):
            ptx, bptx = p_tx[dc % 2]
            B.op("tensor", lambda q, ptx=ptx, dc=dc: q.transpose(
                ptx[:, 0:128], s_xtm[:, dc * 128:(dc + 1) * 128], s_ident[:, :]),
                reads=[b_sxtm, b_ident], writes=[bptx])
            B.op("vector", lambda q, ptx=ptx, dc=dc, i=i: q.tensor_tensor(
                out=s_arb[i][:, dc * 128:(dc + 1) * 128], in0=ptx[:, 0:128],
                in1=s_arb[i][:, dc * 128:(dc + 1) * 128], op=ALU.add),
                reads=[bptx, b_sarb[i]], writes=[b_sarb[i]])
        B.op("scalar", lambda q, i=i: q.activation(out=s_zb[i][:, :], in_=s_arb[i][:, :],
                                                   func=AF.Copy),
             reads=[b_sarb[i]], writes=[b_szb[i]])
        B.op("scalar", lambda q, i=i: q.activation(out=s_zq[i][:, :], in_=s_arb[i][:, :],
                                                   func=AF.Square),
             reads=[b_sarb[i]], writes=[b_szq[i]])
        for dc in range(32):
            B.op("tensor", lambda q, dc=dc, i=i: q.matmul(
                p_m[0][0:1, 0:128], s_invd[:, :], s_zb[i][:, dc * 128:(dc + 1) * 128],
                start=(dc == 0), stop=(dc == 31)), reads=[b_sinvd, b_szb[i]], writes=[p_m[1]])
        for dc in range(32):
            B.op("tensor", lambda q, dc=dc, i=i: q.matmul(
                p_q[0][0:1, 0:128], s_invd[:, :], s_zq[i][:, dc * 128:(dc + 1) * 128],
                start=(dc == 0), stop=(dc == 31)), reads=[b_sinvd, b_szq[i]], writes=[p_q[1]])
        ln_rows(i, p_m, p_q, p_s1[i], p_s2[i])
        for dc in range(32):
            B.op("vector", lambda q, dc=dc, i=i: q.tensor_tensor(
                out=s_hfm[i][:, dc * 128:(dc + 1) * 128],
                in0=s_arb[i][:, dc * 128:(dc + 1) * 128],
                in1=p_s1[i][0][:, 0:128], op=ALU.mult),
                reads=[b_sarb[i], p_s1[i][1]], writes=[b_shfm[i]])
        B.op("vector", lambda q, i=i: q.tensor_tensor(
            out=s_hfm[i][:, :].rearrange("p (a t) -> p a t", a=32),
            in0=s_hfm[i][:, :].rearrange("p (a t) -> p a t", a=32),
            in1=p_s2[i][0][:, 0:128].unsqueeze(1).broadcast_to([128, 32, 128]),
            op=ALU.subtract), reads=[b_shfm[i], p_s2[i][1]], writes=[b_shfm[i]])
        dma("scalar", d_h.rearrange("(a p) t -> p a t", p=128)[:, :, tt * 128:(tt + 1) * 128],
            s_hfm[i][:, :].rearrange("p (a t) -> p a t", a=32),
            reads=[b_shfm[i]], writes=[b_dh], group="g_sp")
        B.op("scalar", lambda q, tt=tt, i=i: q.activation(
            out=s_hbf[:, :].rearrange("p (a t) -> p a t", a=32)[:, :, tt * 128:(tt + 1) * 128],
            in_=s_hfm[i][:, :].rearrange("p (a t) -> p a t", a=32), func=AF.Copy),
            reads=[b_shfm[i]], writes=[b_shbf])
    B.flush(block)
    snap6 = B.snap()
    B.barrier(snap6)
    st6.close()
    if PHASE_LIMIT <= 6:
        B.finalize(block)
        sem_stack.close()
        return nc

    st7 = contextlib.ExitStack()
    s_wbb, b_wbb = [], []
    for i in range(2):
        h_, bb = sb(f"wbb{i}", [128, 32 * 128], BF16, st7)
        s_wbb.append(h_)
        b_wbb.append(bb)
    s_hrb, b_shrb = [], []
    for i in range(2):
        h_, bb = sb(f"hrb{i}", [128, 512], F32, st7)
        s_hrb.append(h_)
        b_shrb.append(bb)
    s_z2e, b_sz2e = [], []
    for i in range(2):
        h_, bb = sb(f"z2e{i}", [128, 512], F32, st7)
        s_z2e.append(h_)
        b_sz2e.append(bb)
    p_ff = [ps(f"pff{i}", st7) for i in range(3)]
    p_e = ps("pe", st7)

    for tt in range(2):
        for kt in range(32):
            B.op("tensor", lambda q, kt=kt, tt=tt: q.matmul(
                p_e[0][0:40, 0:512], s_enc[:, kt * 40:(kt + 1) * 40],
                s_hbf[:, kt * T + tt * 512:kt * T + (tt + 1) * 512],
                start=(kt == 0), stop=(kt == 31)),
                reads=[b_senc, b_shbf], writes=[p_e[1]])
        B.op("vector", lambda q, tt=tt: q.tensor_copy(
            out=s_e01[0:40, tt * 512:(tt + 1) * 512], in_=p_e[0][0:40, 0:512]),
            reads=[p_e[1]], writes=[b_se01])
    for oc in range(32):
        wbb, bwbb = s_wbb[oc % 2], b_wbb[oc % 2]
        dma("gpsimd", wbb[:, :].rearrange("p (a b) -> p a b", a=32),
            w_baseT[oc].rearrange("a p b -> p a b"),
            reads=[b_w_base], writes=[bwbb], group="g_sw%d" % (oc % 2))
        for tt in range(2):
            pp, bpp = p_ff[(oc * 2 + tt) % 3]
            for kt in range(32):
                B.op("tensor", lambda q, pp=pp, wbb=wbb, kt=kt, tt=tt: q.matmul(
                    pp[:, 0:512], wbb[:, kt * 128:(kt + 1) * 128],
                    s_hbf[:, kt * T + tt * 512:kt * T + (tt + 1) * 512],
                    start=(kt == 0), stop=False),
                    reads=[bwbb, b_shbf], writes=[bpp])
            B.op("tensor", lambda q, pp=pp, oc=oc, tt=tt: q.matmul(
                pp[:, 0:512], s_fwt[:, oc * 128:(oc + 1) * 128],
                s_e01[0:40, tt * 512:(tt + 1) * 512], start=False, stop=True),
                reads=[b_sfwt, b_se01], writes=[bpp])
            hr, bhr = s_hrb[tt], b_shrb[tt]
            dma("sync", hr[:, :], d_h[oc * 128:(oc + 1) * 128, tt * 512:(tt + 1) * 512],
                reads=[b_dh], writes=[bhr])
            ze, bze = s_z2e[tt], b_sz2e[tt]
            B.op("vector", lambda q, pp=pp, hr=hr, ze=ze, oc=oc: q.scalar_tensor_tensor(
                out=ze[:, :], in0=pp[:, 0:512], scalar=s_bbc[:, oc:oc + 1], in1=hr[:, :],
                op0=ALU.add, op1=ALU.add), reads=[bpp, b_sbbc, bhr], writes=[bze])
            dma("scalar", d_z2[oc * 128:(oc + 1) * 128, tt * 512:(tt + 1) * 512],
                ze[:, :], reads=[bze], writes=[b_dz2], group="g_sp")
    B.flush(block)
    snap7 = B.snap()
    B.barrier(snap7)
    st7.close()
    stHB.close()
    if PHASE_LIMIT <= 7:
        B.finalize(block)
        sem_stack.close()
        return nc

    st8 = contextlib.ExitStack()
    s_z2, b_sz2, s_zb2, b_szb2, s_zq2, b_szq2, s_otm, b_sotm = [], [], [], [], [], [], [], []
    s_mr2, b_smr2, s_qr2, b_sqr2, s_vr2, b_svr2 = [], [], [], [], [], []
    for i in range(2):
        t_, bb = sb(f"z2fm{i}", [128, D], F32, st8)
        s_z2.append(t_); b_sz2.append(bb)
        t_, bb = sb(f"zbf2{i}", [128, D], BF16, st8)
        s_zb2.append(t_); b_szb2.append(bb)
        t_, bb = sb(f"zsq2{i}", [128, D], BF16, st8)
        s_zq2.append(t_); b_szq2.append(bb)
        t_, bb = sb(f"otm{i}", [128, D], F32, st8)
        s_otm.append(t_); b_sotm.append(bb)
        t_, bb = sb(f"mr2{i}", [1, 128], F32, st8)
        s_mr2.append(t_); b_smr2.append(bb)
        t_, bb = sb(f"qr2{i}", [1, 128], F32, st8)
        s_qr2.append(t_); b_sqr2.append(bb)
        t_, bb = sb(f"vr2{i}", [1, 128], F32, st8)
        s_vr2.append(t_); b_svr2.append(bb)
    p_tx2 = [ps("ptx20", st8), ps("ptx21", st8)]
    p_m2 = ps("pm2", st8)
    p_q2 = ps("pq2", st8)
    p_s12 = [ps("ps12a", st8), ps("ps12b", st8)]
    p_s22 = [ps("ps22a", st8), ps("ps22b", st8)]

    for tt in range(8):
        i = tt % 2
        dma("sync", s_z2[i][:, :].rearrange("p (a t) -> p a t", a=32),
            d_z2.rearrange("(a p) t -> p a t", p=128)[:, :, tt * 128:(tt + 1) * 128],
            reads=[b_dz2], writes=[b_sz2[i]])
        B.op("scalar", lambda q, i=i: q.activation(out=s_zb2[i][:, :], in_=s_z2[i][:, :],
                                                   func=AF.Copy),
             reads=[b_sz2[i]], writes=[b_szb2[i]])
        B.op("scalar", lambda q, i=i: q.activation(out=s_zq2[i][:, :], in_=s_z2[i][:, :],
                                                   func=AF.Square),
             reads=[b_sz2[i]], writes=[b_szq2[i]])
        for dc in range(32):
            B.op("tensor", lambda q, dc=dc, i=i: q.matmul(
                p_m2[0][0:1, 0:128], s_invd[:, :], s_zb2[i][:, dc * 128:(dc + 1) * 128],
                start=(dc == 0), stop=(dc == 31)), reads=[b_sinvd, b_szb2[i]],
                writes=[p_m2[1]])
        for dc in range(32):
            B.op("tensor", lambda q, dc=dc, i=i: q.matmul(
                p_q2[0][0:1, 0:128], s_invd[:, :], s_zq2[i][:, dc * 128:(dc + 1) * 128],
                start=(dc == 0), stop=(dc == 31)), reads=[b_sinvd, b_szq2[i]],
                writes=[p_q2[1]])
        B.op("vector", lambda q, i=i: q.tensor_copy(out=s_mr2[i][:, :], in_=p_m2[0][0:1, 0:128]),
             reads=[p_m2[1]], writes=[b_smr2[i]])
        B.op("vector", lambda q, i=i: q.tensor_copy(out=s_qr2[i][:, :], in_=p_q2[0][0:1, 0:128]),
             reads=[p_q2[1]], writes=[b_sqr2[i]])
        B.op("vector", lambda q, i=i: q.tensor_mul(out=s_vr2[i][:, :], in0=s_mr2[i][:, :],
                                                   in1=s_mr2[i][:, :]),
             reads=[b_smr2[i]], writes=[b_svr2[i]])
        B.op("vector", lambda q, i=i: q.tensor_sub(out=s_vr2[i][:, :], in0=s_qr2[i][:, :],
                                                   in1=s_vr2[i][:, :]),
             reads=[b_sqr2[i], b_svr2[i]], writes=[b_svr2[i]])
        B.op("scalar", lambda q, i=i: q.activation(out=s_vr2[i][:, :], in_=s_vr2[i][:, :],
                                                   func=AF.Ln, bias=s_epsc[0:1, :]),
             reads=[b_svr2[i], b_sepsc], writes=[b_svr2[i]])
        B.op("scalar", lambda q, i=i: q.activation(out=s_vr2[i][:, :], in_=s_vr2[i][:, :],
                                                   func=AF.Exp, scale=-0.5),
             reads=[b_svr2[i]], writes=[b_svr2[i]])
        B.op("vector", lambda q, i=i: q.tensor_mul(out=s_mr2[i][:, :], in0=s_mr2[i][:, :],
                                                   in1=s_vr2[i][:, :]),
             reads=[b_smr2[i], b_svr2[i]], writes=[b_smr2[i]])
        B.op("tensor", lambda q, i=i: q.matmul(p_s12[i][0][:, 0:128], s_ones1[:, :],
                                               s_vr2[i][0:1, :], start=True, stop=True),
             reads=[b_ones1, b_svr2[i]], writes=[p_s12[i][1]])
        B.op("tensor", lambda q, i=i: q.matmul(p_s22[i][0][:, 0:128], s_ones1[:, :],
                                               s_mr2[i][0:1, :], start=True, stop=True),
             reads=[b_ones1, b_smr2[i]], writes=[p_s22[i][1]])
        for dc in range(32):
            B.op("vector", lambda q, dc=dc, i=i: q.tensor_tensor(
                out=s_z2[i][:, dc * 128:(dc + 1) * 128],
                in0=s_z2[i][:, dc * 128:(dc + 1) * 128],
                in1=p_s12[i][0][:, 0:128], op=ALU.mult),
                reads=[b_sz2[i], p_s12[i][1]], writes=[b_sz2[i]])
        B.op("vector", lambda q, i=i: q.tensor_tensor(
            out=s_z2[i][:, :].rearrange("p (a t) -> p a t", a=32),
            in0=s_z2[i][:, :].rearrange("p (a t) -> p a t", a=32),
            in1=p_s22[i][0][:, 0:128].unsqueeze(1).broadcast_to([128, 32, 128]),
            op=ALU.subtract), reads=[b_sz2[i], p_s22[i][1]], writes=[b_sz2[i]])
        for dc in range(32):
            ptx, bptx = p_tx2[dc % 2]
            B.op("tensor", lambda q, ptx=ptx, dc=dc, i=i: q.transpose(
                ptx[:, 0:128], s_z2[i][:, dc * 128:(dc + 1) * 128], s_ident[:, :]),
                reads=[b_sz2[i], b_ident], writes=[bptx])
            B.op("vector", lambda q, ptx=ptx, dc=dc, i=i: q.tensor_copy(
                out=s_otm[i][:, dc * 128:(dc + 1) * 128], in_=ptx[:, 0:128]),
                reads=[bptx], writes=[b_sotm[i]])
        dma("sync", out_ext[tt * 128:(tt + 1) * 128, :], s_otm[i][:, :],
            reads=[b_sotm[i]], writes=[b_out], group="g_sp")
    if KDEBUG:
        for nm, srcap, bsrc in [("dbg_x", d_x, b_dx), ("dbg_sg", d_sg, b_dsg),
                                ("dbg_b", d_b, b_db), ("dbg_attn", d_attn, b_dattn),
                                ("dbg_h", d_h, b_dh), ("dbg_cum", d_cum, b_dcum),
                                ("dbg_g", d_g, b_dg), ("dbg_z2", d_z2, b_dz2),
                                ("dbg_agb", d_agbo, b_agbo), ("dbg_agd", d_agdo, b_agdo)]:
            dma("sync", dbg[nm][0], srcap, reads=[bsrc], writes=[dbg[nm][1]], group="g_sp")
    B.op("gpsimd", lambda q: q.nop(), reads=[b_out], writes=[])
    B.flush(block)
    st8.close()

    B.finalize(block)
    sem_stack.close()
    stack_all.close()
    return nc


_NC_CACHE = None


def kernel(**inputs):
    global _NC_CACHE
    sh, per_core = _host_prep(inputs)
    if _NC_CACHE is None:
        _NC_CACHE = build_program()
    nc = _NC_CACHE
    in_maps = []
    for k in range(NCORES):
        m = dict(sh)
        m.update(per_core[k])
        in_maps.append(m)
    res = run_bass_kernel_spmd(nc, in_maps, core_ids=list(range(NCORES)))
    out = np.concatenate([np.asarray(res.results[k]["out"]) for k in range(NCORES)], axis=0)
    return out.reshape(1, L, D).astype(np.float32)
